# version 02
# baseline (speedup 1.0000x reference)
import math
from contextlib import ExitStack

import numpy as np
import ml_dtypes

import concourse.bass as bass
import concourse.mybir as mybir
from concourse.alu_op_type import AluOpType as ALU
from concourse.bass_utils import run_bass_kernel_spmd

F32 = mybir.dt.float32
BF16 = mybir.dt.bfloat16
AF = mybir.ActivationFunctionType
AX = mybir.AxisListType.X

T = 2048
E = 2048
H = 16
D = 64
DFF = 8192
NCH = 16
HPS = 2
NSUP = H // HPS
SUPW = HPS * 128
LAMBDA_INIT = 0.8 - 0.6 * math.exp(-0.3 * 12)
MAGIC32 = 12582912.0
MAGIC_MUL = 27021597764222976.0
CLAMP_PI = float(np.nextafter(np.float32(np.pi), np.float32(0)))


def _split(c):
    hi = float(np.float32(c))
    lo = float(np.float32(c - hi))
    return hi, lo


def host_prepare(x_all, Wq, Wk, Wv, Wout, lq1, lk1, lq2, lk2, rmsw, W1, W2):
    f32 = np.float32
    x_all = np.asarray(x_all)
    perm = np.empty(D, dtype=np.int64)
    perm[:32] = np.arange(0, D, 2)
    perm[32:] = np.arange(1, D, 2)
    full_perm = (np.arange(2 * H)[:, None] * D + perm[None, :]).reshape(-1)
    wq_p = np.ascontiguousarray(np.asarray(Wq, f32)[:, full_perm])
    wk_p = np.ascontiguousarray(np.asarray(Wk, f32)[:, full_perm])

    w1t = np.ascontiguousarray(np.asarray(W1, f32).T)
    w2tm = np.ascontiguousarray(
        np.asarray(W2, f32).T.reshape(64, 128, 16, 128)
        .transpose(2, 1, 0, 3).reshape(16, 128, DFF))

    tcols = (np.arange(NCH)[None, :] * 128 + np.arange(128)[:, None]).astype(f32)
    pfreq = np.broadcast_to(np.arange(32, dtype=f32)[None, :], (128, 32)).copy()

    k_ = np.arange(1152, dtype=np.int64)[:, None]
    j_ = np.arange(T, dtype=np.int64)[None, :]
    m = ((k_ * j_ + 512 + 1024) % T) - 1024
    absm = np.abs(m)
    big = absm > 512
    m2 = np.where(big, np.sign(m) * (1024 - absm), m)
    sgn = np.ones_like(m, dtype=np.float64)

    kcol = (np.arange(9)[None, :] * 128 + np.arange(128)[:, None]).astype(f32)
    kk = np.arange(9 * 128).reshape(9, 128).T
    wcol = np.zeros((128, 9), dtype=f32)
    wcol[(kk >= 1) & (kk <= 1023)] = 2.0
    wcol[kk == 0] = 1.0
    wcol[kk == 1024] = 1.0

    common = {
        "wq": wq_p, "wk": wk_p,
        "wv": np.ascontiguousarray(np.asarray(Wv, f32)),
        "wo": np.ascontiguousarray(np.asarray(Wout, f32)),
        "w1t": w1t, "w2tm": w2tm,
        "lq1": np.tile(np.asarray(lq1, f32).reshape(1, D), (128, 1)),
        "lk1": np.tile(np.asarray(lk1, f32).reshape(1, D), (128, 1)),
        "lq2": np.tile(np.asarray(lq2, f32).reshape(1, D), (128, 1)),
        "lk2": np.tile(np.asarray(lk2, f32).reshape(1, D), (128, 1)),
        "rmsw128": np.tile(np.asarray(rmsw, f32).reshape(1, 128), (128, 4)),
        "tcols": tcols, "pfreq": pfreq,
        "mg2": m2.astype(f32), "gsgn": sgn.astype(f32),
        "kcol": kcol, "wcol": wcol,
        "ident": np.eye(128, dtype=ml_dtypes.bfloat16),
        "triu": np.triu(np.ones((128, 128))).astype(ml_dtypes.bfloat16),
        "halfones": np.full((2, 128), 0.5, dtype=f32),
        "ones2": np.full((128, 2), 1.0, dtype=ml_dtypes.bfloat16),
    }
    shards = []
    for c in range(x_all.shape[0]):
        x = np.ascontiguousarray(np.asarray(x_all[c], f32))
        d = dict(common)
        d["x"] = x
        d["xt"] = np.ascontiguousarray(x.T)
        shards.append(d)
    return shards


class Ctx:
    def __init__(self, nc, es):
        self.nc = nc
        self.es = es
        self.sems = {}

    def sem(self, name):
        if name not in self.sems:
            h = self.es.enter_context(self.nc.semaphore(name))
            self.sems[name] = [h, 0]


class Phase:
    ENGINES = ("sync", "vector", "scalar", "tensor", "gpsimd")

    def __init__(self, ctx, name=""):
        self.ctx = ctx
        self.name = name
        self.items = {e: [] for e in self.ENGINES}

    def on(self, eng, fn, inc=None, k=1):
        tok = None
        if inc is not None:
            s = self.ctx.sems[inc]
            s[1] += k
            tok = s[1]
            self.items[eng].append(("i", fn, s[0], k))
        else:
            self.items[eng].append(("p", fn, None, 0))
        return tok

    def wait(self, eng, sem, tok):
        if tok is None or tok <= 0:
            return
        h = self.ctx.sems[sem][0]
        self.items[eng].append(("w", None, h, tok))

    def chain(self, eng, sem, fns):
        tok = None
        for fn in fns:
            if tok is not None:
                self.wait(eng, sem, tok)
            tok = self.on(eng, fn, inc=sem)
        return tok

    def barrier(self, engines=("tensor", "vector", "scalar", "sync", "gpsimd")):
        for eng in engines:
            for s in ("sv", "sa", "st", "sg"):
                self.wait(eng, s, self.ctx.sems[s][1])

    def emit(self):
        nc = self.ctx.nc
        with nc.Block() as block:
            def runner(eng_name):
                def run(e):
                    for kind, fn, h, k in self.items[eng_name]:
                        if kind == "w":
                            e.wait_ge(h, k)
                        elif kind == "i":
                            fn(e).then_inc(h, k)
                        else:
                            fn(e)
                return run
            for eng_name in self.ENGINES:
                if self.items[eng_name]:
                    getattr(block, eng_name)(runner(eng_name))


def pstr(t):
    return t.ap().ap[0][0]


def cm3(dram_ap, p=128):
    return dram_ap.rearrange("(k p) c -> p k c", p=p)


def dst3(sb_ap, k):
    return sb_ap.rearrange("p (k c) -> p k c", k=k)


def build_nc(stage=99, debug=()):
    nc = bass.Bass("TRN2", target_bir_lowering=False)
    dbg = set(debug)
    dbg_d = {}

    def din(name, shape, dt=F32):
        return nc.dram_tensor(name, shape, dt, kind="ExternalInput").ap()

    x_d = din("x", [T, E])
    xt_d = din("xt", [E, T])
    wq_d = din("wq", [E, E]); wk_d = din("wk", [E, E])
    wv_d = din("wv", [E, E]); wo_d = din("wo", [E, E])
    w1t_d = din("w1t", [E, DFF])
    w2tm_d = din("w2tm", [16, 128, DFF])
    lq1_d = din("lq1", [128, D]); lk1_d = din("lk1", [128, D])
    lq2_d = din("lq2", [128, D]); lk2_d = din("lk2", [128, D])
    rmsw_d = din("rmsw128", [128, 512])
    tcols_d = din("tcols", [128, NCH])
    pfreq_d = din("pfreq", [128, 32])
    mg2_d = din("mg2", [1152, T]); gsgn_d = din("gsgn", [1152, T])
    kcol_d = din("kcol", [128, 9]); wcol_d = din("wcol", [128, 9])
    ident_d = din("ident", [128, 128], BF16)
    triu_d = din("triu", [128, 128], BF16)
    halfones_d = din("halfones", [2, 128])
    ones2_d = din("ones2", [128, 2], BF16)

    out_d = nc.dram_tensor("out", [T, E], F32, kind="ExternalOutput").ap()

    o_sp = nc.dram_tensor("o_sp", [T, E], BF16).ap()
    x3_sp = nc.dram_tensor("x3_sp", [T, E], F32).ap()
    h_sp = nc.dram_tensor("h_sp", [T, DFF], F32).ap()
    w2q_sp = nc.dram_tensor("w2q_sp", [16, 128, DFF], BF16).ap()
    g_sp_t = nc.dram_tensor("g_sp", [1, 512], F32)
    g_sp = g_sp_t.ap()
    wo_bf_d = nc.dram_tensor("wo_bf", [E, E], BF16).ap()

    def dout(name, shape, dt=F32):
        dbg_d[name] = nc.dram_tensor("dbg_" + name, shape, dt,
                                     kind="ExternalOutput").ap()
        return dbg_d[name]

    es = ExitStack()
    with es:
        ctx = Ctx(nc, es)
        for s in ("sv", "sa", "st", "sg",
                  "d_tab", "d_xt", "d_w", "d_wf0", "d_osp", "d_out", "d_lam",
                  "d_g", "d_oin", "d_xc", "d_wo", "d_w1", "d_hsp", "d_w2",
                  "d_x3", "d_hin", "d_w2q", "d_w2s", "d_gb"):
            ctx.sem(s)

        def sb(name, shape, dt):
            return es.enter_context(nc.sbuf_tensor(name, shape, dt))

        ident = sb("identS", [128, 128], BF16)
        triu_m = sb("triuS", [128, 128], BF16)
        ucol = sb("ucol", [128, 1], F32)
        mcol = sb("mcol", [128, 1], F32)
        rmsw_eff = sb("rmsw_eff", [128, 512], F32)
        neglam = sb("neglam", [128, 1], F32)
        gband = sb("gband", [128, 3 * 128], BF16)
        cos_all = sb("cos_all", [128, 512], F32)
        sin_all = sb("sin_all", [128, 512], F32)
        halfones_s = sb("halfonesS", [2, 128], F32)
        ones2_s = sb("ones2S", [128, 2], BF16)
        cstc = sb("cstc", [128, 8], F32)
        f0acc = sb("f0acc", [2, 1024], F32)
        amax_all = sb("amax_all", [128, NCH], F32)
        hamax = sb("hamax", [128, NCH], F32)

        tok = {}

        with ExitStack() as e0:
            sl = lambda n, sh, dt=F32: e0.enter_context(nc.sbuf_tensor(n, sh, dt))
            tcols_s = sl("tcolsS", [128, NCH])
            pfreq_s = sl("pfreqS", [128, 32])
            invf = sl("invf", [128, 32])
            lamq1 = sl("lamq1", [128, D]); lamk1 = sl("lamk1", [128, D])
            lamq2 = sl("lamq2", [128, D]); lamk2 = sl("lamk2", [128, D])
            rmsw_s = sl("rmsw_s", [128, 512])
            kcol_s = sl("kcol_s", [128, 9])
            wcol_s = sl("wcol_s", [128, 9])
            wfilt2 = sl("wfilt2", [128, 18], BF16)
            sA = sl("p0_sA", [128, 512]); sB = sl("p0_sB", [128, 512])
            sC = sl("p0_sC", [128, 512])
            theta = sl("p0_theta", [128, 512])
            gchA = sl("p0_gA", [128, T]); gchB = sl("p0_gB", [128, T])
            gcos = sl("p0_gcos", [128, T])
            gcos_bf = sl("p0_gcosbf", [128, T], BF16)
            g_ext = sl("p0_gext", [1, 512])
            grev = sl("p0_grev", [1, 512])
            lcol = sl("p0_lcol", [128, 4])
            gps = e0.enter_context(nc.psum_tensor("p0_gps", [2, 2048], F32))

            p = Phase(ctx, "p0")
            t_tab = None
            for (d_, s_) in [(ident, ident_d), (triu_m, triu_d), (tcols_s, tcols_d),
                             (pfreq_s, pfreq_d), (halfones_s, halfones_d),
                             (ones2_s, ones2_d), (kcol_s, kcol_d), (wcol_s, wcol_d)]:
                t_tab = p.on("sync", lambda e, d=d_, s=s_: e.dma_start(out=d[:], in_=s),
                             inc="d_tab", k=16)
            t_lam = None
            for (d_, s_) in [(lamq1, lq1_d), (lamk1, lk1_d), (lamq2, lq2_d),
                             (lamk2, lk2_d), (rmsw_s, rmsw_d)]:
                t_lam = p.on("sync", lambda e, d=d_, s=s_: e.dma_start(out=d[:], in_=s),
                             inc="d_lam", k=16)
            tok["wo_cast"] = p.on("gpsimd", lambda e: e.dma_start(out=wo_bf_d, in_=wo_d),
                                  inc="d_wo", k=16)

            p.wait("vector", "d_tab", t_tab)
            tok["mcol"] = p.chain("vector", "sv", [
                lambda e: e.memset(ucol[:], 0.25),
                lambda e: e.tensor_scalar(ucol[:], ucol[:], 8388608.0, None, ALU.add),
                lambda e: e.tensor_scalar(ucol[:], ucol[:], 8388608.0, None, ALU.subtract),
                lambda e: e.tensor_scalar(mcol[:], ucol[:], MAGIC_MUL, MAGIC32,
                                          ALU.mult, ALU.add),
            ])
            tok["f0acc"] = p.on("vector", lambda e: e.memset(f0acc[:], 0.0), inc="sv")
            tok["hamax0"] = p.on("vector", lambda e: e.memset(hamax[:], 0.0), inc="sv")
            tok["r127"] = p.chain("vector", "sv", [
                lambda e: e.memset(cstc[:, 0:1], 127.0),
                lambda e: e.reciprocal(cstc[:, 0:1], cstc[:, 0:1]),
                lambda e: e.memset(cstc[:, 5:6], 1e-5),
            ])

            cln_hi, cln_lo = _split(-math.log(10000.0) / 32.0)
            tv = p.chain("vector", "sv", [
                lambda e: e.tensor_scalar(sA[:, 0:32], pfreq_s[:], cln_hi, None, ALU.mult),
                lambda e: e.tensor_scalar(sB[:, 0:32], pfreq_s[:], cln_lo, None, ALU.mult),
                lambda e: e.tensor_tensor(sC[:, 0:32], sA[:, 0:32], sB[:, 0:32], ALU.add),
            ])
            p.wait("scalar", "sv", tv)
            ta = p.on("scalar", lambda e: e.activation(invf[:], sC[:, 0:32], AF.Exp),
                      inc="sa")
            p.wait("vector", "sa", ta)
            tcb = bass.AP(tcols_s, 0, [[pstr(tcols_s), 128], [1, NCH], [0, 32]])
            ivb = bass.AP(invf, 0, [[pstr(invf), 128], [0, NCH], [1, 32]])
            th3 = theta[:].rearrange("p (a b) -> p a b", a=NCH)
            t_theta = p.on("vector",
                           lambda e: e.tensor_tensor(th3, tcb, ivb, ALU.mult), inc="sv")

            i2p_hi, i2p_lo = _split(1.0 / (2 * math.pi))
            p2a = float(np.float32(2 * math.pi))
            p2b = float(np.float32(2 * math.pi - p2a))
            p2c = float(np.float32(2 * math.pi - p2a - p2b))
            hp_hi, hp_lo = _split(math.pi / 2)

            def reduce_sin(out_t, src_t, shift, tok_in):
                fns = []
                if shift:
                    fns += [
                        lambda e: e.tensor_scalar(sA[:], src_t[:], hp_hi, None, ALU.add),
                        lambda e: e.tensor_scalar(sA[:], sA[:], hp_lo, None, ALU.add),
                    ]
                    src = sA
                else:
                    src = src_t
                fns += [
                    lambda e: e.tensor_scalar(sB[:], src[:], i2p_hi, None, ALU.mult),
                    lambda e: e.tensor_scalar(sC[:], src[:], i2p_lo, None, ALU.mult),
                    lambda e: e.tensor_tensor(sB[:], sB[:], sC[:], ALU.add),
                    lambda e: e.tensor_scalar(sB[:], sB[:], mcol[:], None, ALU.add),
                    lambda e: e.tensor_scalar(sB[:], sB[:], mcol[:], None, ALU.subtract),
                    lambda e: e.tensor_scalar(sC[:], sB[:], p2a, None, ALU.mult),
                    lambda e: e.tensor_tensor(out_t[:], src[:], sC[:], ALU.subtract),
                    lambda e: e.tensor_scalar(sC[:], sB[:], p2b, None, ALU.mult),
                    lambda e: e.tensor_tensor(out_t[:], out_t[:], sC[:], ALU.subtract),
                    lambda e: e.tensor_scalar(sC[:], sB[:], p2c, None, ALU.mult),
                    lambda e: e.tensor_tensor(out_t[:], out_t[:], sC[:], ALU.subtract),
                    lambda e: e.tensor_scalar(out_t[:], out_t[:], CLAMP_PI, -CLAMP_PI,
                                              ALU.min, ALU.max),
                ]
                if tok_in is not None:
                    p.wait("vector", "sv", tok_in)
                tvv = p.chain("vector", "sv", fns)
                p.wait("scalar", "sv", tvv)
                return p.on("scalar",
                            lambda e: e.activation(out_t[:], out_t[:], AF.Sin), inc="sa")

            ta_sin = reduce_sin(sin_all, theta, False, t_theta)
            p.wait("vector", "sa", ta_sin)
            ta_cos = reduce_sin(cos_all, theta, True, None)
            tok["sincos"] = ta_cos

            li_hi, li_lo = _split(LAMBDA_INIT)
            p.wait("vector", "d_lam", t_lam)
            p.wait("vector", "sa", ta_cos)
            tv = p.chain("vector", "sv", [
                lambda e: e.tensor_tensor(sA[:, 0:D], lamq1[:], lamk1[:], ALU.mult),
                lambda e: e.tensor_tensor(sB[:, 0:D], lamq2[:], lamk2[:], ALU.mult),
                lambda e: e.tensor_reduce(lcol[:, 0:1], sA[:, 0:D], axis=AX, op=ALU.add),
                lambda e: e.tensor_reduce(lcol[:, 1:2], sB[:, 0:D], axis=AX, op=ALU.add),
            ])
            p.wait("scalar", "sv", tv)
            ta = p.on("scalar", lambda e: e.activation(lcol[:, 2:4], lcol[:, 0:2], AF.Exp),
                      inc="sa")
            p.wait("vector", "sa", ta)
            tok["neglam"] = p.chain("vector", "sv", [
                lambda e: e.tensor_tensor(neglam[:], lcol[:, 2:3], lcol[:, 3:4], ALU.subtract),
                lambda e: e.tensor_scalar(neglam[:], neglam[:], li_hi, None, ALU.add),
                lambda e: e.tensor_scalar(neglam[:], neglam[:], li_lo, None, ALU.add),
                lambda e: e.tensor_scalar(neglam[:], neglam[:], -1.0, None, ALU.mult),
            ])

            om_hi, om_lo = _split(1.0 - LAMBDA_INIT)
            tok["rmsw"] = p.chain("vector", "sv", [
                lambda e: e.tensor_scalar(sA[:], rmsw_s[:], om_hi, None, ALU.mult),
                lambda e: e.tensor_scalar(sB[:], rmsw_s[:], om_lo, None, ALU.mult),
                lambda e: e.tensor_tensor(rmsw_eff[:], sA[:], sB[:], ALU.add),
            ])

            u_hi, u_lo = _split(1.0 / (1024.0 * 0.15))
            g_hi, g_lo = _split(math.pi / 1024.0)
            tv = p.chain("vector", "sv", [
                lambda e: e.tensor_scalar(sA[:, 0:9], kcol_s[:], u_hi, None, ALU.mult),
                lambda e: e.tensor_scalar(sB[:, 0:9], kcol_s[:], u_lo, None, ALU.mult),
                lambda e: e.tensor_tensor(sA[:, 0:9], sA[:, 0:9], sB[:, 0:9], ALU.add),
                lambda e: e.tensor_tensor(sB[:, 0:9], sA[:, 0:9], sA[:, 0:9], ALU.mult),
            ])
            p.wait("scalar", "sv", tv)
            ta = p.on("scalar",
                      lambda e: e.activation(sC[:, 0:9], sB[:, 0:9], AF.Exp, scale=-0.5),
                      inc="sa")
            p.wait("vector", "sa", ta)
            w2vo = wfilt2[:].rearrange("p (a b) -> p a b", a=9)
            s3v = bass.AP(sC, 9, [[pstr(sC), 128], [1, 9], [0, 2]])
            t_wfilt = p.chain("vector", "sv", [
                lambda e: e.tensor_tensor(sC[:, 9:18], sC[:, 0:9], wcol_s[:], ALU.mult),
                lambda e: e.tensor_scalar(sC[:, 9:18], sC[:, 9:18], 0.00048828125,
                                          None, ALU.mult),
                lambda e: e.tensor_copy(w2vo, s3v),
            ])
            p.wait("tensor", "sv", t_wfilt)

            t_gmm = 0
            t_gread = None
            for r in range(9):
                p.wait("sync", "sv", t_gread)
                t_gd = p.on("sync", lambda e, r=r: e.dma_start(
                    out=gchA[:], in_=mg2_d[r * 128:(r + 1) * 128, :]), inc="d_g", k=16)
                t_gd = p.on("sync", lambda e, r=r: e.dma_start(
                    out=gchB[:], in_=gsgn_d[r * 128:(r + 1) * 128, :]), inc="d_g", k=16)
                p.wait("vector", "d_g", t_gd)
                if t_gmm:
                    p.wait("vector", "st", t_gmm)
                fns = [lambda e: e.tensor_scalar(gcos[:], gchA[:], g_hi, None, ALU.mult)]
                for pc in range(4):
                    fns += [
                        lambda e, pc=pc: e.tensor_scalar(
                            sA[:], gchA[:, pc * 512:(pc + 1) * 512], g_lo, None, ALU.mult),
                        lambda e, pc=pc: e.tensor_tensor(
                            gcos[:, pc * 512:(pc + 1) * 512],
                            gcos[:, pc * 512:(pc + 1) * 512], sA[:], ALU.add),
                    ]
                tv = p.chain("vector", "sv", fns)
                p.wait("scalar", "sv", tv)
                ta = p.on("scalar", lambda e: e.activation(gcos[:], gcos[:], AF.Sin),
                          inc="sa")
                p.wait("vector", "sa", ta)
                tv = p.chain("vector", "sv", [
                    lambda e: e.tensor_tensor(gcos[:], gcos[:], gchB[:], ALU.mult),
                    lambda e: e.tensor_copy(gcos_bf[:], gcos[:]),
                ])
                t_gread = tv
                p.wait("tensor", "sv", tv)
                for w in range(4):
                    t_gmm = p.on("tensor", lambda e, w=w, r=r: e.matmul(
                        gps[:, w * 512:(w + 1) * 512], wfilt2[:, 2 * r:2 * r + 2],
                        gcos_bf[:, w * 512:(w + 1) * 512],
                        start=(r == 0), stop=(r == 8)), inc="st")
            p.wait("vector", "st", t_gmm)
            t_grev = p.chain("vector", "sv", [
                lambda e: e.tensor_copy(sA[0:1, 0:512], gps[0:1, 0:512]),
                lambda e: e.tensor_copy(sB[0:1, 0:512], gps[0:1, 512:1024]),
                lambda e: e.tensor_copy(sC[0:1, 0:512], gps[0:1, 1024:1536]),
                lambda e: e.tensor_copy(theta[0:1, 0:512], gps[0:1, 1536:2048]),
                lambda e: e.tensor_copy(g_ext[0:1, 0:256], theta[0:1, 256:512]),
                lambda e: e.tensor_copy(g_ext[0:1, 256:512], sA[0:1, 0:256]),
            ])
            p.wait("sync", "sv", t_grev)
            t_gsp = p.on("sync", lambda e: e.dma_start(out=g_sp, in_=g_ext[0:1, :]),
                         inc="d_g", k=16)
            p.wait("gpsimd", "d_g", t_gsp)
            t_gbd = None
            for di in range(3):
                base = 129 + 128 * (di - 1)
                t_gbd = p.on("sync", lambda e, di=di, base=base: e.dma_start(
                    out=gcos[:, di * 128:(di + 1) * 128],
                    in_=bass.AP(g_sp_t, base, [[1, 128], [1, 128]])), inc="d_g", k=16)
            p.wait("vector", "d_g", t_gbd)
            t_gb = None
            for di in range(3):
                t_gb = p.on("vector", lambda e, di=di: e.tensor_copy(
                    gband[:, di * 128:(di + 1) * 128],
                    bass.AP(gcos, di * 128 + 127, [[pstr(gcos), 128], [-1, 128]])),
                    inc="sv")
            tok["gband"] = t_gb

            if "g" in dbg:
                gdb = dout("g", [1, 512])
                p.wait("sync", "sv", t_grev)
                p.on("sync", lambda e: e.dma_start(out=gdb, in_=g_ext[0:1, :]),
                     inc="d_out", k=16)
                gbdb = dout("gband", [128, 384], BF16)
                p.wait("sync", "sv", t_gb)
                p.on("sync", lambda e: e.dma_start(out=gbdb, in_=gband[:]),
                     inc="d_out", k=16)
            if "tabs" in dbg:
                cdb = dout("cos", [128, 512]); sdb = dout("sin", [128, 512])
                ldb = dout("neglam", [128, 1])
                p.wait("sync", "sa", tok["sincos"])
                p.on("sync", lambda e: e.dma_start(out=cdb, in_=cos_all[:]), inc="d_out", k=16)
                p.on("sync", lambda e: e.dma_start(out=sdb, in_=sin_all[:]), inc="d_out", k=16)
                p.wait("sync", "sv", tok["neglam"])
                p.on("sync", lambda e: e.dma_start(out=ldb, in_=neglam[:]), inc="d_out", k=16)
            p.emit()

        if stage == 0:
            _finish(ctx, out_d)
            return nc, dbg_d

        with ExitStack() as eP1:
            sl1 = lambda n, sh, dt: eP1.enter_context(nc.sbuf_tensor(n, sh, dt))
            xt_bf = sl1("xt_bf", [128, NCH * T], BF16)
            xt_src = cm3(xt_d)
            ph_x = Phase(ctx, "p1x")
            tok["xt"] = ph_x.on("gpsimd", lambda e: e.dma_start(
                out=dst3(xt_bf[:], NCH), in_=xt_src), inc="d_xt", k=16)
            ph_x.emit()

            q_fm = sl1("q_fm", [128, HPS * T], BF16)
            k_fm = sl1("k_fm", [128, HPS * T], BF16)
            v_aug = sl1("v_aug", [128, NCH * HPS * 132], BF16)
            wst_q = sl1("wst_q", [128, NCH * SUPW], BF16)
            wst_k = sl1("wst_k", [128, NCH * SUPW], BF16)
            wst_v = sl1("wst_v", [128, NCH * SUPW], BF16)
            qtm = sl1("qtm", [128, 2 * SUPW], BF16)
            ktm = sl1("ktm", [128, 2 * SUPW], BF16)
            rsc = [sl1("rsc%d" % i, [128, 128], F32) for i in range(4)]
            sqb = sl1("sqb", [128, T], BF16)
            negc = sl1("negc", [128, 4], F32)
            ncsc = sl1("ncsc", [2, 16], F32)
            wf0 = sl1("wf0", [128, 2 * 2048], F32)
            wf0a = sl1("wf0a", [128, 2048], BF16)
            wf0f = sl1("wf0f", [128, 2048], F32)
            wf0lo = sl1("wf0lo", [128, 2048], BF16)
            p1s = sl1("p1s", [128, 2 * 512], BF16)
            p2s = sl1("p2s", [128, 2 * 512], BF16)
            ost = sl1("ost", [128, NCH * SUPW], BF16)

            wq_col = lambda s: cm3(wq_d[:, s * SUPW:(s + 1) * SUPW])
            wk_col = lambda s: cm3(wk_d[:, s * SUPW:(s + 1) * SUPW])
            wv_col = lambda s: cm3(wv_d[:, s * SUPW:(s + 1) * SUPW])

            o_free = {}
            p_free = {}
            s_free = {}

            n_sup = 1 if stage <= 2 else NSUP
            for sup in range(n_sup):
                with ExitStack() as e1a:
                    pq = lambda n, sh, dt=F32: e1a.enter_context(
                        nc.psum_tensor(n + "_%d" % sup, sh, dt))
                    ps_q = [pq("ps_q%d" % i, [128, SUPW]) for i in range(2)]
                    ps_k = [pq("ps_k%d" % i, [128, SUPW]) for i in range(2)]
                    ps_v = pq("ps_v", [128, SUPW])
                    ps_tr = [pq("ps_tr%d" % i, [128, 128], BF16) for i in range(2)]
                    ps_f0 = pq("ps_f0", [128, 512])

                    pa = Phase(ctx, "p1a_%d" % sup)
                    pa.wait("gpsimd", "st", tok.get("projmm"))
                    t_wq = pa.on("gpsimd", lambda e, s=sup: e.dma_start(out=dst3(wst_q[:], NCH), in_=wq_col(s)), inc="d_w", k=16)
                    t_wk = pa.on("gpsimd", lambda e, s=sup: e.dma_start(out=dst3(wst_k[:], NCH), in_=wk_col(s)), inc="d_w", k=16)
                    t_wv = pa.on("gpsimd", lambda e, s=sup: e.dma_start(out=dst3(wst_v[:], NCH), in_=wv_col(s)), inc="d_w", k=16)
                    pa.wait("tensor", "d_w", t_wv)
                    pa.wait("tensor", "d_xt", tok["xt"])
                    if s_free:
                        pa.wait("tensor", "sa", max(s_free.values()))
                    if o_free:
                        pa.wait("tensor", "sv", max(o_free.values()))
                    if p_free:
                        pa.wait("tensor", "st", max(p_free.values()))
                    pa.wait("vector", "sv", tok["mcol"])
                    pa.wait("vector", "sa", tok["sincos"])
                    t_vones = pa.on("vector", lambda e: e.memset(
                        bass.AP(v_aug, 128, [[pstr(v_aug), 128], [132, NCH * HPS], [1, 1]]),
                        1.0), inc="sv")

                    wsrc = []
                    for qq in range(16):
                        if sup < 4:
                            ch, qtr = divmod(sup * 16 + qq, 4)
                            wsrc.append(w1t_d.rearrange("(a p) f -> a p f", p=128)[ch]
                                        [:, qtr * 2048:(qtr + 1) * 2048])
                        else:
                            me, qtr = divmod((sup - 4) * 16 + qq, 4)
                            wsrc.append(w2tm_d[me][:, qtr * 2048:(qtr + 1) * 2048])

                    f0_base = ctx.sems["d_wf0"][1]
                    f0_dma = pa.on("sync", lambda e, s=wsrc[0]: e.dma_start(
                        out=wf0[:, 0:2048], in_=s), inc="d_wf0", k=16)
                    f0_abs = {}
                    f0_mm = 0

                    rot_tok = {}
                    vev_tok = {}
                    tr_tok = {}
                    for ti in range(NCH):
                        par = ti % 2
                        if ti >= 2:
                            pa.wait("tensor", "sv", rot_tok[ti - 2])
                        if ti >= 1:
                            pa.wait("tensor", "sv", vev_tok[ti - 1])
                        mmv = None
                        for k in range(NCH):
                            st_ = (k == 0); sp_ = (k == NCH - 1)
                            pa.on("tensor", lambda e, k=k, ti=ti, par=par, st_=st_, sp_=sp_: e.matmul(
                                ps_q[par][:, :],
                                xt_bf[:, (k * NCH + ti) * 128:(k * NCH + ti) * 128 + 128],
                                wst_q[:, k * SUPW:(k + 1) * SUPW], start=st_, stop=sp_), inc="st")
                            pa.on("tensor", lambda e, k=k, ti=ti, par=par, st_=st_, sp_=sp_: e.matmul(
                                ps_k[par][:, :],
                                xt_bf[:, (k * NCH + ti) * 128:(k * NCH + ti) * 128 + 128],
                                wst_k[:, k * SUPW:(k + 1) * SUPW], start=st_, stop=sp_), inc="st")
                            mmv = pa.on("tensor", lambda e, k=k, ti=ti, st_=st_, sp_=sp_: e.matmul(
                                ps_v[:, :],
                                xt_bf[:, (k * NCH + ti) * 128:(k * NCH + ti) * 128 + 128],
                                wst_v[:, k * SUPW:(k + 1) * SUPW], start=st_, stop=sp_), inc="st")
                        tok["projmm"] = mmv

                        pa.wait("vector", "st", mmv)
                        tvv = None
                        for hl in range(HPS):
                            tvv = pa.on("vector", lambda e, ti=ti, hl=hl: e.tensor_copy(
                                v_aug[:, (ti * HPS + hl) * 132:(ti * HPS + hl) * 132 + 128],
                                ps_v[:, hl * 128:(hl + 1) * 128]), inc="sv")
                        vev_tok[ti] = tvv

                        cosv = bass.AP(cos_all, 32 * ti, [[pstr(cos_all), 128], [0, 4], [1, 32]])
                        sinv = bass.AP(sin_all, 32 * ti, [[pstr(sin_all), 128], [0, 4], [1, 32]])
                        cmp = lambda t_: t_[:].rearrange("p (a b) -> p a b", b=32)
                        tlast = None
                        for (ps_, tmb) in ((ps_q[par], qtm), (ps_k[par], ktm)):
                            x1v = bass.AP(ps_, 0, [[pstr(ps_), 128], [64, 4], [1, 32]])
                            x2v = bass.AP(ps_, 32, [[pstr(ps_), 128], [64, 4], [1, 32]])
                            o1v = bass.AP(tmb, par * SUPW, [[pstr(tmb), 128], [64, 4], [1, 32]])
                            o2v = bass.AP(tmb, par * SUPW + 32, [[pstr(tmb), 128], [64, 4], [1, 32]])
                            tv1 = pa.on("vector", lambda e, x1v=x1v, cosv=cosv: e.tensor_tensor(cmp(rsc[0]), x1v, cosv, ALU.mult), inc="sv")
                            tv2 = pa.on("vector", lambda e, x2v=x2v, sinv=sinv: e.tensor_tensor(cmp(rsc[1]), x2v, sinv, ALU.mult), inc="sv")
                            tv3 = pa.on("vector", lambda e, x1v=x1v, sinv=sinv: e.tensor_tensor(cmp(rsc[2]), x1v, sinv, ALU.mult), inc="sv")
                            tv4 = pa.on("vector", lambda e, x2v=x2v, cosv=cosv: e.tensor_tensor(cmp(rsc[3]), x2v, cosv, ALU.mult), inc="sv")
                            pa.wait("vector", "sv", tv2)
                            pa.on("vector", lambda e, o1v=o1v: e.tensor_tensor(o1v, cmp(rsc[0]), cmp(rsc[1]), ALU.subtract), inc="sv")
                            pa.wait("vector", "sv", tv4)
                            tlast = pa.on("vector", lambda e, o2v=o2v: e.tensor_tensor(o2v, cmp(rsc[2]), cmp(rsc[3]), ALU.add), inc="sv")
                        rot_tok[ti] = tlast

                        pa.wait("tensor", "sv", tlast)
                        tte = None
                        tile_i = 0
                        for (tmb, fmb) in ((qtm, q_fm), (ktm, k_fm)):
                            for lc in range(HPS):
                                b = tile_i % 2
                                if tr_tok.get((ti, b)) or tr_tok.get((ti - 1, b)):
                                    pa.wait("tensor", "sa", tr_tok.get((ti, b)) or tr_tok.get((ti - 1, b)))
                                tt = pa.on("tensor", lambda e, tmb=tmb, par=par, lc=lc, b=b: e.transpose(
                                    ps_tr[b][:, :],
                                    tmb[:, par * SUPW + lc * 128:par * SUPW + (lc + 1) * 128],
                                    ident[:]), inc="st")
                                pa.wait("scalar", "st", tt)
                                tte = pa.on("scalar", lambda e, fmb=fmb, lc=lc, ti=ti, b=b: e.copy(
                                    fmb[:, lc * T + ti * 128:lc * T + (ti + 1) * 128],
                                    ps_tr[b][:, :]), inc="sa")
                                tr_tok[(ti, b)] = tte
                                tile_i += 1
                        tr_tok[ti] = tte

                        if ti < len(wsrc):
                            slot = ti % 2
                            compensate = sup >= 4
                            if ti + 1 < len(wsrc):
                                pa.wait("sync", "sa", f0_abs.get(ti - 1))
                                f0_dma = pa.on("sync", lambda e, s=wsrc[ti + 1], sl_=(ti + 1) % 2: e.dma_start(
                                    out=wf0[:, sl_ * 2048:(sl_ + 1) * 2048], in_=s), inc="d_wf0", k=16)
                            pa.wait("scalar", "d_wf0", f0_base + 16 * (ti + 1))
                            if f0_mm:
                                pa.wait("scalar", "st", f0_mm)
                            if not compensate:
                                f0_abs[ti] = pa.on("scalar", lambda e, sl_=slot: e.activation(
                                    wf0a[:], wf0[:, sl_ * 2048:(sl_ + 1) * 2048], AF.Abs), inc="sa")
                                pa.wait("tensor", "sa", f0_abs[ti])
                                if ti == 0:
                                    pa.wait("tensor", "sv", tok["f0acc"])
                                for w in range(4):
                                    f0_mm = pa.on("tensor", lambda e, w=w, ti=ti: e.matmul(
                                        ps_f0[0:2, :], ones2_s[:], wf0a[:, w * 512:(w + 1) * 512],
                                        start=(ti == 0 and w == 0), stop=(ti == 15 and w == 3)), inc="st")
                            else:
                                ta_f = pa.on("scalar", lambda e, sl_=slot: e.activation(
                                    wf0f[:], wf0[:, sl_ * 2048:(sl_ + 1) * 2048], AF.Abs), inc="sa")
                                pa.wait("scalar", "sa", ta_f)
                                f0_abs[ti] = pa.on("scalar", lambda e: e.copy(
                                    wf0a[:], wf0f[:]), inc="sa")
                                pa.wait("gpsimd", "sa", f0_abs[ti])
                                tg_lo = pa.on("gpsimd", lambda e: e.tensor_tensor(
                                    wf0lo[:], wf0f[:], wf0a[:], ALU.subtract), inc="sg")
                                pa.wait("tensor", "sa", f0_abs[ti])
                                pa.wait("tensor", "sg", tg_lo)
                                if ti == 0:
                                    pa.wait("tensor", "sv", tok["f0acc"])
                                for w in range(4):
                                    f0_mm = pa.on("tensor", lambda e, w=w, ti=ti: e.matmul(
                                        ps_f0[0:2, :], ones2_s[:], wf0a[:, w * 512:(w + 1) * 512],
                                        start=(ti == 0 and w == 0), stop=False), inc="st")
                                for w in range(4):
                                    f0_mm = pa.on("tensor", lambda e, w=w, ti=ti: e.matmul(
                                        ps_f0[0:2, :], ones2_s[:], wf0lo[:, w * 512:(w + 1) * 512],
                                        start=False, stop=(ti == 15 and w == 3)), inc="st")

                    pa.wait("vector", "st", f0_mm)
                    accoff = 0 if sup < 4 else 512
                    tok["f0acc"] = pa.on("vector", lambda e, a=accoff: e.tensor_tensor(
                        f0acc[:, a:a + 512], f0acc[:, a:a + 512], ps_f0[0:2, :], ALU.add), inc="sv")

                    t_ms_read = tok["f0acc"]
                    t_negc = None
                    for gi, fmb in enumerate((q_fm, k_fm)):
                        for lc in range(HPS):
                            if gi or lc:
                                pa.wait("scalar", "st", t_stab_mm)
                            tsq = pa.on("scalar", lambda e, fmb=fmb, lc=lc: e.activation(
                                sqb[:], fmb[:, lc * T:(lc + 1) * T], AF.Square), inc="sa")
                            for bi, base in enumerate((0, 64)):
                                col = gi * 8 + 2 * lc + bi
                                pa.wait("tensor", "sa", tsq)
                                for w in range(4):
                                    pa.wait("tensor", "sv", t_ms_read)
                                    t_stab_mm = pa.on("tensor", lambda e, base=base, w=w: e.matmul(
                                        ps_f0[0:2, :], ones2_s[base:base + 64, :],
                                        sqb[base:base + 64, w * 512:(w + 1) * 512],
                                        start=True, stop=True,
                                        tile_position=(base, 0)), inc="st")
                                    pa.wait("vector", "st", t_stab_mm)
                                    if w == 0:
                                        t_ms_read = pa.on("vector", lambda e, col=col: e.tensor_reduce(
                                            ncsc[0:2, col:col + 1], ps_f0[0:2, :], axis=AX,
                                            op=ALU.max), inc="sv")
                                    else:
                                        t_ms_read = pa.chain("vector", "sv", [
                                            lambda e: e.tensor_reduce(
                                                rsc[0][0:2, 0:1], ps_f0[0:2, :], axis=AX, op=ALU.max),
                                            lambda e, col=col: e.tensor_tensor(
                                                ncsc[0:2, col:col + 1], ncsc[0:2, col:col + 1],
                                                rsc[0][0:2, 0:1], ALU.max),
                                        ])
                    pa.wait("scalar", "sv", t_ms_read)
                    tsr = pa.on("scalar", lambda e: e.activation(ncsc[0:2, 0:16], ncsc[0:2, 0:16], AF.Sqrt), inc="sa")
                    pa.wait("vector", "sa", tsr)
                    t_nc2 = pa.chain("vector", "sv", [
                        lambda e: e.tensor_tensor(ncsc[0:2, 0:4], ncsc[0:2, 0:4], ncsc[0:2, 8:12], ALU.mult),
                        lambda e: e.tensor_scalar(ncsc[0:2, 0:4], ncsc[0:2, 0:4], -0.125, None, ALU.mult),
                    ])
                    pa.wait("tensor", "sv", t_nc2)
                    if s_free:
                        pa.wait("vector", "sa", max(s_free.values()))
                    for hh in range(4):
                        t_bc = pa.on("tensor", lambda e, hh=hh: e.matmul(
                            ps_f0[:, 0:1], halfones_s[:, :], ncsc[0:2, hh:hh + 1],
                            start=True, stop=True), inc="st")
                        pa.wait("vector", "st", t_bc)
                        t_negc = pa.on("vector", lambda e, hh=hh: e.tensor_copy(
                            negc[:, hh:hh + 1], ps_f0[:, 0:1]), inc="sv")
                        if hh < 3:
                            pa.wait("tensor", "sv", t_negc)
                    tok["negc"] = t_negc
                    tok["trlast"] = tr_tok[NCH - 1]
                    tok["vaug"] = max(vev_tok[NCH - 1], t_vones)
                    pa.emit()

                if stage == 1:
                    pd = Phase(ctx, "dbg1")
                    if "qkv" in dbg:
                        qdb = dout("q_fm", [128, HPS * T], BF16)
                        kdb = dout("k_fm", [128, HPS * T], BF16)
                        vdb = dout("v_aug", [128, NCH * HPS * 132], BF16)
                        cdb2 = dout("negc", [128, 4])
                        pd.wait("sync", "sa", ctx.sems["sa"][1])
                        pd.wait("sync", "sv", ctx.sems["sv"][1])
                        pd.on("sync", lambda e: e.dma_start(out=qdb, in_=q_fm[:]), inc="d_out", k=16)
                        pd.on("sync", lambda e: e.dma_start(out=kdb, in_=k_fm[:]), inc="d_out", k=16)
                        pd.on("sync", lambda e: e.dma_start(out=vdb, in_=v_aug[:]), inc="d_out", k=16)
                        pd.on("sync", lambda e: e.dma_start(out=cdb2, in_=negc[:]), inc="d_out", k=16)
                    pd.emit()
                    _finish(ctx, out_d)
                    return nc, dbg_d

                with ExitStack() as e1b:
                    pq = lambda n, sh: e1b.enter_context(
                        nc.psum_tensor(n + "_%d" % sup, sh, F32))
                    ps_s1 = [pq("ps_s1%d" % i, [128, 512]) for i in range(2)]
                    ps_s2 = [pq("ps_s2%d" % i, [128, 512]) for i in range(2)]
                    ps_o = [pq("ps_o%d" % i, [128, 264]) for i in range(4)]

                    pb = Phase(ctx, "p1b_%d" % sup)
                    pb.wait("tensor", "sa", tok["trlast"])
                    pb.wait("tensor", "sv", tok["vaug"])
                    pb.wait("tensor", "sv", tok["negc"])
                    pb.wait("scalar", "sv", tok["negc"])
                    pb.wait("vector", "sv", tok["neglam"])
                    pb.wait("vector", "d_osp", tok.get("osp"))

                    for hl in range(HPS):
                        for W in range(4):
                            jmax = 4 * W + 3
                            for j in range(jmax + 1):
                                jp = j % 2
                                nst = max(512 * W, 128 * j)
                                wid = 512 * W + 512 - nst
                                off = nst - 512 * W
                                pb.wait("tensor", "sa", s_free.get(jp))
                                t1 = pb.on("tensor", lambda e, hl=hl, j=j, jp=jp, nst=nst, wid=wid, off=off: e.matmul(
                                    ps_s1[jp][:, off:off + wid],
                                    k_fm[0:64, hl * T + j * 128:hl * T + (j + 1) * 128],
                                    q_fm[0:64, hl * T + nst:hl * T + nst + wid],
                                    start=True, stop=True, tile_position=(0, 0)), inc="st")
                                t2 = pb.on("tensor", lambda e, hl=hl, j=j, jp=jp, nst=nst, wid=wid, off=off: e.matmul(
                                    ps_s2[jp][:, off:off + wid],
                                    k_fm[64:128, hl * T + j * 128:hl * T + (j + 1) * 128],
                                    q_fm[64:128, hl * T + nst:hl * T + nst + wid],
                                    start=True, stop=True, tile_position=(64, 0)), inc="st")
                                pb.wait("scalar", "st", t2)
                                pb.wait("scalar", "st", p_free.get(jp))
                                e1t = pb.on("scalar", lambda e, hl=hl, jp=jp, off=off, wid=wid: e.activation(
                                    p1s[:, jp * 512 + off:jp * 512 + off + wid],
                                    ps_s1[jp][:, off:off + wid], AF.Exp,
                                    bias=negc[:, 2 * hl:2 * hl + 1], scale=0.125), inc="sa")
                                e2t = pb.on("scalar", lambda e, hl=hl, jp=jp, off=off, wid=wid: e.activation(
                                    p2s[:, jp * 512 + off:jp * 512 + off + wid],
                                    ps_s2[jp][:, off:off + wid], AF.Exp,
                                    bias=negc[:, 2 * hl + 1:2 * hl + 2], scale=0.125), inc="sa")
                                s_free[jp] = e2t
                                mask_tok = None
                                if j >= 4 * W:
                                    pb.wait("vector", "sa", e2t)
                                    pb.on("vector", lambda e, jp=jp, off=off: e.tensor_tensor(
                                        p1s[:, jp * 512 + off:jp * 512 + off + 128],
                                        p1s[:, jp * 512 + off:jp * 512 + off + 128],
                                        triu_m[:], ALU.mult), inc="sv")
                                    mask_tok = pb.on("vector", lambda e, jp=jp, off=off: e.tensor_tensor(
                                        p2s[:, jp * 512 + off:jp * 512 + off + 128],
                                        p2s[:, jp * 512 + off:jp * 512 + off + 128],
                                        triu_m[:], ALU.mult), inc="sv")
                                if mask_tok is not None:
                                    pb.wait("tensor", "sv", mask_tok)
                                else:
                                    pb.wait("tensor", "sa", e2t)
                                avt = None
                                for i in range(max(j, 4 * W), 4 * W + 4):
                                    il = i - 4 * W
                                    if j == 0:
                                        pb.wait("tensor", "sv", o_free.get(il))
                                    avt = pb.on("tensor", lambda e, jp=jp, il=il, i=i, j=j, W=W, hl=hl: e.matmul(
                                        ps_o[il][:, 0:129],
                                        p1s[:, jp * 512 + (i - 4 * W) * 128:jp * 512 + (i - 4 * W) * 128 + 128],
                                        v_aug[:, (j * HPS + hl) * 132:(j * HPS + hl) * 132 + 129],
                                        start=(j == 0), stop=(j == i)), inc="st")
                                    avt = pb.on("tensor", lambda e, jp=jp, il=il, i=i, j=j, W=W, hl=hl: e.matmul(
                                        ps_o[il][:, 132:261],
                                        p2s[:, jp * 512 + (i - 4 * W) * 128:jp * 512 + (i - 4 * W) * 128 + 128],
                                        v_aug[:, (j * HPS + hl) * 132:(j * HPS + hl) * 132 + 129],
                                        start=False, stop=(j == i)), inc="st")
                                    if i == j:
                                        pb.wait("vector", "st", avt)
                                        tvc = pb.chain("vector", "sv", [
                                            lambda e, il=il: e.reciprocal(
                                                rsc[0][:, 0:2],
                                                bass.AP(ps_o[il], 128, [[pstr(ps_o[il]), 128], [132, 2], [1, 1]])),
                                            lambda e: e.tensor_tensor(
                                                rsc[0][:, 2:3], rsc[0][:, 1:2], neglam[:], ALU.mult),
                                            lambda e, il=il: e.tensor_scalar(
                                                rsc[1][:, :], ps_o[il][:, 0:128], rsc[0][:, 0:1], None, ALU.mult),
                                        ])
                                        pb.wait("vector", "sv", tvc)
                                        o_free[il] = pb.on("vector", lambda e, il=il, i=i, hl=hl: e.scalar_tensor_tensor(
                                            ost[:, (i * HPS + hl) * 128:(i * HPS + hl) * 128 + 128],
                                            ps_o[il][:, 132:260], rsc[0][:, 2:3], rsc[1][:, :],
                                            ALU.mult, ALU.add), inc="sv")
                                p_free[jp] = avt
                    last_ev = max(o_free.values())
                    pb.wait("sync", "sv", last_ev)
                    osp_dst = cm3(o_sp[:, sup * SUPW:(sup + 1) * SUPW])
                    t_osp = pb.on("sync", lambda e: e.dma_start(
                        out=osp_dst, in_=dst3(ost[:], NCH)), inc="d_osp", k=16)
                    tok["osp"] = t_osp
                    pb.emit()

                if stage == 2:
                    pd = Phase(ctx, "dbg2")
                    if "attn" in dbg:
                        odb = dout("ost", [128, NCH * SUPW], BF16)
                        pd.wait("sync", "sv", ctx.sems["sv"][1])
                        pd.on("sync", lambda e: e.dma_start(out=odb, in_=ost[:]), inc="d_out", k=16)
                    pd.emit()
                    _finish(ctx, out_d)
                    return nc, dbg_d

        if stage == 3:
            if "osp" in dbg:
                pd = Phase(ctx, "dbg3")
                odb = dout("osp", [T, E], BF16)
                pd.wait("sync", "d_osp", ctx.sems["d_osp"][1])
                pd.on("sync", lambda e: e.dma_start(out=odb, in_=o_sp), inc="d_out", k=16)
                pd.emit()
            _finish(ctx, out_d)
            return nc, dbg_d

        with ExitStack() as eQ:
            xq_fm = eQ.enter_context(nc.sbuf_tensor("xq_fm", [128, NCH * E], BF16))

            with ExitStack() as e2:
                sl2 = lambda n, sh, dt=F32: e2.enter_context(nc.sbuf_tensor(n, sh, dt))
                wo_sb = sl2("wo_sb", [128, NCH * E], BF16)
                oring = sl2("oring", [128, 4 * T], BF16)
                x2tm = sl2("x2tm", [128, E], BF16)
                x2fm = sl2("x2fm", [128, E], BF16)
                xqtm = sl2("xqtm", [128, E], BF16)
                xch = sl2("xch", [128, 2 * E])
                y1 = sl2("y1", [128, E])
                sq2 = sl2("sq2", [128, E])
                st2 = sl2("st2", [128, 16])
                ms2 = sl2("ms2", [128, 16])

                p2q = lambda n, sh, dt=F32: e2.enter_context(nc.psum_tensor(n, sh, dt))
                ps_go = [p2q("ps_go%d" % i, [128, 512]) for i in range(2)]
                ps_t2 = [p2q("ps_t2%d" % i, [128, 128], BF16) for i in range(2)]
                ps_y = [p2q("ps_y%d" % i, [128, 512]) for i in range(4)]

                p2 = Phase(ctx, "p2")
                p2.barrier()
                p2.wait("vector", "sv", tok["f0acc"])
                p2.wait("vector", "sv", tok["r127"])
                tmean = p2.chain("vector", "sv", [
                    lambda e: e.tensor_reduce(st2[0:2, 0:1], f0acc[:, 0:512], axis=AX, op=ALU.add),
                    lambda e: e.tensor_reduce(st2[0:2, 1:2], f0acc[:, 512:1024], axis=AX, op=ALU.add),
                    lambda e: e.tensor_scalar(st2[0:2, 4:6], st2[0:2, 0:2],
                                              1.0 / 512.0, None, ALU.mult),
                    lambda e: e.tensor_scalar(sq2[0:2, 0:512], f0acc[:, 0:512],
                                              st2[0:2, 4:5], None, ALU.subtract),
                    lambda e: e.tensor_scalar(sq2[0:2, 512:1024], f0acc[:, 512:1024],
                                              st2[0:2, 5:6], None, ALU.subtract),
                    lambda e: e.tensor_reduce(st2[0:2, 6:7], sq2[0:2, 0:512], axis=AX, op=ALU.add),
                    lambda e: e.tensor_reduce(st2[0:2, 7:8], sq2[0:2, 512:1024], axis=AX, op=ALU.add),
                    lambda e: e.tensor_tensor(st2[0:2, 0:2], st2[0:2, 0:2], st2[0:2, 6:8], ALU.add),
                    lambda e: e.tensor_scalar(st2[0:2, 0:2], st2[0:2, 0:2],
                                              1.0 / 16777216.0, None, ALU.mult),
                ])
                p2.wait("tensor", "sv", tmean)
                tb1 = p2.on("tensor", lambda e: e.matmul(
                    ps_y[0][:, 0:1], halfones_s[:, :], st2[0:2, 0:1], start=True, stop=True), inc="st")
                tb2 = p2.on("tensor", lambda e: e.matmul(
                    ps_y[1][:, 0:1], halfones_s[:, :], st2[0:2, 1:2], start=True, stop=True), inc="st")
                p2.wait("vector", "st", tb2)
                tmean2 = p2.chain("vector", "sv", [
                    lambda e: e.tensor_copy(ms2[:, 0:1], ps_y[0][:, 0:1]),
                    lambda e: e.tensor_copy(ms2[:, 1:2], ps_y[1][:, 0:1]),
                    lambda e: e.tensor_tensor(cstc[:, 3:4], ms2[:, 0:1], cstc[:, 0:1], ALU.mult),
                    lambda e: e.tensor_tensor(cstc[:, 4:5], ms2[:, 1:2], cstc[:, 0:1], ALU.mult),
                    lambda e: e.tensor_scalar(ms2[:, 0:2], ms2[:, 0:2], 1e-5, None, ALU.max),
                    lambda e: e.reciprocal(cstc[:, 1:3], ms2[:, 0:2]),
                ])
                tok["means"] = tmean2

                p2.wait("sync", "d_wo", tok["wo_cast"])
                wo_src = cm3(wo_bf_d)
                t_wo = p2.on("sync", lambda e: e.dma_start(
                    out=dst3(wo_sb[:], NCH), in_=wo_src), inc="d_w", k=16)

                orow = lambda c: o_sp[(c % NCH) * 128:(c % NCH) * 128 + 128, :]
                t_oin = {}
                for c in (-1, 0, 1):
                    t_oin[c % NCH] = p2.on("sync", lambda e, c=c: e.dma_start(
                        out=oring[:, (c % 4) * T:(c % 4) * T + T], in_=orow(c)), inc="d_oin", k=16)
                t_xc = {0: p2.on("sync", lambda e: e.dma_start(
                    out=xch[:, 0:E], in_=x_d[0:128, :]), inc="d_xc", k=16)}

                p2.wait("tensor", "sv", tok["gband"])
                p2.wait("tensor", "d_w", t_wo)
                p2.wait("vector", "sv", tok["rmsw"])
                p2.wait("vector", "sv", tok["means"])
                p2.wait("vector", "sv", tok["hamax0"])

                go_free = {}
                y_free = {}
                t2_free = {}
                xq_tr = {}
                gmm_last = {}
                tres_hist = {}
                tsq2_hist = {}
                for mi in range(NCH):
                    if mi + 2 <= NCH:
                        cnext = (mi + 2) % NCH
                        p2.wait("sync", "st", gmm_last.get(mi - 1))
                        t_oin[cnext] = p2.on("sync", lambda e, c=cnext: e.dma_start(
                            out=oring[:, (c % 4) * T:(c % 4) * T + T], in_=orow(c)), inc="d_oin", k=16)
                    if mi + 1 < NCH:
                        p2.wait("sync", "sv", tres_hist.get(mi - 1))
                        t_xc[mi + 1] = p2.on("sync", lambda e, m=mi + 1: e.dma_start(
                            out=xch[:, (m % 2) * E:(m % 2) * E + E],
                            in_=x_d[m * 128:(m + 1) * 128, :]), inc="d_xc", k=16)
                    for c in ((mi - 1) % NCH, mi, (mi + 1) % NCH):
                        p2.wait("tensor", "d_oin", t_oin[c])

                    rstd_tok = None
                    for wp in range(2):
                        gof = go_free.get((mi, 0)) if wp == 1 else go_free.get((mi - 1, 1))
                        if gof:
                            p2.wait("tensor", "sv", gof)
                        gmt = None
                        for wloc in range(2):
                            w = wp * 2 + wloc
                            for dd in range(3):
                                jb = (mi + dd - 1) % NCH
                                gmt = p2.on("tensor", lambda e, w=w, wloc=wloc, dd=dd, jb=jb: e.matmul(
                                    ps_go[wloc][:, :],
                                    gband[:, dd * 128:(dd + 1) * 128],
                                    oring[:, (jb % 4) * T + w * 512:(jb % 4) * T + w * 512 + 512],
                                    start=(dd == 0), stop=(dd == 2)), inc="st")
                        gmm_last[mi] = gmt
                        p2.wait("scalar", "st", gmt)
                        tsq = {}
                        for wloc in range(2):
                            w = wp * 2 + wloc
                            tsq[wloc] = p2.on("scalar", lambda e, wloc=wloc, w=w: e.activation(
                                sq2[:, w * 512:(w + 1) * 512], ps_go[wloc][:, :], AF.Square), inc="sa")
                        tred = None
                        for wloc in range(2):
                            w = wp * 2 + wloc
                            p2.wait("vector", "sa", tsq[wloc])
                            for g in range(4):
                                tred = p2.on("vector", lambda e, w=w, g=g: e.tensor_reduce(
                                    ms2[:, w * 4 + g:w * 4 + g + 1],
                                    sq2[:, w * 512 + g * 128:w * 512 + (g + 1) * 128],
                                    axis=AX, op=ALU.add), inc="sv")
                        p2.wait("scalar", "sv", tred)
                        ta1 = p2.on("scalar", lambda e, wp=wp: e.activation(
                            st2[:, wp * 8:wp * 8 + 8], ms2[:, wp * 8:wp * 8 + 8], AF.Ln,
                            scale=0.0078125, bias=cstc[:, 5:6]), inc="sa")
                        p2.wait("scalar", "sa", ta1)
                        rstd_tok = p2.on("scalar", lambda e, wp=wp: e.activation(
                            st2[:, wp * 8:wp * 8 + 8], st2[:, wp * 8:wp * 8 + 8], AF.Exp,
                            scale=-0.5), inc="sa")
                        p2.wait("vector", "sa", rstd_tok)
                        tev = None
                        for wloc in range(2):
                            w = wp * 2 + wloc
                            for g in range(4):
                                tev = p2.on("vector", lambda e, w=w, g=g, wloc=wloc: e.scalar_tensor_tensor(
                                    x2tm[:, w * 512 + g * 128:w * 512 + (g + 1) * 128],
                                    ps_go[wloc][:, g * 128:(g + 1) * 128],
                                    st2[:, w * 4 + g:w * 4 + g + 1],
                                    rmsw_eff[:, g * 128:(g + 1) * 128],
                                    ALU.mult, ALU.mult), inc="sv")
                        go_free[(mi, wp)] = tev

                    t_tr = None
                    for k in range(NCH):
                        b = k % 2
                        p2.wait("tensor", "sv", go_free[(mi, k // 8)] if k // 8 in (0, 1) else None)
                        tf = t2_free.get(b)
                        if tf:
                            p2.wait("tensor", "sa", tf)
                        tt = p2.on("tensor", lambda e, k=k, b=b: e.transpose(
                            ps_t2[b][:, :], x2tm[:, k * 128:(k + 1) * 128], ident[:]), inc="st")
                        p2.wait("scalar", "st", tt)
                        t2_free[b] = p2.on("scalar", lambda e, k=k, b=b: e.copy(
                            x2fm[:, k * 128:(k + 1) * 128], ps_t2[b][:, :]), inc="sa")
                    t_tr = t2_free[(NCH - 1) % 2]

                    p2.wait("tensor", "sa", t_tr)
                    ymt = None
                    for k in range(NCH):
                        if k == 0:
                            for ew in range(4):
                                yf = y_free.get(ew)
                                if yf:
                                    p2.wait("tensor", "sv", yf)
                        for ew in range(4):
                            ymt = p2.on("tensor", lambda e, k=k, ew=ew: e.matmul(
                                ps_y[ew][:, :], x2fm[:, k * 128:(k + 1) * 128],
                                wo_sb[:, k * E + ew * 512:k * E + ew * 512 + 512],
                                start=(k == 0), stop=(k == NCH - 1)), inc="st")

                    p2.wait("vector", "st", ymt)
                    p2.wait("vector", "d_xc", t_xc[mi])
                    p2.wait("vector", "sa", tsq2_hist.get(mi - 1) if mi else None)
                    tres = None
                    for ew in range(4):
                        tres = p2.on("vector", lambda e, ew=ew, mi=mi: e.tensor_tensor(
                            y1[:, ew * 512:(ew + 1) * 512], ps_y[ew][:, :],
                            xch[:, (mi % 2) * E + ew * 512:(mi % 2) * E + ew * 512 + 512],
                            ALU.add), inc="sv")
                        y_free[ew] = tres
                    tres_hist[mi] = tres
                    p2.wait("vector", "sv", tres)
                    tmu = p2.on("vector", lambda e: e.tensor_reduce(
                        st2[:, 0:1], y1[:], axis=AX, op=ALU.add), inc="sv")
                    p2.wait("scalar", "sv", tres)
                    tsq2 = p2.on("scalar", lambda e: e.activation(sq2[:], y1[:], AF.Square), inc="sa")
                    tsq2_hist[mi] = tsq2
                    p2.wait("vector", "sa", tsq2)
                    tss = p2.on("vector", lambda e: e.tensor_reduce(
                        st2[:, 1:2], sq2[:], axis=AX, op=ALU.add), inc="sv")
                    p2.wait("vector", "sv", max(tmu, tss))
                    tvar = p2.chain("vector", "sv", [
                        lambda e: e.tensor_scalar(st2[:, 0:2], st2[:, 0:2],
                                                  0.00048828125, None, ALU.mult),
                        lambda e: e.tensor_tensor(st2[:, 2:3], st2[:, 0:1], st2[:, 0:1], ALU.mult),
                        lambda e: e.tensor_tensor(st2[:, 1:2], st2[:, 1:2], st2[:, 2:3], ALU.subtract),
                        lambda e: e.tensor_scalar(st2[:, 1:2], st2[:, 1:2], 1e-5, None, ALU.add),
                    ])
                    p2.wait("scalar", "sv", tvar)
                    ta1 = p2.on("scalar", lambda e: e.activation(st2[:, 1:2], st2[:, 1:2], AF.Ln), inc="sa")
                    p2.wait("scalar", "sa", ta1)
                    ta2 = p2.on("scalar", lambda e: e.activation(st2[:, 1:2], st2[:, 1:2], AF.Exp, scale=-0.5), inc="sa")
                    p2.wait("vector", "sa", ta2)
                    tx3 = p2.on("vector", lambda e: e.tensor_scalar(
                        sq2[:], y1[:], st2[:, 0:1], st2[:, 1:2], ALU.subtract, ALU.mult), inc="sv")
                    p2.wait("sync", "sv", tx3)
                    t_x3 = p2.on("sync", lambda e, mi=mi: e.dma_start(
                        out=x3_sp[mi * 128:(mi + 1) * 128, :], in_=sq2[:]), inc="d_x3", k=16)
                    p2.wait("vector", "sv", tx3)
                    tq = p2.chain("vector", "sv", [
                        lambda e, mi=mi: e.tensor_reduce(
                            amax_all[:, mi:mi + 1], sq2[:], axis=AX, op=ALU.max,
                            apply_absolute_value=True),
                        lambda e, mi=mi: e.tensor_scalar(
                            st2[:, 3:4], amax_all[:, mi:mi + 1], 1e-5, None, ALU.max),
                        lambda e: e.reciprocal(st2[:, 3:4], st2[:, 3:4]),
                        lambda e: e.tensor_scalar(st2[:, 3:4], st2[:, 3:4], 127.0, None, ALU.mult),
                        lambda e: e.tensor_scalar(y1[:], sq2[:], st2[:, 3:4], None, ALU.mult),
                        lambda e: e.tensor_scalar(y1[:], y1[:], mcol[:], None, ALU.add),
                        lambda e: e.tensor_scalar(xqtm[:], y1[:], mcol[:], None, ALU.subtract),
                    ])
                    for k in range(NCH):
                        b = k % 2
                        if k == 0:
                            p2.wait("tensor", "sv", tq)
                        tf = t2_free.get(b)
                        if tf:
                            p2.wait("tensor", "sa", tf)
                        tt = p2.on("tensor", lambda e, k=k, b=b: e.transpose(
                            ps_t2[b][:, :], xqtm[:, k * 128:(k + 1) * 128], ident[:]), inc="st")
                        p2.wait("scalar", "st", tt)
                        t2_free[b] = p2.on("scalar", lambda e, k=k, b=b, mi=mi: e.copy(
                            xq_fm[:, (k * NCH + mi) * 128:(k * NCH + mi) * 128 + 128],
                            ps_t2[b][:, :]), inc="sa")
                    xq_tr[mi] = t2_free[(NCH - 1) % 2]
                    p2.wait("scalar", "d_x3", t_x3)
                    p2.wait("scalar", "sv", tq)
                tok["xq"] = max(xq_tr.values())
                tok["x3done"] = ctx.sems["d_x3"][1]
                p2.emit()

                if stage == 4 and "p2" in dbg:
                    pd = Phase(ctx, "dbgp2")
                    xdb = dout("xq_fm", [128, NCH * E], BF16)
                    adb = dout("amax", [128, NCH])
                    pd.wait("sync", "sa", ctx.sems["sa"][1])
                    pd.wait("sync", "sv", ctx.sems["sv"][1])
                    pd.on("sync", lambda e: e.dma_start(out=xdb, in_=xq_fm[:]), inc="d_out", k=16)
                    pd.on("sync", lambda e: e.dma_start(out=adb, in_=amax_all[:]), inc="d_out", k=16)
                    pd.emit()
            if stage == 4:
                _finish(ctx, out_d)
                return nc, dbg_d

            with ExitStack() as e3:
                sl3 = lambda n, sh, dt=F32: e3.enter_context(nc.sbuf_tensor(n, sh, dt))
                w1st = sl3("w1st", [128, NCH * 512])
                w1q = sl3("w1q", [128, NCH * 512], BF16)
                hbuf = [sl3("hbuf%d" % i, [128, 512]) for i in range(2)]
                w2st = sl3("w2st", [128, DFF])
                w2qb = sl3("w2qb", [128, DFF], BF16)
                rs1 = sl3("rs1", [128, NCH])
                tmpc = sl3("tmpc", [128, 4])

                ps_h = [e3.enter_context(nc.psum_tensor("ps_h%d" % i, [128, 512], F32))
                        for i in range(2)]

                p3 = Phase(ctx, "p3")
                p3.barrier()
                t_rs1 = p3.on("vector", lambda e: e.tensor_scalar(
                    rs1[:], amax_all[:], cstc[:, 3:4], None, ALU.mult), inc="sv")

                sw1_hi = None
                w1src = lambda fw: cm3(w1t_d[:, fw * 512:(fw + 1) * 512])
                t_w1d = {}
                t_w1d[0] = p3.on("sync", lambda e: e.dma_start(
                    out=dst3(w1st[:], NCH), in_=w1src(0)), inc="d_w1", k=16)
                hm_tok = {}
                hb_amax = {}
                q_done = None
                h_mm = {}
                t_w2q_all = None
                tq2 = None
                for fw in range(NCH):
                    p3.wait("gpsimd", "d_w1", t_w1d[fw])
                    if fw > 0:
                        p3.wait("gpsimd", "sv", q_done)
                    tg = p3.chain("gpsimd", "sg", [
                        lambda e: e.tensor_scalar(w1st[:], w1st[:], cstc[:, 1:2], None, ALU.mult),
                        lambda e: e.tensor_scalar(w1st[:], w1st[:], 1.0, -1.0, ALU.min, ALU.max),
                    ])
                    p3.wait("vector", "sg", tg)
                    if fw > 0:
                        p3.wait("vector", "st", h_mm.get(fw - 1))
                    q_done = p3.chain("vector", "sv", [
                        lambda e: e.tensor_scalar(w1st[:], w1st[:], mcol[:], None, ALU.add),
                        lambda e: e.tensor_scalar(w1q[:], w1st[:], mcol[:], None, ALU.subtract),
                    ])
                    if fw + 1 < NCH:
                        p3.wait("sync", "sv", q_done)
                        t_w1d[fw + 1] = p3.on("sync", lambda e, fw=fw: e.dma_start(
                            out=dst3(w1st[:], NCH), in_=w1src(fw + 1)), inc="d_w1", k=16)
                    p3.wait("tensor", "sv", q_done)
                    if fw == 0:
                        p3.wait("tensor", "sa", tok["xq"])
                    for m in range(NCH):
                        par = m % 2
                        hf = hm_tok.get((fw - 1, par) if m < 2 else (fw, (m - 2) % 2))
                        prev = (fw, m - 2) if m >= 2 else (fw - 1, NCH - 2 + m)
                        p3.wait("tensor", "sa", hm_tok.get(prev))
                        mm = None
                        for k in range(NCH):
                            mm = p3.on("tensor", lambda e, k=k, m=m, par=par: e.matmul(
                                ps_h[par][:, :],
                                xq_fm[:, (k * NCH + m) * 128:(k * NCH + m) * 128 + 128],
                                w1q[:, k * 512:(k + 1) * 512],
                                start=(k == 0), stop=(k == NCH - 1)), inc="st")
                        h_mm[fw] = mm
                        p3.wait("scalar", "st", mm)
                        p3.wait("scalar", "sv", t_rs1)
                        p3.wait("scalar", "d_hsp", hm_tok.get(("dma", par)))
                        p3.wait("scalar", "sv", hb_amax.get(par))
                        tev = p3.on("scalar", lambda e, par=par, m=m: e.activation(
                            hbuf[par][:, :], ps_h[par][:, :], AF.Silu,
                            scale=rs1[:, m:m + 1]), inc="sa")
                        hm_tok[(fw, m)] = tev
                        p3.wait("vector", "sa", tev)
                        p3.wait("vector", "sv", tok["hamax0"])
                        tam = p3.chain("vector", "sv", [
                            lambda e, par=par: e.tensor_reduce(
                                tmpc[:, 0:1], hbuf[par][:, :], axis=AX, op=ALU.max,
                                apply_absolute_value=True),
                            lambda e, m=m: e.tensor_tensor(
                                hamax[:, m:m + 1], hamax[:, m:m + 1], tmpc[:, 0:1], ALU.max),
                        ])
                        tok["hamax0"] = tam
                        hb_amax[par] = tam
                        p3.wait("sync", "sa", tev)
                        tdma = p3.on("sync", lambda e, m=m, fw=fw, par=par: e.dma_start(
                            out=h_sp[m * 128:(m + 1) * 128, fw * 512:(fw + 1) * 512],
                            in_=hbuf[par][:, :]), inc="d_hsp", k=16)
                        hm_tok[("dma", par)] = tdma
                    me = fw
                    p3.wait("sync", "sv", tq2)
                    t_w2d = p3.on("sync", lambda e, me=me: e.dma_start(
                        out=w2st[:], in_=w2tm_d[me]), inc="d_w2", k=16)
                    p3.wait("gpsimd", "d_w2", t_w2d)
                    tg2 = p3.chain("gpsimd", "sg", [
                        lambda e: e.tensor_scalar(w2st[:], w2st[:], cstc[:, 2:3], None, ALU.mult),
                        lambda e: e.tensor_scalar(w2st[:], w2st[:], 1.0, -1.0, ALU.min, ALU.max),
                    ])
                    p3.wait("vector", "sg", tg2)
                    if fw > 0:
                        p3.wait("vector", "d_w2q", t_w2q_all)
                    tq2 = p3.chain("vector", "sv", [
                        lambda e: e.tensor_scalar(w2st[:], w2st[:], mcol[:], None, ALU.add),
                        lambda e: e.tensor_scalar(w2qb[:], w2st[:], mcol[:], None, ALU.subtract),
                    ])
                    p3.wait("sync", "sv", tq2)
                    t_w2q_all = p3.on("sync", lambda e, me=me: e.dma_start(
                        out=w2q_sp[me], in_=w2qb[:]), inc="d_w2q", k=16)
                tok["w2q"] = t_w2q_all
                tok["hdone"] = ctx.sems["d_hsp"][1]
                tok["hamax"] = tok["hamax0"]
                p3.emit()

            if stage == 5:
                if "h" in dbg:
                    pd = Phase(ctx, "dbg5")
                    hdb = dout("h", [T, DFF])
                    hadb = dout("hamax", [128, NCH])
                    x3db = dout("x3", [T, E])
                    pd.wait("sync", "d_hsp", ctx.sems["d_hsp"][1])
                    pd.wait("sync", "d_x3", ctx.sems["d_x3"][1])
                    pd.wait("sync", "sv", ctx.sems["sv"][1])
                    pd.on("sync", lambda e: e.dma_start(out=hdb, in_=h_sp), inc="d_out", k=16)
                    pd.on("sync", lambda e: e.dma_start(out=hadb, in_=hamax[:]), inc="d_out", k=16)
                    pd.on("sync", lambda e: e.dma_start(out=x3db, in_=x3_sp), inc="d_out", k=16)
                    pd.emit()
                _finish(ctx, out_d)
                return nc, dbg_d

        with ExitStack() as e4:
            sl4 = lambda n, sh, dt=F32: e4.enter_context(nc.sbuf_tensor(n, sh, dt))
            hq_fm = sl4("hq_fm", [128, 64 * 512], BF16)
            hbuf2 = sl4("hbuf2", [128, DFF])
            xq2 = sl4("xq2", [128, DFF], BF16)
            w2sg = [sl4("w2sg%d" % i, [128, DFF], BF16) for i in range(2)]
            h2fm = sl4("h2fm", [128, NCH * 512], BF16)
            x3c = sl4("x3c", [128, E])
            y2 = sl4("y2", [128, E])
            sq4 = sl4("sq4", [128, E])
            st4 = sl4("st4", [128, 24])
            rs2 = sl4("rs2", [128, NCH])

            ps_h2 = [e4.enter_context(nc.psum_tensor("ps_h2%d" % i, [128, 512], F32))
                     for i in range(2)]
            ps_t4 = [e4.enter_context(nc.psum_tensor("ps_t4%d" % i, [128, 128], BF16))
                     for i in range(2)]

            p4 = Phase(ctx, "p4")
            p4.barrier()
            p4.wait("vector", "sv", tok["hamax"])
            t_rs2 = p4.on("vector", lambda e: e.tensor_scalar(
                rs2[:], hamax[:], cstc[:, 4:5], None, ALU.mult), inc="sv")
            t_sc2 = p4.chain("vector", "sv", [
                lambda e: e.tensor_scalar(st4[:, 0:NCH], hamax[:], 1e-5, None, ALU.max),
                lambda e: e.reciprocal(st4[:, 0:NCH], st4[:, 0:NCH]),
                lambda e: e.tensor_scalar(st4[:, 0:NCH], st4[:, 0:NCH], 127.0, None, ALU.mult),
            ])

            t_hin = {}
            t_w2in = {}
            t_x3in = {}
            tr4_free = {}
            h2_free = {}
            h2ev = {}
            bt_last = None
            hq_mm_done = None
            quant_prev = None
            out_tok = None
            out_prev = None
            out_prev_tev = {}
            for tw in range(4):
                for ml in range(4):
                    m = tw * 4 + ml
                    p4.wait("sync", "d_hsp", tok["hdone"])
                    p4.wait("sync", "sv", quant_prev)
                    t_hin[m] = p4.on("sync", lambda e, m=m: e.dma_start(
                        out=hbuf2[:], in_=h_sp[m * 128:(m + 1) * 128, :]), inc="d_hin", k=16)
                    p4.wait("vector", "d_hin", t_hin[m])
                    p4.wait("vector", "sv", t_sc2)
                    if hq_mm_done and ml == 0:
                        p4.wait("vector", "st", hq_mm_done)
                    if ml > 0:
                        p4.wait("vector", "st", hq_tr_last)
                    tq = p4.chain("vector", "sv", [
                        lambda e, m=m: e.tensor_scalar(hbuf2[:], hbuf2[:], st4[:, m:m + 1], None, ALU.mult),
                        lambda e: e.tensor_scalar(hbuf2[:], hbuf2[:], mcol[:], None, ALU.add),
                        lambda e: e.tensor_scalar(xq2[:], hbuf2[:], mcol[:], None, ALU.subtract),
                    ])
                    quant_prev = tq
                    tt = None
                    for k in range(64):
                        b = k % 2
                        if k == 0:
                            p4.wait("tensor", "sv", tq)
                            if hq_mm_done and ml == 0:
                                p4.wait("tensor", "st", hq_mm_done)
                        tf = tr4_free.get(b)
                        if tf:
                            p4.wait("tensor", tf[0], tf[1])
                        tt = p4.on("tensor", lambda e, k=k, b=b: e.transpose(
                            ps_t4[b][:, :], xq2[:, k * 128:(k + 1) * 128], ident[:]), inc="st")
                        p4.wait("scalar", "st", tt)
                        if hq_mm_done and ml == 0 and k < 2:
                            p4.wait("scalar", "st", hq_mm_done)
                        tr4_free[b] = ("sa", p4.on("scalar", lambda e, k=k, b=b, ml=ml: e.copy(
                            hq_fm[:, (k * 4 + ml) * 128:(k * 4 + ml) * 128 + 128],
                            ps_t4[b][:, :]), inc="sa"))
                    hq_tr_last = tt
                t_hq = tr4_free[63 % 2][1]

                p4.wait("sync", "d_w2q", tok["w2q"])
                p4.wait("sync", "st", h2_free.get((tw - 1, NCH - 1)))
                t_w2in[0] = p4.on("sync", lambda e: e.dma_start(
                    out=w2sg[0][:], in_=w2q_sp[0]), inc="d_w2s", k=16)
                for me in range(NCH):
                    par = me % 2
                    if me + 1 < NCH:
                        p4.wait("sync", "st", h2_free.get((tw, me - 1)))
                        t_w2in[me + 1] = p4.on("sync", lambda e, me=me: e.dma_start(
                            out=w2sg[(me + 1) % 2][:], in_=w2q_sp[me + 1]), inc="d_w2s", k=16)
                    p4.wait("tensor", "d_w2s", t_w2in[me])
                    if me == 0:
                        p4.wait("tensor", "sa", t_hq)
                    hprev = h2ev.get((tw, me - 2)) if me >= 2 else h2ev.get((tw - 1, NCH - 2 + me))
                    p4.wait("tensor", "sv", hprev)
                    mm = None
                    for k in range(64):
                        mm = p4.on("tensor", lambda e, k=k, me=me, par=par: e.matmul(
                            ps_h2[par][:, :], w2sg[par][:, k * 128:(k + 1) * 128],
                            hq_fm[:, k * 512:(k + 1) * 512],
                            start=(k == 0), stop=(k == 63)), inc="st")
                    h2_free[(tw, me)] = mm
                    hq_mm_done = mm
                    p4.wait("vector", "st", mm)
                    if tw > 0 and me < 2:
                        p4.wait("vector", "st", bt_last)
                    h2ev[(tw, me)] = p4.on("vector", lambda e, me=me, par=par: e.tensor_copy(
                        h2fm[:, me * 512:(me + 1) * 512], ps_h2[par][:, :]), inc="sv")
                t_h2 = h2ev[(tw, NCH - 1)]

                for ml in range(4):
                    m = tw * 4 + ml
                    p4.wait("sync", "d_x3", tok["x3done"])
                    p4.wait("sync", "sv", out_prev_tev.get(0))
                    t_x3in[m] = p4.on("sync", lambda e, m=m: e.dma_start(
                        out=x3c[:], in_=x3_sp[m * 128:(m + 1) * 128, :]), inc="d_oin", k=16)
                    p4.wait("vector", "d_oin", t_x3in[m])
                    p4.wait("vector", "sv", t_h2)
                    p4.wait("vector", "sv", t_rs2)
                    tev = None
                    if ml == 0:
                        p4.wait("tensor", "sv", t_h2)
                    for me in range(NCH):
                        b = me % 2
                        tf = tr4_free.get(b)
                        if tf:
                            p4.wait("tensor", tf[0], tf[1])
                        tt = p4.on("tensor", lambda e, me=me, ml=ml, b=b: e.transpose(
                            ps_t4[b][:, :], h2fm[:, me * 512 + ml * 128:me * 512 + ml * 128 + 128],
                            ident[:]), inc="st")
                        bt_last = tt
                        p4.wait("vector", "st", tt)
                        tev = p4.on("vector", lambda e, me=me, m=m, b=b: e.scalar_tensor_tensor(
                            y2[:, me * 128:(me + 1) * 128], ps_t4[b][:, :],
                            rs2[:, m:m + 1], x3c[:, me * 128:(me + 1) * 128],
                            ALU.mult, ALU.add), inc="sv")
                        tr4_free[b] = ("sv", tev)
                    out_prev_tev[0] = tev
                    p4.wait("vector", "sv", tev)
                    tmu = p4.on("vector", lambda e: e.tensor_reduce(
                        st4[:, 20:21], y2[:], axis=AX, op=ALU.add), inc="sv")
                    p4.wait("scalar", "sv", tev)
                    p4.wait("scalar", "d_out", out_prev)
                    tsq4 = p4.on("scalar", lambda e: e.activation(sq4[:], y2[:], AF.Square), inc="sa")
                    p4.wait("vector", "sa", tsq4)
                    tss = p4.on("vector", lambda e: e.tensor_reduce(
                        st4[:, 21:22], sq4[:], axis=AX, op=ALU.add), inc="sv")
                    p4.wait("vector", "sv", max(tmu, tss))
                    tvar = p4.chain("vector", "sv", [
                        lambda e: e.tensor_scalar(st4[:, 20:22], st4[:, 20:22],
                                                  0.00048828125, None, ALU.mult),
                        lambda e: e.tensor_tensor(st4[:, 22:23], st4[:, 20:21], st4[:, 20:21], ALU.mult),
                        lambda e: e.tensor_tensor(st4[:, 21:22], st4[:, 21:22], st4[:, 22:23], ALU.subtract),
                        lambda e: e.tensor_scalar(st4[:, 21:22], st4[:, 21:22], 1e-5, None, ALU.add),
                    ])
                    p4.wait("scalar", "sv", tvar)
                    ta1 = p4.on("scalar", lambda e: e.activation(st4[:, 21:22], st4[:, 21:22], AF.Ln), inc="sa")
                    p4.wait("scalar", "sa", ta1)
                    ta2 = p4.on("scalar", lambda e: e.activation(st4[:, 21:22], st4[:, 21:22], AF.Exp, scale=-0.5), inc="sa")
                    p4.wait("vector", "sa", ta2)
                    tout = p4.on("vector", lambda e: e.tensor_scalar(
                        sq4[:], y2[:], st4[:, 20:21], st4[:, 21:22],
                        ALU.subtract, ALU.mult), inc="sv")
                    p4.wait("sync", "sv", tout)
                    out_tok = p4.on("sync", lambda e, m=m: e.dma_start(
                        out=out_d[m * 128:(m + 1) * 128, :], in_=sq4[:]), inc="d_out", k=16)
                    out_prev = out_tok
                if "hqdump%d" % tw in dbg:
                    hqdb = dout("hq_tw%d" % tw, [128, 64 * 512], BF16)
                    h2db = dout("h2_tw%d" % tw, [128, NCH * 512], BF16)
                    p4.wait("sync", "sa", ctx.sems["sa"][1])
                    p4.wait("sync", "sv", ctx.sems["sv"][1])
                    p4.on("sync", lambda e, tw=tw: e.dma_start(out=hqdb, in_=hq_fm[:]), inc="d_out", k=16)
                    p4.on("sync", lambda e, tw=tw: e.dma_start(out=h2db, in_=h2fm[:]), inc="d_out", k=16)
                    break
                if tw == 0 and "p4m0" in dbg:
                    y2db = dout("y2m0", [128, E])
                    p4.wait("sync", "sv", out_prev_tev[0])
                    p4.on("sync", lambda e: e.dma_start(out=y2db, in_=y2[:]), inc="d_out", k=16)
                if tw == 0 and "p4t0" in dbg:
                    hqdb = dout("hq_fm", [128, 64 * 512], BF16)
                    h2db = dout("h2fm", [128, NCH * 512], BF16)
                    y2db = dout("y2", [128, E])
                    rsdb = dout("rs2", [128, NCH])
                    p4.wait("sync", "sa", ctx.sems["sa"][1])
                    p4.wait("sync", "sv", ctx.sems["sv"][1])
                    p4.on("sync", lambda e: e.dma_start(out=hqdb, in_=hq_fm[:]), inc="d_out", k=16)
                    p4.on("sync", lambda e: e.dma_start(out=h2db, in_=h2fm[:]), inc="d_out", k=16)
                    p4.on("sync", lambda e: e.dma_start(out=y2db, in_=y2[:]), inc="d_out", k=16)
                    p4.on("sync", lambda e: e.dma_start(out=rsdb, in_=rs2[:]), inc="d_out", k=16)
                    break
            p4.emit()

        _finish(ctx, out_d)
    return nc, dbg_d


def _finish(ctx, out_d):
    p = Phase(ctx, "fin")
    for s, rec in ctx.sems.items():
        if rec[1] > 0:
            p.wait("sync", s, rec[1])
    p.emit()


def kernel(**inputs):
    shards = host_prepare(
        inputs["L0_attn_query"], inputs["L0_attn_Wq"], inputs["L0_attn_Wk"],
        inputs["L0_attn_Wv"], inputs["L0_attn_Wout"],
        inputs["L0_attn_lambda_q1"], inputs["L0_attn_lambda_k1"],
        inputs["L0_attn_lambda_q2"], inputs["L0_attn_lambda_k2"],
        inputs["L0_attn_rms_weight"], inputs["L0_ffn_W1"], inputs["L0_ffn_W2"])
    nc, _ = build_nc()
    res = run_bass_kernel_spmd(nc, shards, core_ids=list(range(len(shards))))
    return np.stack([np.asarray(r["out"], dtype=np.float32) for r in res.results])


# version 05
# speedup vs baseline: 1.3872x; 1.3872x over previous
import math
from contextlib import ExitStack

import numpy as np
import ml_dtypes

import concourse.bass as bass
import concourse.mybir as mybir
from concourse.alu_op_type import AluOpType as ALU
from concourse.bass_utils import run_bass_kernel_spmd

F32 = mybir.dt.float32
BF16 = mybir.dt.bfloat16
AF = mybir.ActivationFunctionType
AX = mybir.AxisListType.X

T = 2048
E = 2048
H = 16
D = 64
DFF = 8192
NCH = 16
HPS = 2
NSUP = H // HPS
SUPW = HPS * 128
LAMBDA_INIT = 0.8 - 0.6 * math.exp(-0.3 * 12)
MAGIC32 = 12582912.0
MAGIC_MUL = 27021597764222976.0
CLAMP_PI = float(np.nextafter(np.float32(np.pi), np.float32(0)))


def _split(c):
    hi = float(np.float32(c))
    lo = float(np.float32(c - hi))
    return hi, lo


def host_prepare(x_all, Wq, Wk, Wv, Wout, lq1, lk1, lq2, lk2, rmsw, W1, W2):
    f32 = np.float32
    x_all = np.asarray(x_all)
    perm = np.empty(D, dtype=np.int64)
    perm[:32] = np.arange(0, D, 2)
    perm[32:] = np.arange(1, D, 2)
    full_perm = (np.arange(2 * H)[:, None] * D + perm[None, :]).reshape(-1)
    wq_p = np.ascontiguousarray(np.asarray(Wq, f32)[:, full_perm])
    wk_p = np.ascontiguousarray(np.asarray(Wk, f32)[:, full_perm])

    w1t = np.ascontiguousarray(np.asarray(W1, f32).T)
    w2tm = np.ascontiguousarray(
        np.asarray(W2, f32).T.reshape(64, 128, 16, 128)
        .transpose(2, 1, 0, 3).reshape(16, 128, DFF))

    tcols = (np.arange(NCH)[None, :] * 128 + np.arange(128)[:, None]).astype(f32)
    pfreq = np.broadcast_to(np.arange(32, dtype=f32)[None, :], (128, 32)).copy()

    k_ = np.arange(1152, dtype=np.int64)[:, None]
    j_ = np.arange(T, dtype=np.int64)[None, :]
    m = ((k_ * j_ + 512 + 1024) % T) - 1024
    absm = np.abs(m)
    big = absm > 512
    m2 = np.where(big, np.sign(m) * (1024 - absm), m)
    sgn = np.ones_like(m, dtype=np.float64)

    kcol = (np.arange(9)[None, :] * 128 + np.arange(128)[:, None]).astype(f32)
    kk = np.arange(9 * 128).reshape(9, 128).T
    wcol = np.zeros((128, 9), dtype=f32)
    wcol[(kk >= 1) & (kk <= 1023)] = 2.0
    wcol[kk == 0] = 1.0
    wcol[kk == 1024] = 1.0

    common = {
        "wq": wq_p, "wk": wk_p,
        "wv": np.ascontiguousarray(np.asarray(Wv, f32)),
        "wo": np.ascontiguousarray(np.asarray(Wout, f32)),
        "w1t": w1t, "w2tm": w2tm,
        "lq1": np.tile(np.asarray(lq1, f32).reshape(1, D), (128, 1)),
        "lk1": np.tile(np.asarray(lk1, f32).reshape(1, D), (128, 1)),
        "lq2": np.tile(np.asarray(lq2, f32).reshape(1, D), (128, 1)),
        "lk2": np.tile(np.asarray(lk2, f32).reshape(1, D), (128, 1)),
        "rmsw128": np.tile(np.asarray(rmsw, f32).reshape(1, 128), (128, 4)),
        "tcols": tcols, "pfreq": pfreq,
        "mg2": m2.astype(f32), "gsgn": sgn.astype(f32),
        "kcol": kcol, "wcol": wcol,
        "ident": np.eye(128, dtype=ml_dtypes.bfloat16),
        "identf": np.eye(128, dtype=f32),
        "triu": np.triu(np.ones((128, 128))).astype(ml_dtypes.bfloat16),
        "halfones": np.full((2, 128), 0.5, dtype=f32),
        "ones2": np.full((128, 2), 1.0, dtype=ml_dtypes.bfloat16),
    }
    shards = []
    for c in range(x_all.shape[0]):
        x = np.ascontiguousarray(np.asarray(x_all[c], f32))
        d = dict(common)
        d["x"] = x
        d["xt"] = np.ascontiguousarray(x.T)
        shards.append(d)
    return shards


class Ctx:
    def __init__(self, nc, es):
        self.nc = nc
        self.es = es
        self.sems = {}

    def sem(self, name):
        if name not in self.sems:
            h = self.es.enter_context(self.nc.semaphore(name))
            self.sems[name] = [h, 0]


class Phase:
    ENGINES = ("sync", "vector", "scalar", "tensor", "gpsimd")

    def __init__(self, ctx, name=""):
        self.ctx = ctx
        self.name = name
        self.items = {e: [] for e in self.ENGINES}

    def on(self, eng, fn, inc=None, k=1):
        tok = None
        if inc is not None:
            s = self.ctx.sems[inc]
            s[1] += k
            tok = s[1]
            self.items[eng].append(("i", fn, s[0], k))
        else:
            self.items[eng].append(("p", fn, None, 0))
        return tok

    def wait(self, eng, sem, tok):
        if tok is None or tok <= 0:
            return
        h = self.ctx.sems[sem][0]
        self.items[eng].append(("w", None, h, tok))

    def chain(self, eng, sem, fns):
        tok = None
        for fn in fns:
            if tok is not None:
                self.wait(eng, sem, tok)
            tok = self.on(eng, fn, inc=sem)
        return tok

    def barrier(self, engines=("tensor", "vector", "scalar", "sync", "gpsimd")):
        for eng in engines:
            for s in ("sv", "sa", "st", "sg"):
                self.wait(eng, s, self.ctx.sems[s][1])

    def emit(self):
        nc = self.ctx.nc
        with nc.Block() as block:
            def runner(eng_name):
                def run(e):
                    for kind, fn, h, k in self.items[eng_name]:
                        if kind == "w":
                            e.wait_ge(h, k)
                        elif kind == "i":
                            fn(e).then_inc(h, k)
                        else:
                            fn(e)
                return run
            for eng_name in self.ENGINES:
                if self.items[eng_name]:
                    getattr(block, eng_name)(runner(eng_name))


def pstr(t):
    return t.ap().ap[0][0]


def cm3(dram_ap, p=128):
    return dram_ap.rearrange("(k p) c -> p k c", p=p)


def dst3(sb_ap, k):
    return sb_ap.rearrange("p (k c) -> p k c", k=k)


def build_nc(stage=99, debug=()):
    nc = bass.Bass("TRN2", target_bir_lowering=False)
    dbg = set(debug)
    dbg_d = {}

    def din(name, shape, dt=F32):
        return nc.dram_tensor(name, shape, dt, kind="ExternalInput").ap()

    x_d = din("x", [T, E])
    xt_d = din("xt", [E, T])
    wq_d = din("wq", [E, E]); wk_d = din("wk", [E, E])
    wv_d = din("wv", [E, E]); wo_d = din("wo", [E, E])
    w1t_d = din("w1t", [E, DFF])
    w2tm_d = din("w2tm", [16, 128, DFF])
    lq1_d = din("lq1", [128, D]); lk1_d = din("lk1", [128, D])
    lq2_d = din("lq2", [128, D]); lk2_d = din("lk2", [128, D])
    rmsw_d = din("rmsw128", [128, 512])
    tcols_d = din("tcols", [128, NCH])
    pfreq_d = din("pfreq", [128, 32])
    mg2_d = din("mg2", [1152, T]); gsgn_d = din("gsgn", [1152, T])
    kcol_d = din("kcol", [128, 9]); wcol_d = din("wcol", [128, 9])
    ident_d = din("ident", [128, 128], BF16)
    identf_d = din("identf", [128, 128], F32)
    triu_d = din("triu", [128, 128], BF16)
    halfones_d = din("halfones", [2, 128])
    ones2_d = din("ones2", [128, 2], BF16)

    out_d = nc.dram_tensor("out", [T, E], F32, kind="ExternalOutput").ap()

    o_sp = nc.dram_tensor("o_sp", [T, E], BF16).ap()
    x3_sp = nc.dram_tensor("x3_sp", [T, E], F32).ap()
    h_spT = nc.dram_tensor("h_spT", [DFF, T], F32).ap()
    row_sp_t = nc.dram_tensor("row_sp", [1, T], F32)
    w2q_sp = nc.dram_tensor("w2q_sp", [16, 128, DFF], BF16).ap()
    g_sp_t = nc.dram_tensor("g_sp", [1, 512], F32)
    g_sp = g_sp_t.ap()
    wo_bf_d = nc.dram_tensor("wo_bf", [E, E], BF16).ap()

    def dout(name, shape, dt=F32):
        dbg_d[name] = nc.dram_tensor("dbg_" + name, shape, dt,
                                     kind="ExternalOutput").ap()
        return dbg_d[name]

    es = ExitStack()
    with es:
        ctx = Ctx(nc, es)
        for s in ("sv", "sa", "st", "sg",
                  "d_tab", "d_xt", "d_w", "d_wf0", "d_osp", "d_out", "d_lam",
                  "d_g", "d_oin", "d_xc", "d_wo", "d_w1", "d_hsp", "d_w2",
                  "d_x3", "d_hin", "d_w2q", "d_w2s", "d_gb", "d_row",
                  "d_hin0", "d_hin1", "d_hsp0", "d_hsp1", "d_w2s0", "d_w2s1",
                  "d_oin0", "d_oin1", "d_oin2", "d_oin3", "d_xc0", "d_xc1",
                  "d_wf00", "d_wf01"):
            ctx.sem(s)

        def sb(name, shape, dt):
            return es.enter_context(nc.sbuf_tensor(name, shape, dt))

        ident = sb("identS", [128, 128], BF16)
        identf_s = sb("identfS", [128, 128], F32)
        triu_m = sb("triuS", [128, 128], BF16)
        ucol = sb("ucol", [128, 1], F32)
        mcol = sb("mcol", [128, 1], F32)
        rmsw_eff = sb("rmsw_eff", [128, 512], F32)
        neglam = sb("neglam", [128, 1], F32)
        gband = sb("gband", [128, 3 * 128], BF16)
        cos_all = sb("cos_all", [128, 512], F32)
        sin_all = sb("sin_all", [128, 512], F32)
        halfones_s = sb("halfonesS", [2, 128], F32)
        ones2_s = sb("ones2S", [128, 2], BF16)
        cstc = sb("cstc", [128, 8], F32)
        f0acc = sb("f0acc", [2, 1024], F32)
        amax_all = sb("amax_all", [128, NCH], F32)
        hamax = sb("hamax", [128, NCH], F32)

        tok = {}

        with ExitStack() as e0:
            sl = lambda n, sh, dt=F32: e0.enter_context(nc.sbuf_tensor(n, sh, dt))
            tcols_s = sl("tcolsS", [128, NCH])
            pfreq_s = sl("pfreqS", [128, 32])
            invf = sl("invf", [128, 32])
            lamq1 = sl("lamq1", [128, D]); lamk1 = sl("lamk1", [128, D])
            lamq2 = sl("lamq2", [128, D]); lamk2 = sl("lamk2", [128, D])
            rmsw_s = sl("rmsw_s", [128, 512])
            kcol_s = sl("kcol_s", [128, 9])
            wcol_s = sl("wcol_s", [128, 9])
            wfilt2 = sl("wfilt2", [128, 18], BF16)
            sA = sl("p0_sA", [128, 512]); sB = sl("p0_sB", [128, 512])
            sC = sl("p0_sC", [128, 512])
            theta = sl("p0_theta", [128, 512])
            gchA = sl("p0_gA", [128, T]); gchB = sl("p0_gB", [128, T])
            gcos = sl("p0_gcos", [128, T])
            gcos_bf = sl("p0_gcosbf", [128, T], BF16)
            g_ext = sl("p0_gext", [1, 512])
            grev = sl("p0_grev", [1, 512])
            lcol = sl("p0_lcol", [128, 4])
            gps = e0.enter_context(nc.psum_tensor("p0_gps", [2, 2048], F32))

            p = Phase(ctx, "p0")
            t_tab = None
            for (d_, s_) in [(ident, ident_d), (identf_s, identf_d),
                             (triu_m, triu_d), (tcols_s, tcols_d),
                             (pfreq_s, pfreq_d), (halfones_s, halfones_d),
                             (ones2_s, ones2_d), (kcol_s, kcol_d), (wcol_s, wcol_d)]:
                t_tab = p.on("sync", lambda e, d=d_, s=s_: e.dma_start(out=d[:], in_=s),
                             inc="d_tab", k=16)
            t_lam = None
            for (d_, s_) in [(lamq1, lq1_d), (lamk1, lk1_d), (lamq2, lq2_d),
                             (lamk2, lk2_d), (rmsw_s, rmsw_d)]:
                t_lam = p.on("sync", lambda e, d=d_, s=s_: e.dma_start(out=d[:], in_=s),
                             inc="d_lam", k=16)
            tok["wo_cast"] = p.on("gpsimd", lambda e: e.dma_start(out=wo_bf_d, in_=wo_d),
                                  inc="d_wo", k=16)

            p.wait("vector", "d_tab", t_tab)
            tok["mcol"] = p.chain("vector", "sv", [
                lambda e: e.memset(ucol[:], 0.25),
                lambda e: e.tensor_scalar(ucol[:], ucol[:], 8388608.0, None, ALU.add),
                lambda e: e.tensor_scalar(ucol[:], ucol[:], 8388608.0, None, ALU.subtract),
                lambda e: e.tensor_scalar(mcol[:], ucol[:], MAGIC_MUL, MAGIC32,
                                          ALU.mult, ALU.add),
            ])
            tok["f0acc"] = p.on("vector", lambda e: e.memset(f0acc[:], 0.0), inc="sv")
            tok["r127"] = p.chain("vector", "sv", [
                lambda e: e.memset(cstc[:, 0:1], 127.0),
                lambda e: e.reciprocal(cstc[:, 0:1], cstc[:, 0:1]),
                lambda e: e.memset(cstc[:, 5:6], 1e-5),
            ])

            cln_hi, cln_lo = _split(-math.log(10000.0) / 32.0)
            tv = p.chain("vector", "sv", [
                lambda e: e.tensor_scalar(sA[:, 0:32], pfreq_s[:], cln_hi, None, ALU.mult),
                lambda e: e.tensor_scalar(sB[:, 0:32], pfreq_s[:], cln_lo, None, ALU.mult),
                lambda e: e.tensor_tensor(sC[:, 0:32], sA[:, 0:32], sB[:, 0:32], ALU.add),
            ])
            p.wait("scalar", "sv", tv)
            ta = p.on("scalar", lambda e: e.activation(invf[:], sC[:, 0:32], AF.Exp),
                      inc="sa")
            p.wait("vector", "sa", ta)
            tcb = bass.AP(tcols_s, 0, [[pstr(tcols_s), 128], [1, NCH], [0, 32]])
            ivb = bass.AP(invf, 0, [[pstr(invf), 128], [0, NCH], [1, 32]])
            th3 = theta[:].rearrange("p (a b) -> p a b", a=NCH)
            t_theta = p.on("vector",
                           lambda e: e.tensor_tensor(th3, tcb, ivb, ALU.mult), inc="sv")

            i2p_hi, i2p_lo = _split(1.0 / (2 * math.pi))
            p2a = float(np.float32(2 * math.pi))
            p2b = float(np.float32(2 * math.pi - p2a))
            p2c = float(np.float32(2 * math.pi - p2a - p2b))
            hp_hi, hp_lo = _split(math.pi / 2)

            def reduce_sin(out_t, src_t, shift, tok_in):
                fns = []
                if shift:
                    fns += [
                        lambda e: e.tensor_scalar(sA[:], src_t[:], hp_hi, None, ALU.add),
                        lambda e: e.tensor_scalar(sA[:], sA[:], hp_lo, None, ALU.add),
                    ]
                    src = sA
                else:
                    src = src_t
                fns += [
                    lambda e: e.tensor_scalar(sB[:], src[:], i2p_hi, None, ALU.mult),
                    lambda e: e.tensor_scalar(sC[:], src[:], i2p_lo, None, ALU.mult),
                    lambda e: e.tensor_tensor(sB[:], sB[:], sC[:], ALU.add),
                    lambda e: e.tensor_scalar(sB[:], sB[:], mcol[:], None, ALU.add),
                    lambda e: e.tensor_scalar(sB[:], sB[:], mcol[:], None, ALU.subtract),
                    lambda e: e.tensor_scalar(sC[:], sB[:], p2a, None, ALU.mult),
                    lambda e: e.tensor_tensor(out_t[:], src[:], sC[:], ALU.subtract),
                    lambda e: e.tensor_scalar(sC[:], sB[:], p2b, None, ALU.mult),
                    lambda e: e.tensor_tensor(out_t[:], out_t[:], sC[:], ALU.subtract),
                    lambda e: e.tensor_scalar(sC[:], sB[:], p2c, None, ALU.mult),
                    lambda e: e.tensor_tensor(out_t[:], out_t[:], sC[:], ALU.subtract),
                    lambda e: e.tensor_scalar(out_t[:], out_t[:], CLAMP_PI, -CLAMP_PI,
                                              ALU.min, ALU.max),
                ]
                if tok_in is not None:
                    p.wait("vector", "sv", tok_in)
                tvv = p.chain("vector", "sv", fns)
                p.wait("scalar", "sv", tvv)
                return p.on("scalar",
                            lambda e: e.activation(out_t[:], out_t[:], AF.Sin), inc="sa")

            ta_sin = reduce_sin(sin_all, theta, False, t_theta)
            p.wait("vector", "sa", ta_sin)
            ta_cos = reduce_sin(cos_all, theta, True, None)
            tok["sincos"] = ta_cos

            li_hi, li_lo = _split(LAMBDA_INIT)
            p.wait("vector", "d_lam", t_lam)
            p.wait("vector", "sa", ta_cos)
            tv = p.chain("vector", "sv", [
                lambda e: e.tensor_tensor(sA[:, 0:D], lamq1[:], lamk1[:], ALU.mult),
                lambda e: e.tensor_tensor(sB[:, 0:D], lamq2[:], lamk2[:], ALU.mult),
                lambda e: e.tensor_reduce(lcol[:, 0:1], sA[:, 0:D], axis=AX, op=ALU.add),
                lambda e: e.tensor_reduce(lcol[:, 1:2], sB[:, 0:D], axis=AX, op=ALU.add),
            ])
            p.wait("scalar", "sv", tv)
            ta = p.on("scalar", lambda e: e.activation(lcol[:, 2:4], lcol[:, 0:2], AF.Exp),
                      inc="sa")
            p.wait("vector", "sa", ta)
            tok["neglam"] = p.chain("vector", "sv", [
                lambda e: e.tensor_tensor(neglam[:], lcol[:, 2:3], lcol[:, 3:4], ALU.subtract),
                lambda e: e.tensor_scalar(neglam[:], neglam[:], li_hi, None, ALU.add),
                lambda e: e.tensor_scalar(neglam[:], neglam[:], li_lo, None, ALU.add),
                lambda e: e.tensor_scalar(neglam[:], neglam[:], -1.0, None, ALU.mult),
            ])

            om_hi, om_lo = _split(1.0 - LAMBDA_INIT)
            tok["rmsw"] = p.chain("vector", "sv", [
                lambda e: e.tensor_scalar(sA[:], rmsw_s[:], om_hi, None, ALU.mult),
                lambda e: e.tensor_scalar(sB[:], rmsw_s[:], om_lo, None, ALU.mult),
                lambda e: e.tensor_tensor(rmsw_eff[:], sA[:], sB[:], ALU.add),
            ])

            u_hi, u_lo = _split(1.0 / (1024.0 * 0.15))
            g_hi, g_lo = _split(math.pi / 1024.0)
            tv = p.chain("vector", "sv", [
                lambda e: e.tensor_scalar(sA[:, 0:9], kcol_s[:], u_hi, None, ALU.mult),
                lambda e: e.tensor_scalar(sB[:, 0:9], kcol_s[:], u_lo, None, ALU.mult),
                lambda e: e.tensor_tensor(sA[:, 0:9], sA[:, 0:9], sB[:, 0:9], ALU.add),
                lambda e: e.tensor_tensor(sB[:, 0:9], sA[:, 0:9], sA[:, 0:9], ALU.mult),
            ])
            p.wait("scalar", "sv", tv)
            ta = p.on("scalar",
                      lambda e: e.activation(sC[:, 0:9], sB[:, 0:9], AF.Exp, scale=-0.5),
                      inc="sa")
            p.wait("vector", "sa", ta)
            w2vo = wfilt2[:].rearrange("p (a b) -> p a b", a=9)
            s3v = bass.AP(sC, 9, [[pstr(sC), 128], [1, 9], [0, 2]])
            t_wfilt = p.chain("vector", "sv", [
                lambda e: e.tensor_tensor(sC[:, 9:18], sC[:, 0:9], wcol_s[:], ALU.mult),
                lambda e: e.tensor_scalar(sC[:, 9:18], sC[:, 9:18], 0.00048828125,
                                          None, ALU.mult),
                lambda e: e.tensor_copy(w2vo, s3v),
            ])
            p.wait("tensor", "sv", t_wfilt)

            t_gmm = 0
            t_gread = None
            for r in range(9):
                p.wait("sync", "sv", t_gread)
                t_gd = p.on("sync", lambda e, r=r: e.dma_start(
                    out=gchA[:], in_=mg2_d[r * 128:(r + 1) * 128, :]), inc="d_g", k=16)
                t_gd = p.on("sync", lambda e, r=r: e.dma_start(
                    out=gchB[:], in_=gsgn_d[r * 128:(r + 1) * 128, :]), inc="d_g", k=16)
                p.wait("vector", "d_g", t_gd)
                if t_gmm:
                    p.wait("vector", "st", t_gmm)
                fns = [lambda e: e.tensor_scalar(gcos[:], gchA[:], g_hi, None, ALU.mult)]
                for pc in range(4):
                    fns += [
                        lambda e, pc=pc: e.tensor_scalar(
                            sA[:], gchA[:, pc * 512:(pc + 1) * 512], g_lo, None, ALU.mult),
                        lambda e, pc=pc: e.tensor_tensor(
                            gcos[:, pc * 512:(pc + 1) * 512],
                            gcos[:, pc * 512:(pc + 1) * 512], sA[:], ALU.add),
                    ]
                tv = p.chain("vector", "sv", fns)
                p.wait("scalar", "sv", tv)
                ta = p.on("scalar", lambda e: e.activation(gcos[:], gcos[:], AF.Sin),
                          inc="sa")
                p.wait("vector", "sa", ta)
                tv = p.chain("vector", "sv", [
                    lambda e: e.tensor_tensor(gcos[:], gcos[:], gchB[:], ALU.mult),
                    lambda e: e.tensor_copy(gcos_bf[:], gcos[:]),
                ])
                t_gread = tv
                p.wait("tensor", "sv", tv)
                for w in range(4):
                    t_gmm = p.on("tensor", lambda e, w=w, r=r: e.matmul(
                        gps[:, w * 512:(w + 1) * 512], wfilt2[:, 2 * r:2 * r + 2],
                        gcos_bf[:, w * 512:(w + 1) * 512],
                        start=(r == 0), stop=(r == 8)), inc="st")
            p.wait("vector", "st", t_gmm)
            t_grev = p.chain("vector", "sv", [
                lambda e: e.tensor_copy(sA[0:1, 0:512], gps[0:1, 0:512]),
                lambda e: e.tensor_copy(sB[0:1, 0:512], gps[0:1, 512:1024]),
                lambda e: e.tensor_copy(sC[0:1, 0:512], gps[0:1, 1024:1536]),
                lambda e: e.tensor_copy(theta[0:1, 0:512], gps[0:1, 1536:2048]),
                lambda e: e.tensor_copy(g_ext[0:1, 0:256], theta[0:1, 256:512]),
                lambda e: e.tensor_copy(g_ext[0:1, 256:512], sA[0:1, 0:256]),
            ])
            p.wait("sync", "sv", t_grev)
            t_gsp = p.on("sync", lambda e: e.dma_start(out=g_sp, in_=g_ext[0:1, :]),
                         inc="d_g", k=16)
            p.wait("gpsimd", "d_g", t_gsp)
            t_gbd = None
            for di in range(3):
                base = 129 + 128 * (di - 1)
                t_gbd = p.on("sync", lambda e, di=di, base=base: e.dma_start(
                    out=gcos[:, di * 128:(di + 1) * 128],
                    in_=bass.AP(g_sp_t, base, [[1, 128], [1, 128]])), inc="d_g", k=16)
            p.wait("vector", "d_g", t_gbd)
            t_gb = None
            for di in range(3):
                t_gb = p.on("vector", lambda e, di=di: e.tensor_copy(
                    gband[:, di * 128:(di + 1) * 128],
                    bass.AP(gcos, di * 128 + 127, [[pstr(gcos), 128], [-1, 128]])),
                    inc="sv")
            tok["gband"] = t_gb

            if "g" in dbg:
                gdb = dout("g", [1, 512])
                p.wait("sync", "sv", t_grev)
                p.on("sync", lambda e: e.dma_start(out=gdb, in_=g_ext[0:1, :]),
                     inc="d_out", k=16)
                gbdb = dout("gband", [128, 384], BF16)
                p.wait("sync", "sv", t_gb)
                p.on("sync", lambda e: e.dma_start(out=gbdb, in_=gband[:]),
                     inc="d_out", k=16)
            if "tabs" in dbg:
                cdb = dout("cos", [128, 512]); sdb = dout("sin", [128, 512])
                ldb = dout("neglam", [128, 1])
                p.wait("sync", "sa", tok["sincos"])
                p.on("sync", lambda e: e.dma_start(out=cdb, in_=cos_all[:]), inc="d_out", k=16)
                p.on("sync", lambda e: e.dma_start(out=sdb, in_=sin_all[:]), inc="d_out", k=16)
                p.wait("sync", "sv", tok["neglam"])
                p.on("sync", lambda e: e.dma_start(out=ldb, in_=neglam[:]), inc="d_out", k=16)
            p.emit()

        if stage == 0:
            _finish(ctx, out_d)
            return nc, dbg_d

        with ExitStack() as eP1:
            sl1 = lambda n, sh, dt: eP1.enter_context(nc.sbuf_tensor(n, sh, dt))
            xt_bf = sl1("xt_bf", [128, NCH * T], BF16)
            xt_src = cm3(xt_d)
            ph_x = Phase(ctx, "p1x")
            tok["xt"] = ph_x.on("gpsimd", lambda e: e.dma_start(
                out=dst3(xt_bf[:], NCH), in_=xt_src), inc="d_xt", k=16)
            ph_x.emit()

            q_fm = sl1("q_fm", [128, HPS * T], BF16)
            k_fm = sl1("k_fm", [128, HPS * T], BF16)
            v_aug = sl1("v_aug", [128, NCH * HPS * 132], BF16)
            wst_q = sl1("wst_q", [128, NCH * SUPW], BF16)
            wst_k = sl1("wst_k", [128, NCH * SUPW], BF16)
            wst_v = sl1("wst_v", [128, NCH * SUPW], BF16)
            qtm = sl1("qtm", [128, 2 * SUPW], BF16)
            ktm = sl1("ktm", [128, 2 * SUPW], BF16)
            rsc = [sl1("rsc%d" % i, [128, 128], F32) for i in range(4)]
            sqb = sl1("sqb", [128, T], BF16)
            negc = sl1("negc", [128, 4], F32)
            ncsc = sl1("ncsc", [2, 16], F32)
            wf0 = sl1("wf0", [128, 2 * 2048], F32)
            wf0a = sl1("wf0a", [128, 2048], BF16)
            wf0f = sl1("wf0f", [128, 2048], F32)
            wf0lo = sl1("wf0lo", [128, 2048], BF16)
            p1s = sl1("p1s", [128, 2 * 512], BF16)
            p2s = sl1("p2s", [128, 2 * 512], BF16)
            ost = sl1("ost", [128, NCH * SUPW], BF16)

            wq_col = lambda s: cm3(wq_d[:, s * SUPW:(s + 1) * SUPW])
            wk_col = lambda s: cm3(wk_d[:, s * SUPW:(s + 1) * SUPW])
            wv_col = lambda s: cm3(wv_d[:, s * SUPW:(s + 1) * SUPW])

            o_free = {}
            p_free = {}
            s_free = {}

            n_sup = 1 if stage <= 2 else NSUP
            for sup in range(n_sup):
                with ExitStack() as e1a:
                    pq = lambda n, sh, dt=F32: e1a.enter_context(
                        nc.psum_tensor(n + "_%d" % sup, sh, dt))
                    ps_q = [pq("ps_q%d" % i, [128, SUPW]) for i in range(2)]
                    ps_k = [pq("ps_k%d" % i, [128, SUPW]) for i in range(2)]
                    ps_v = pq("ps_v", [128, SUPW])
                    ps_tr = [pq("ps_tr%d" % i, [128, 128], BF16) for i in range(2)]
                    ps_f0 = pq("ps_f0", [128, 512])

                    pa = Phase(ctx, "p1a_%d" % sup)
                    pa.wait("gpsimd", "st", tok.get("projmm"))
                    t_wq = pa.on("gpsimd", lambda e, s=sup: e.dma_start(out=dst3(wst_q[:], NCH), in_=wq_col(s)), inc="d_w", k=16)
                    t_wk = pa.on("gpsimd", lambda e, s=sup: e.dma_start(out=dst3(wst_k[:], NCH), in_=wk_col(s)), inc="d_w", k=16)
                    t_wv = pa.on("gpsimd", lambda e, s=sup: e.dma_start(out=dst3(wst_v[:], NCH), in_=wv_col(s)), inc="d_w", k=16)
                    pa.wait("tensor", "d_w", t_wv)
                    pa.wait("tensor", "d_xt", tok["xt"])
                    if s_free:
                        pa.wait("tensor", "sa", max(s_free.values()))
                    if o_free:
                        pa.wait("tensor", "sv", max(o_free.values()))
                    if p_free:
                        pa.wait("tensor", "st", max(p_free.values()))
                    pa.wait("vector", "sv", tok["mcol"])
                    pa.wait("vector", "sa", tok["sincos"])
                    t_vones = pa.on("vector", lambda e: e.memset(
                        bass.AP(v_aug, 128, [[pstr(v_aug), 128], [132, NCH * HPS], [1, 1]]),
                        1.0), inc="sv")

                    wsrc = []
                    for qq in range(16):
                        if sup < 4:
                            ch, qtr = divmod(sup * 16 + qq, 4)
                            wsrc.append(w1t_d.rearrange("(a p) f -> a p f", p=128)[ch]
                                        [:, qtr * 2048:(qtr + 1) * 2048])
                        else:
                            me, qtr = divmod((sup - 4) * 16 + qq, 4)
                            wsrc.append(w2tm_d[me][:, qtr * 2048:(qtr + 1) * 2048])

                    f0_tok = {0: pa.on("sync", lambda e, s=wsrc[0]: e.dma_start(
                        out=wf0[:, 0:2048], in_=s), inc="d_wf00", k=16)}
                    f0_abs = {}
                    f0_mm = 0

                    rot_tok = {}
                    vev_tok = {}
                    tr_tok = {}
                    for ti in range(NCH):
                        par = ti % 2
                        if ti >= 2:
                            pa.wait("tensor", "sv", rot_tok[ti - 2])
                        if ti >= 1:
                            pa.wait("tensor", "sv", vev_tok[ti - 1])
                        mmv = None
                        for k in range(NCH):
                            st_ = (k == 0); sp_ = (k == NCH - 1)
                            pa.on("tensor", lambda e, k=k, ti=ti, par=par, st_=st_, sp_=sp_: e.matmul(
                                ps_q[par][:, :],
                                xt_bf[:, (k * NCH + ti) * 128:(k * NCH + ti) * 128 + 128],
                                wst_q[:, k * SUPW:(k + 1) * SUPW], start=st_, stop=sp_), inc="st")
                            pa.on("tensor", lambda e, k=k, ti=ti, par=par, st_=st_, sp_=sp_: e.matmul(
                                ps_k[par][:, :],
                                xt_bf[:, (k * NCH + ti) * 128:(k * NCH + ti) * 128 + 128],
                                wst_k[:, k * SUPW:(k + 1) * SUPW], start=st_, stop=sp_), inc="st")
                            mmv = pa.on("tensor", lambda e, k=k, ti=ti, st_=st_, sp_=sp_: e.matmul(
                                ps_v[:, :],
                                xt_bf[:, (k * NCH + ti) * 128:(k * NCH + ti) * 128 + 128],
                                wst_v[:, k * SUPW:(k + 1) * SUPW], start=st_, stop=sp_), inc="st")
                        tok["projmm"] = mmv

                        pa.wait("vector", "st", mmv)
                        tvv = None
                        for hl in range(HPS):
                            tvv = pa.on("vector", lambda e, ti=ti, hl=hl: e.tensor_copy(
                                v_aug[:, (ti * HPS + hl) * 132:(ti * HPS + hl) * 132 + 128],
                                ps_v[:, hl * 128:(hl + 1) * 128]), inc="sv")
                        vev_tok[ti] = tvv

                        cosv = bass.AP(cos_all, 32 * ti, [[pstr(cos_all), 128], [0, 4], [1, 32]])
                        sinv = bass.AP(sin_all, 32 * ti, [[pstr(sin_all), 128], [0, 4], [1, 32]])
                        cmp = lambda t_: t_[:].rearrange("p (a b) -> p a b", b=32)
                        tlast = None
                        for (ps_, tmb) in ((ps_q[par], qtm), (ps_k[par], ktm)):
                            x1v = bass.AP(ps_, 0, [[pstr(ps_), 128], [64, 4], [1, 32]])
                            x2v = bass.AP(ps_, 32, [[pstr(ps_), 128], [64, 4], [1, 32]])
                            o1v = bass.AP(tmb, par * SUPW, [[pstr(tmb), 128], [64, 4], [1, 32]])
                            o2v = bass.AP(tmb, par * SUPW + 32, [[pstr(tmb), 128], [64, 4], [1, 32]])
                            tv1 = pa.on("vector", lambda e, x1v=x1v, cosv=cosv: e.tensor_tensor(cmp(rsc[0]), x1v, cosv, ALU.mult), inc="sv")
                            tv2 = pa.on("vector", lambda e, x2v=x2v, sinv=sinv: e.tensor_tensor(cmp(rsc[1]), x2v, sinv, ALU.mult), inc="sv")
                            tv3 = pa.on("vector", lambda e, x1v=x1v, sinv=sinv: e.tensor_tensor(cmp(rsc[2]), x1v, sinv, ALU.mult), inc="sv")
                            tv4 = pa.on("vector", lambda e, x2v=x2v, cosv=cosv: e.tensor_tensor(cmp(rsc[3]), x2v, cosv, ALU.mult), inc="sv")
                            pa.wait("vector", "sv", tv2)
                            pa.on("vector", lambda e, o1v=o1v: e.tensor_tensor(o1v, cmp(rsc[0]), cmp(rsc[1]), ALU.subtract), inc="sv")
                            pa.wait("vector", "sv", tv4)
                            tlast = pa.on("vector", lambda e, o2v=o2v: e.tensor_tensor(o2v, cmp(rsc[2]), cmp(rsc[3]), ALU.add), inc="sv")
                        rot_tok[ti] = tlast

                        pa.wait("tensor", "sv", tlast)
                        tte = None
                        tile_i = 0
                        for (tmb, fmb) in ((qtm, q_fm), (ktm, k_fm)):
                            for lc in range(HPS):
                                b = tile_i % 2
                                if tr_tok.get((ti, b)) or tr_tok.get((ti - 1, b)):
                                    pa.wait("tensor", "sa", tr_tok.get((ti, b)) or tr_tok.get((ti - 1, b)))
                                tt = pa.on("tensor", lambda e, tmb=tmb, par=par, lc=lc, b=b: e.transpose(
                                    ps_tr[b][:, :],
                                    tmb[:, par * SUPW + lc * 128:par * SUPW + (lc + 1) * 128],
                                    ident[:]), inc="st")
                                pa.wait("scalar", "st", tt)
                                tte = pa.on("scalar", lambda e, fmb=fmb, lc=lc, ti=ti, b=b: e.copy(
                                    fmb[:, lc * T + ti * 128:lc * T + (ti + 1) * 128],
                                    ps_tr[b][:, :]), inc="sa")
                                tr_tok[(ti, b)] = tte
                                tile_i += 1
                        tr_tok[ti] = tte

                        if ti < len(wsrc):
                            slot = ti % 2
                            compensate = sup >= 4
                            if ti + 1 < len(wsrc):
                                pa.wait("sync", "sa", f0_abs.get(ti - 1))
                                f0_tok[ti + 1] = pa.on("sync", lambda e, s=wsrc[ti + 1], sl_=(ti + 1) % 2: e.dma_start(
                                    out=wf0[:, sl_ * 2048:(sl_ + 1) * 2048], in_=s),
                                    inc="d_wf0%d" % ((ti + 1) % 2), k=16)
                            pa.wait("scalar", "d_wf0%d" % (ti % 2), f0_tok[ti])
                            if f0_mm:
                                pa.wait("scalar", "st", f0_mm)
                            if not compensate:
                                f0_abs[ti] = pa.on("scalar", lambda e, sl_=slot: e.activation(
                                    wf0a[:], wf0[:, sl_ * 2048:(sl_ + 1) * 2048], AF.Abs), inc="sa")
                                pa.wait("tensor", "sa", f0_abs[ti])
                                if ti == 0:
                                    pa.wait("tensor", "sv", tok["f0acc"])
                                for w in range(4):
                                    f0_mm = pa.on("tensor", lambda e, w=w, ti=ti: e.matmul(
                                        ps_f0[0:2, :], ones2_s[:], wf0a[:, w * 512:(w + 1) * 512],
                                        start=(ti == 0 and w == 0), stop=(ti == 15 and w == 3)), inc="st")
                            else:
                                ta_f = pa.on("scalar", lambda e, sl_=slot: e.activation(
                                    wf0f[:], wf0[:, sl_ * 2048:(sl_ + 1) * 2048], AF.Abs), inc="sa")
                                pa.wait("scalar", "sa", ta_f)
                                f0_abs[ti] = pa.on("scalar", lambda e: e.copy(
                                    wf0a[:], wf0f[:]), inc="sa")
                                pa.wait("gpsimd", "sa", f0_abs[ti])
                                tg_lo = pa.on("gpsimd", lambda e: e.tensor_tensor(
                                    wf0lo[:], wf0f[:], wf0a[:], ALU.subtract), inc="sg")
                                pa.wait("tensor", "sa", f0_abs[ti])
                                pa.wait("tensor", "sg", tg_lo)
                                if ti == 0:
                                    pa.wait("tensor", "sv", tok["f0acc"])
                                for w in range(4):
                                    f0_mm = pa.on("tensor", lambda e, w=w, ti=ti: e.matmul(
                                        ps_f0[0:2, :], ones2_s[:], wf0a[:, w * 512:(w + 1) * 512],
                                        start=(ti == 0 and w == 0), stop=False), inc="st")
                                for w in range(4):
                                    f0_mm = pa.on("tensor", lambda e, w=w, ti=ti: e.matmul(
                                        ps_f0[0:2, :], ones2_s[:], wf0lo[:, w * 512:(w + 1) * 512],
                                        start=False, stop=(ti == 15 and w == 3)), inc="st")

                    pa.wait("vector", "st", f0_mm)
                    accoff = 0 if sup < 4 else 512
                    tok["f0acc"] = pa.on("vector", lambda e, a=accoff: e.tensor_tensor(
                        f0acc[:, a:a + 512], f0acc[:, a:a + 512], ps_f0[0:2, :], ALU.add), inc="sv")

                    t_ms_read = tok["f0acc"]
                    t_negc = None
                    for gi, fmb in enumerate((q_fm, k_fm)):
                        for lc in range(HPS):
                            if gi or lc:
                                pa.wait("scalar", "st", t_stab_mm)
                            tsq = pa.on("scalar", lambda e, fmb=fmb, lc=lc: e.activation(
                                sqb[:], fmb[:, lc * T:(lc + 1) * T], AF.Square), inc="sa")
                            for bi, base in enumerate((0, 64)):
                                col = gi * 8 + 2 * lc + bi
                                pa.wait("tensor", "sa", tsq)
                                for w in range(4):
                                    pa.wait("tensor", "sv", t_ms_read)
                                    t_stab_mm = pa.on("tensor", lambda e, base=base, w=w: e.matmul(
                                        ps_f0[0:2, :], ones2_s[base:base + 64, :],
                                        sqb[base:base + 64, w * 512:(w + 1) * 512],
                                        start=True, stop=True,
                                        tile_position=(base, 0)), inc="st")
                                    pa.wait("vector", "st", t_stab_mm)
                                    if w == 0:
                                        t_ms_read = pa.on("vector", lambda e, col=col: e.tensor_reduce(
                                            ncsc[0:2, col:col + 1], ps_f0[0:2, :], axis=AX,
                                            op=ALU.max), inc="sv")
                                    else:
                                        t_ms_read = pa.chain("vector", "sv", [
                                            lambda e: e.tensor_reduce(
                                                rsc[0][0:2, 0:1], ps_f0[0:2, :], axis=AX, op=ALU.max),
                                            lambda e, col=col: e.tensor_tensor(
                                                ncsc[0:2, col:col + 1], ncsc[0:2, col:col + 1],
                                                rsc[0][0:2, 0:1], ALU.max),
                                        ])
                    pa.wait("scalar", "sv", t_ms_read)
                    tsr = pa.on("scalar", lambda e: e.activation(ncsc[0:2, 0:16], ncsc[0:2, 0:16], AF.Sqrt), inc="sa")
                    pa.wait("vector", "sa", tsr)
                    t_nc2 = pa.chain("vector", "sv", [
                        lambda e: e.tensor_tensor(ncsc[0:2, 0:4], ncsc[0:2, 0:4], ncsc[0:2, 8:12], ALU.mult),
                        lambda e: e.tensor_scalar(ncsc[0:2, 0:4], ncsc[0:2, 0:4], -0.125, None, ALU.mult),
                    ])
                    pa.wait("tensor", "sv", t_nc2)
                    if s_free:
                        pa.wait("vector", "sa", max(s_free.values()))
                    for hh in range(4):
                        t_bc = pa.on("tensor", lambda e, hh=hh: e.matmul(
                            ps_f0[:, 0:1], halfones_s[:, :], ncsc[0:2, hh:hh + 1],
                            start=True, stop=True), inc="st")
                        pa.wait("vector", "st", t_bc)
                        t_negc = pa.on("vector", lambda e, hh=hh: e.tensor_copy(
                            negc[:, hh:hh + 1], ps_f0[:, 0:1]), inc="sv")
                        if hh < 3:
                            pa.wait("tensor", "sv", t_negc)
                    tok["negc"] = t_negc
                    tok["trlast"] = tr_tok[NCH - 1]
                    tok["vaug"] = max(vev_tok[NCH - 1], t_vones)
                    pa.emit()

                if stage == 1:
                    pd = Phase(ctx, "dbg1")
                    if "qkv" in dbg:
                        qdb = dout("q_fm", [128, HPS * T], BF16)
                        kdb = dout("k_fm", [128, HPS * T], BF16)
                        vdb = dout("v_aug", [128, NCH * HPS * 132], BF16)
                        cdb2 = dout("negc", [128, 4])
                        pd.wait("sync", "sa", ctx.sems["sa"][1])
                        pd.wait("sync", "sv", ctx.sems["sv"][1])
                        pd.on("sync", lambda e: e.dma_start(out=qdb, in_=q_fm[:]), inc="d_out", k=16)
                        pd.on("sync", lambda e: e.dma_start(out=kdb, in_=k_fm[:]), inc="d_out", k=16)
                        pd.on("sync", lambda e: e.dma_start(out=vdb, in_=v_aug[:]), inc="d_out", k=16)
                        pd.on("sync", lambda e: e.dma_start(out=cdb2, in_=negc[:]), inc="d_out", k=16)
                    pd.emit()
                    _finish(ctx, out_d)
                    return nc, dbg_d

                with ExitStack() as e1b:
                    pq = lambda n, sh: e1b.enter_context(
                        nc.psum_tensor(n + "_%d" % sup, sh, F32))
                    ps_s1 = [pq("ps_s1%d" % i, [128, 512]) for i in range(2)]
                    ps_s2 = [pq("ps_s2%d" % i, [128, 512]) for i in range(2)]
                    ps_o = [pq("ps_o%d" % i, [128, 264]) for i in range(4)]

                    pb = Phase(ctx, "p1b_%d" % sup)
                    pb.wait("tensor", "sa", tok["trlast"])
                    pb.wait("tensor", "sv", tok["vaug"])
                    pb.wait("tensor", "sv", tok["negc"])
                    pb.wait("scalar", "sv", tok["negc"])
                    pb.wait("vector", "sv", tok["neglam"])
                    pb.wait("vector", "d_osp", tok.get("osp"))

                    for hl in range(HPS):
                        for W in range(4):
                            jmax = 4 * W + 3
                            for j in range(jmax + 1):
                                jp = j % 2
                                nst = max(512 * W, 128 * j)
                                wid = 512 * W + 512 - nst
                                off = nst - 512 * W
                                pb.wait("tensor", "sa", s_free.get(jp))
                                t1 = pb.on("tensor", lambda e, hl=hl, j=j, jp=jp, nst=nst, wid=wid, off=off: e.matmul(
                                    ps_s1[jp][:, off:off + wid],
                                    k_fm[0:64, hl * T + j * 128:hl * T + (j + 1) * 128],
                                    q_fm[0:64, hl * T + nst:hl * T + nst + wid],
                                    start=True, stop=True, tile_position=(0, 0)), inc="st")
                                t2 = pb.on("tensor", lambda e, hl=hl, j=j, jp=jp, nst=nst, wid=wid, off=off: e.matmul(
                                    ps_s2[jp][:, off:off + wid],
                                    k_fm[64:128, hl * T + j * 128:hl * T + (j + 1) * 128],
                                    q_fm[64:128, hl * T + nst:hl * T + nst + wid],
                                    start=True, stop=True, tile_position=(64, 0)), inc="st")
                                pb.wait("scalar", "st", t2)
                                pb.wait("scalar", "st", p_free.get(jp))
                                e1t = pb.on("scalar", lambda e, hl=hl, jp=jp, off=off, wid=wid: e.activation(
                                    p1s[:, jp * 512 + off:jp * 512 + off + wid],
                                    ps_s1[jp][:, off:off + wid], AF.Exp,
                                    bias=negc[:, 2 * hl:2 * hl + 1], scale=0.125), inc="sa")
                                e2t = pb.on("scalar", lambda e, hl=hl, jp=jp, off=off, wid=wid: e.activation(
                                    p2s[:, jp * 512 + off:jp * 512 + off + wid],
                                    ps_s2[jp][:, off:off + wid], AF.Exp,
                                    bias=negc[:, 2 * hl + 1:2 * hl + 2], scale=0.125), inc="sa")
                                s_free[jp] = e2t
                                mask_tok = None
                                if j >= 4 * W:
                                    pb.wait("vector", "sa", e2t)
                                    pb.on("vector", lambda e, jp=jp, off=off: e.tensor_tensor(
                                        p1s[:, jp * 512 + off:jp * 512 + off + 128],
                                        p1s[:, jp * 512 + off:jp * 512 + off + 128],
                                        triu_m[:], ALU.mult), inc="sv")
                                    mask_tok = pb.on("vector", lambda e, jp=jp, off=off: e.tensor_tensor(
                                        p2s[:, jp * 512 + off:jp * 512 + off + 128],
                                        p2s[:, jp * 512 + off:jp * 512 + off + 128],
                                        triu_m[:], ALU.mult), inc="sv")
                                if mask_tok is not None:
                                    pb.wait("tensor", "sv", mask_tok)
                                else:
                                    pb.wait("tensor", "sa", e2t)
                                avt = None
                                for i in range(max(j, 4 * W), 4 * W + 4):
                                    il = i - 4 * W
                                    if j == 0:
                                        pb.wait("tensor", "sv", o_free.get(il))
                                    avt = pb.on("tensor", lambda e, jp=jp, il=il, i=i, j=j, W=W, hl=hl: e.matmul(
                                        ps_o[il][:, 0:129],
                                        p1s[:, jp * 512 + (i - 4 * W) * 128:jp * 512 + (i - 4 * W) * 128 + 128],
                                        v_aug[:, (j * HPS + hl) * 132:(j * HPS + hl) * 132 + 129],
                                        start=(j == 0), stop=(j == i)), inc="st")
                                    avt = pb.on("tensor", lambda e, jp=jp, il=il, i=i, j=j, W=W, hl=hl: e.matmul(
                                        ps_o[il][:, 132:261],
                                        p2s[:, jp * 512 + (i - 4 * W) * 128:jp * 512 + (i - 4 * W) * 128 + 128],
                                        v_aug[:, (j * HPS + hl) * 132:(j * HPS + hl) * 132 + 129],
                                        start=False, stop=(j == i)), inc="st")
                                    if i == j:
                                        pb.wait("vector", "st", avt)
                                        tvc = pb.chain("vector", "sv", [
                                            lambda e, il=il: e.reciprocal(
                                                rsc[0][:, 0:2],
                                                bass.AP(ps_o[il], 128, [[pstr(ps_o[il]), 128], [132, 2], [1, 1]])),
                                            lambda e: e.tensor_tensor(
                                                rsc[0][:, 2:3], rsc[0][:, 1:2], neglam[:], ALU.mult),
                                            lambda e, il=il: e.tensor_scalar(
                                                rsc[1][:, :], ps_o[il][:, 0:128], rsc[0][:, 0:1], None, ALU.mult),
                                        ])
                                        pb.wait("vector", "sv", tvc)
                                        o_free[il] = pb.on("vector", lambda e, il=il, i=i, hl=hl: e.scalar_tensor_tensor(
                                            ost[:, (i * HPS + hl) * 128:(i * HPS + hl) * 128 + 128],
                                            ps_o[il][:, 132:260], rsc[0][:, 2:3], rsc[1][:, :],
                                            ALU.mult, ALU.add), inc="sv")
                                p_free[jp] = avt
                    last_ev = max(o_free.values())
                    pb.wait("sync", "sv", last_ev)
                    osp_dst = cm3(o_sp[:, sup * SUPW:(sup + 1) * SUPW])
                    t_osp = pb.on("sync", lambda e: e.dma_start(
                        out=osp_dst, in_=dst3(ost[:], NCH)), inc="d_osp", k=16)
                    tok["osp"] = t_osp
                    pb.emit()

                if stage == 2:
                    pd = Phase(ctx, "dbg2")
                    if "attn" in dbg:
                        odb = dout("ost", [128, NCH * SUPW], BF16)
                        pd.wait("sync", "sv", ctx.sems["sv"][1])
                        pd.on("sync", lambda e: e.dma_start(out=odb, in_=ost[:]), inc="d_out", k=16)
                    pd.emit()
                    _finish(ctx, out_d)
                    return nc, dbg_d

        if stage == 3:
            if "osp" in dbg:
                pd = Phase(ctx, "dbg3")
                odb = dout("osp", [T, E], BF16)
                pd.wait("sync", "d_osp", ctx.sems["d_osp"][1])
                pd.on("sync", lambda e: e.dma_start(out=odb, in_=o_sp), inc="d_out", k=16)
                pd.emit()
            _finish(ctx, out_d)
            return nc, dbg_d

        with ExitStack() as eQ:
            xq_fm = eQ.enter_context(nc.sbuf_tensor("xq_fm", [128, NCH * E], BF16))

            with ExitStack() as e2:
                sl2 = lambda n, sh, dt=F32: e2.enter_context(nc.sbuf_tensor(n, sh, dt))
                wo_sb = sl2("wo_sb", [128, NCH * E], BF16)
                oring = sl2("oring", [128, 4 * T], BF16)
                x2tm = sl2("x2tm", [128, E], BF16)
                x2fm = sl2("x2fm", [128, E], BF16)
                xqtm = sl2("xqtm", [128, E], BF16)
                xch = sl2("xch", [128, 2 * E])
                y1 = sl2("y1", [128, E])
                sq2 = sl2("sq2", [128, E])
                st2 = sl2("st2", [128, 16])
                ms2 = sl2("ms2", [128, 16])

                p2q = lambda n, sh, dt=F32: e2.enter_context(nc.psum_tensor(n, sh, dt))
                ps_go = [p2q("ps_go%d" % i, [128, 512]) for i in range(2)]
                ps_t2 = [p2q("ps_t2%d" % i, [128, 128], BF16) for i in range(2)]
                ps_y = [p2q("ps_y%d" % i, [128, 512]) for i in range(4)]

                p2 = Phase(ctx, "p2")
                p2.barrier()
                p2.wait("vector", "sv", tok["f0acc"])
                p2.wait("vector", "sv", tok["r127"])
                tmean = p2.chain("vector", "sv", [
                    lambda e: e.tensor_reduce(st2[0:2, 0:1], f0acc[:, 0:512], axis=AX, op=ALU.add),
                    lambda e: e.tensor_reduce(st2[0:2, 1:2], f0acc[:, 512:1024], axis=AX, op=ALU.add),
                    lambda e: e.tensor_scalar(st2[0:2, 4:6], st2[0:2, 0:2],
                                              1.0 / 512.0, None, ALU.mult),
                    lambda e: e.tensor_scalar(sq2[0:2, 0:512], f0acc[:, 0:512],
                                              st2[0:2, 4:5], None, ALU.subtract),
                    lambda e: e.tensor_scalar(sq2[0:2, 512:1024], f0acc[:, 512:1024],
                                              st2[0:2, 5:6], None, ALU.subtract),
                    lambda e: e.tensor_reduce(st2[0:2, 6:7], sq2[0:2, 0:512], axis=AX, op=ALU.add),
                    lambda e: e.tensor_reduce(st2[0:2, 7:8], sq2[0:2, 512:1024], axis=AX, op=ALU.add),
                    lambda e: e.tensor_tensor(st2[0:2, 0:2], st2[0:2, 0:2], st2[0:2, 6:8], ALU.add),
                    lambda e: e.tensor_scalar(st2[0:2, 0:2], st2[0:2, 0:2],
                                              1.0 / 16777216.0, None, ALU.mult),
                ])
                p2.wait("tensor", "sv", tmean)
                tb1 = p2.on("tensor", lambda e: e.matmul(
                    ps_y[0][:, 0:1], halfones_s[:, :], st2[0:2, 0:1], start=True, stop=True), inc="st")
                tb2 = p2.on("tensor", lambda e: e.matmul(
                    ps_y[1][:, 0:1], halfones_s[:, :], st2[0:2, 1:2], start=True, stop=True), inc="st")
                p2.wait("vector", "st", tb2)
                tmean2 = p2.chain("vector", "sv", [
                    lambda e: e.tensor_copy(ms2[:, 0:1], ps_y[0][:, 0:1]),
                    lambda e: e.tensor_copy(ms2[:, 1:2], ps_y[1][:, 0:1]),
                    lambda e: e.tensor_tensor(cstc[:, 3:4], ms2[:, 0:1], cstc[:, 0:1], ALU.mult),
                    lambda e: e.tensor_tensor(cstc[:, 4:5], ms2[:, 1:2], cstc[:, 0:1], ALU.mult),
                    lambda e: e.tensor_scalar(ms2[:, 0:2], ms2[:, 0:2], 1e-5, None, ALU.max),
                    lambda e: e.reciprocal(cstc[:, 1:3], ms2[:, 0:2]),
                ])
                tok["means"] = tmean2

                p2.wait("sync", "d_wo", tok["wo_cast"])
                wo_src = cm3(wo_bf_d)
                t_wo = p2.on("sync", lambda e: e.dma_start(
                    out=dst3(wo_sb[:], NCH), in_=wo_src), inc="d_w", k=16)

                p2.wait("sync", "d_osp", ctx.sems["d_osp"][1])
                orow = lambda c: o_sp[(c % NCH) * 128:(c % NCH) * 128 + 128, :]
                t_oin = {}
                for c in (-1, 0, 1):
                    t_oin[c % NCH] = p2.on("sync", lambda e, c=c: e.dma_start(
                        out=oring[:, (c % 4) * T:(c % 4) * T + T], in_=orow(c)),
                        inc="d_oin%d" % (c % 4), k=16)
                t_xc = {0: p2.on("sync", lambda e: e.dma_start(
                    out=xch[:, 0:E], in_=x_d[0:128, :]), inc="d_xc0", k=16)}

                p2.wait("tensor", "sv", tok["gband"])
                p2.wait("tensor", "d_w", t_wo)
                p2.wait("vector", "sv", tok["rmsw"])
                p2.wait("vector", "sv", tok["means"])

                go_free = {}
                y_free = {}
                t2_free = {}
                xq_tr = {}
                gmm_last = {}
                tres_hist = {}
                tsq2_hist = {}
                for mi in range(NCH):
                    if mi + 2 <= NCH:
                        cnext = (mi + 2) % NCH
                        p2.wait("sync", "st", gmm_last.get(mi - 1))
                        t_oin[cnext] = p2.on("sync", lambda e, c=cnext: e.dma_start(
                            out=oring[:, (c % 4) * T:(c % 4) * T + T], in_=orow(c)),
                            inc="d_oin%d" % (cnext % 4), k=16)
                    if mi + 1 < NCH:
                        p2.wait("sync", "sv", tres_hist.get(mi - 1))
                        t_xc[mi + 1] = p2.on("sync", lambda e, m=mi + 1: e.dma_start(
                            out=xch[:, (m % 2) * E:(m % 2) * E + E],
                            in_=x_d[m * 128:(m + 1) * 128, :]),
                            inc="d_xc%d" % ((mi + 1) % 2), k=16)
                    for c in ((mi - 1) % NCH, mi, (mi + 1) % NCH):
                        p2.wait("tensor", "d_oin%d" % (c % 4), t_oin[c])

                    rstd_tok = None
                    for wp in range(2):
                        gof = go_free.get((mi, 0)) if wp == 1 else go_free.get((mi - 1, 1))
                        if gof:
                            p2.wait("tensor", "sv", gof)
                        gmt = None
                        for wloc in range(2):
                            w = wp * 2 + wloc
                            for dd in range(3):
                                jb = (mi + dd - 1) % NCH
                                gmt = p2.on("tensor", lambda e, w=w, wloc=wloc, dd=dd, jb=jb: e.matmul(
                                    ps_go[wloc][:, :],
                                    gband[:, dd * 128:(dd + 1) * 128],
                                    oring[:, (jb % 4) * T + w * 512:(jb % 4) * T + w * 512 + 512],
                                    start=(dd == 0), stop=(dd == 2)), inc="st")
                        gmm_last[mi] = gmt
                        p2.wait("scalar", "st", gmt)
                        tsq = {}
                        for wloc in range(2):
                            w = wp * 2 + wloc
                            tsq[wloc] = p2.on("scalar", lambda e, wloc=wloc, w=w: e.activation(
                                sq2[:, w * 512:(w + 1) * 512], ps_go[wloc][:, :], AF.Square), inc="sa")
                        tred = None
                        for wloc in range(2):
                            w = wp * 2 + wloc
                            p2.wait("vector", "sa", tsq[wloc])
                            for g in range(4):
                                tred = p2.on("vector", lambda e, w=w, g=g: e.tensor_reduce(
                                    ms2[:, w * 4 + g:w * 4 + g + 1],
                                    sq2[:, w * 512 + g * 128:w * 512 + (g + 1) * 128],
                                    axis=AX, op=ALU.add), inc="sv")
                        p2.wait("scalar", "sv", tred)
                        ta1 = p2.on("scalar", lambda e, wp=wp: e.activation(
                            st2[:, wp * 8:wp * 8 + 8], ms2[:, wp * 8:wp * 8 + 8], AF.Ln,
                            scale=0.0078125, bias=cstc[:, 5:6]), inc="sa")
                        p2.wait("scalar", "sa", ta1)
                        rstd_tok = p2.on("scalar", lambda e, wp=wp: e.activation(
                            st2[:, wp * 8:wp * 8 + 8], st2[:, wp * 8:wp * 8 + 8], AF.Exp,
                            scale=-0.5), inc="sa")
                        p2.wait("vector", "sa", rstd_tok)
                        tev = None
                        for wloc in range(2):
                            w = wp * 2 + wloc
                            for g in range(4):
                                tev = p2.on("vector", lambda e, w=w, g=g, wloc=wloc: e.scalar_tensor_tensor(
                                    x2tm[:, w * 512 + g * 128:w * 512 + (g + 1) * 128],
                                    ps_go[wloc][:, g * 128:(g + 1) * 128],
                                    st2[:, w * 4 + g:w * 4 + g + 1],
                                    rmsw_eff[:, g * 128:(g + 1) * 128],
                                    ALU.mult, ALU.mult), inc="sv")
                        go_free[(mi, wp)] = tev

                    t_tr = None
                    for k in range(NCH):
                        b = k % 2
                        p2.wait("tensor", "sv", go_free[(mi, k // 8)] if k // 8 in (0, 1) else None)
                        tf = t2_free.get(b)
                        if tf:
                            p2.wait("tensor", "sa", tf)
                        tt = p2.on("tensor", lambda e, k=k, b=b: e.transpose(
                            ps_t2[b][:, :], x2tm[:, k * 128:(k + 1) * 128], ident[:]), inc="st")
                        p2.wait("scalar", "st", tt)
                        t2_free[b] = p2.on("scalar", lambda e, k=k, b=b: e.copy(
                            x2fm[:, k * 128:(k + 1) * 128], ps_t2[b][:, :]), inc="sa")
                    t_tr = t2_free[(NCH - 1) % 2]

                    p2.wait("tensor", "sa", t_tr)
                    ymt = None
                    for k in range(NCH):
                        if k == 0:
                            for ew in range(4):
                                yf = y_free.get(ew)
                                if yf:
                                    p2.wait("tensor", "sv", yf)
                        for ew in range(4):
                            ymt = p2.on("tensor", lambda e, k=k, ew=ew: e.matmul(
                                ps_y[ew][:, :], x2fm[:, k * 128:(k + 1) * 128],
                                wo_sb[:, k * E + ew * 512:k * E + ew * 512 + 512],
                                start=(k == 0), stop=(k == NCH - 1)), inc="st")

                    p2.wait("vector", "st", ymt)
                    p2.wait("vector", "d_xc%d" % (mi % 2), t_xc[mi])
                    p2.wait("vector", "sa", tsq2_hist.get(mi - 1) if mi else None)
                    tres = None
                    for ew in range(4):
                        tres = p2.on("vector", lambda e, ew=ew, mi=mi: e.tensor_tensor(
                            y1[:, ew * 512:(ew + 1) * 512], ps_y[ew][:, :],
                            xch[:, (mi % 2) * E + ew * 512:(mi % 2) * E + ew * 512 + 512],
                            ALU.add), inc="sv")
                        y_free[ew] = tres
                    tres_hist[mi] = tres
                    p2.wait("vector", "sv", tres)
                    tmu = p2.on("vector", lambda e: e.tensor_reduce(
                        st2[:, 0:1], y1[:], axis=AX, op=ALU.add), inc="sv")
                    p2.wait("scalar", "sv", tres)
                    tsq2 = p2.on("scalar", lambda e: e.activation(sq2[:], y1[:], AF.Square), inc="sa")
                    tsq2_hist[mi] = tsq2
                    p2.wait("vector", "sa", tsq2)
                    tss = p2.on("vector", lambda e: e.tensor_reduce(
                        st2[:, 1:2], sq2[:], axis=AX, op=ALU.add), inc="sv")
                    p2.wait("vector", "sv", max(tmu, tss))
                    tvar = p2.chain("vector", "sv", [
                        lambda e: e.tensor_scalar(st2[:, 0:2], st2[:, 0:2],
                                                  0.00048828125, None, ALU.mult),
                        lambda e: e.tensor_tensor(st2[:, 2:3], st2[:, 0:1], st2[:, 0:1], ALU.mult),
                        lambda e: e.tensor_tensor(st2[:, 1:2], st2[:, 1:2], st2[:, 2:3], ALU.subtract),
                        lambda e: e.tensor_scalar(st2[:, 1:2], st2[:, 1:2], 1e-5, None, ALU.add),
                    ])
                    p2.wait("scalar", "sv", tvar)
                    ta1 = p2.on("scalar", lambda e: e.activation(st2[:, 1:2], st2[:, 1:2], AF.Ln), inc="sa")
                    p2.wait("scalar", "sa", ta1)
                    ta2 = p2.on("scalar", lambda e: e.activation(st2[:, 1:2], st2[:, 1:2], AF.Exp, scale=-0.5), inc="sa")
                    p2.wait("vector", "sa", ta2)
                    tx3 = p2.on("vector", lambda e: e.tensor_scalar(
                        sq2[:], y1[:], st2[:, 0:1], st2[:, 1:2], ALU.subtract, ALU.mult), inc="sv")
                    p2.wait("sync", "sv", tx3)
                    t_x3 = p2.on("sync", lambda e, mi=mi: e.dma_start(
                        out=x3_sp[mi * 128:(mi + 1) * 128, :], in_=sq2[:]), inc="d_x3", k=16)
                    p2.wait("vector", "sv", tx3)
                    tq = p2.chain("vector", "sv", [
                        lambda e, mi=mi: e.tensor_reduce(
                            amax_all[:, mi:mi + 1], sq2[:], axis=AX, op=ALU.max,
                            apply_absolute_value=True),
                        lambda e, mi=mi: e.tensor_scalar(
                            st2[:, 3:4], amax_all[:, mi:mi + 1], 1e-5, None, ALU.max),
                        lambda e: e.reciprocal(st2[:, 3:4], st2[:, 3:4]),
                        lambda e: e.tensor_scalar(st2[:, 3:4], st2[:, 3:4], 127.0, None, ALU.mult),
                        lambda e: e.tensor_scalar(y1[:], sq2[:], st2[:, 3:4], None, ALU.mult),
                        lambda e: e.tensor_scalar(y1[:], y1[:], mcol[:], None, ALU.add),
                        lambda e: e.tensor_scalar(xqtm[:], y1[:], mcol[:], None, ALU.subtract),
                    ])
                    for k in range(NCH):
                        b = k % 2
                        if k == 0:
                            p2.wait("tensor", "sv", tq)
                        tf = t2_free.get(b)
                        if tf:
                            p2.wait("tensor", "sa", tf)
                        tt = p2.on("tensor", lambda e, k=k, b=b: e.transpose(
                            ps_t2[b][:, :], xqtm[:, k * 128:(k + 1) * 128], ident[:]), inc="st")
                        p2.wait("scalar", "st", tt)
                        t2_free[b] = p2.on("scalar", lambda e, k=k, b=b, mi=mi: e.copy(
                            xq_fm[:, (k * NCH + mi) * 128:(k * NCH + mi) * 128 + 128],
                            ps_t2[b][:, :]), inc="sa")
                    xq_tr[mi] = t2_free[(NCH - 1) % 2]
                    p2.wait("scalar", "d_x3", t_x3)
                    p2.wait("scalar", "sv", tq)
                tok["xq"] = max(xq_tr.values())
                tok["x3done"] = ctx.sems["d_x3"][1]
                p2.emit()

                if stage == 4 and "p2" in dbg:
                    pd = Phase(ctx, "dbgp2")
                    xdb = dout("xq_fm", [128, NCH * E], BF16)
                    adb = dout("amax", [128, NCH])
                    pd.wait("sync", "sa", ctx.sems["sa"][1])
                    pd.wait("sync", "sv", ctx.sems["sv"][1])
                    pd.on("sync", lambda e: e.dma_start(out=xdb, in_=xq_fm[:]), inc="d_out", k=16)
                    pd.on("sync", lambda e: e.dma_start(out=adb, in_=amax_all[:]), inc="d_out", k=16)
                    pd.emit()
            if stage == 4:
                _finish(ctx, out_d)
                return nc, dbg_d

            with ExitStack() as e3:
                sl3 = lambda n, sh, dt=F32: e3.enter_context(nc.sbuf_tensor(n, sh, dt))
                w1a = sl3("w1a", [128, NCH * 512])
                w1q = sl3("w1q", [128, NCH * 512], BF16)
                hT = [sl3("hT%d" % i, [128, T]) for i in range(2)]
                mxacc = sl3("mxacc", [128, T])
                mnacc = sl3("mnacc", [128, T])
                rs1rep = sl3("rs1rep", [128, T])
                w2st = sl3("w2st", [128, DFF // 2])
                w2qb = sl3("w2qb", [128, DFF // 2], BF16)
                rcol = sl3("rcol", [128, 128])
                rrow16 = sl3("rrow16", [16, 128])

                ps_f1 = [e3.enter_context(nc.psum_tensor("ps_f1%d" % i, [128, 512], F32))
                         for i in range(8)]

                p3 = Phase(ctx, "p3")
                p3.barrier()
                t_rc = p3.chain("vector", "sv", [
                    lambda e: e.memset(rcol[:], 0.0),
                    lambda e: e.tensor_scalar(
                        rcol[:, 0:NCH], amax_all[:], cstc[:, 3:4], None, ALU.mult),
                ])
                p3.wait("tensor", "sv", t_rc)
                t_rt = p3.on("tensor", lambda e: e.transpose(
                    ps_f1[0][:, 0:128], rcol[:], identf_s[:]), inc="st")
                p3.wait("vector", "st", t_rt)
                t_rv = p3.on("vector", lambda e: e.tensor_copy(
                    rrow16[0:16, :], ps_f1[0][0:16, 0:128]), inc="sv")
                ev_tok0 = t_rv
                p3.wait("sync", "sv", t_rv)
                t_rr = p3.on("sync", lambda e: e.dma_start(
                    out=bass.AP(row_sp_t, 0, [[128, 16], [1, 128]]),
                    in_=rrow16[0:16, :]), inc="d_row", k=16)
                p3.wait("sync", "d_row", t_rr)
                tok["rs1rep"] = p3.on("sync", lambda e: e.dma_start(
                    out=rs1rep[:], in_=bass.AP(row_sp_t, 0, [[0, 128], [1, T]])),
                    inc="d_row", k=16)
                t_mm0 = p3.on("vector", lambda e: e.memset(mxacc[:], 0.0), inc="sv")
                t_mm0 = p3.on("vector", lambda e: e.memset(mnacc[:], 0.0), inc="sv")

                ev_tok = {0: ev_tok0}
                po_tok = {}
                sp_tok = {}
                mm_last = None
                w2_in = {}
                w2_q = {}
                w2_out = {}
                w2_in[0] = p3.on("sync", lambda e: e.dma_start(
                    out=w2st[:], in_=w2tm_d[0][:, 0:4096]), inc="d_w2", k=16)

                def w1_mult(t_dma):
                    p3.wait("scalar", "d_w1", t_dma)
                    return p3.on("scalar", lambda e: e.activation(
                        w1a[:], w1a[:], AF.Copy, scale=cstc[:, 1:2]), inc="sa")

                def w1_clip(t_mult):
                    p3.wait("vector", "sa", t_mult)
                    return p3.on("vector", lambda e: e.tensor_scalar(
                        w1a[:], w1a[:], 1.0, -1.0, ALU.min, ALU.max), inc="sv")

                def w1_round():
                    return p3.on("vector", lambda e: e.tensor_scalar(
                        w1q[:], w1a[:], mcol[:], mcol[:], ALU.add, ALU.subtract),
                        inc="sv")

                def w1_dma(fw):
                    return p3.on("sync", lambda e, fw=fw: e.dma_start(
                        out=dst3(w1a[:], NCH),
                        in_=cm3(w1t_d[:, fw * 512:(fw + 1) * 512])), inc="d_w1", k=16)

                pend = None

                def flush_pend(e_unused=None):
                    par, tt = pend
                    p3.on("vector", lambda e, par=par, tt=tt: e.tensor_tensor(
                        mxacc[:, tt * 512:(tt + 1) * 512],
                        mxacc[:, tt * 512:(tt + 1) * 512],
                        hT[par][:, tt * 512:(tt + 1) * 512], ALU.max), inc="sv")
                    return p3.on("vector", lambda e, par=par, tt=tt: e.tensor_tensor(
                        mnacc[:, tt * 512:(tt + 1) * 512],
                        mnacc[:, tt * 512:(tt + 1) * 512],
                        hT[par][:, tt * 512:(tt + 1) * 512], ALU.min), inc="sv")

                t_w1 = w1_dma(0)
                w1_clip(w1_mult(t_w1))
                q3_tok = w1_round()
                for fw in range(NCH):
                    p3.wait("tensor", "sv", q3_tok)
                    if fw == 0:
                        p3.wait("tensor", "sa", tok["xq"])
                    if fw + 1 < NCH:
                        p3.wait("sync", "sv", q3_tok)
                        t_w1n = w1_dma(fw + 1)
                    tile = 0
                    mm = None
                    for fc in range(4):
                        par = fc % 2
                        first_ev = True
                        tsi = None
                        for tt in range(4):
                            b = (fc * 4 + tt) % 8
                            p3.wait("tensor", "sv", ev_tok.get(b))
                            for k in range(NCH):
                                mm = p3.on("tensor", lambda e, fc=fc, tt=tt, k=k, b=b: e.matmul(
                                    ps_f1[b][:, :],
                                    w1q[:, k * 512 + fc * 128:k * 512 + fc * 128 + 128],
                                    xq_fm[:, k * T + tt * 512:k * T + tt * 512 + 512],
                                    start=(k == 0), stop=(k == NCH - 1)), inc="st")
                            p3.wait("vector", "st", mm)
                            if fw == 0 and fc == 0 and tt == 0:
                                p3.wait("vector", "d_row", tok["rs1rep"])
                            if first_ev:
                                p3.wait("vector", "d_hsp%d" % par, sp_tok.get(par))
                                first_ev = False
                            tev = p3.on("vector", lambda e, par=par, tt=tt, b=b: e.tensor_tensor(
                                hT[par][:, tt * 512:(tt + 1) * 512], ps_f1[b][:, :],
                                rs1rep[:, tt * 512:(tt + 1) * 512], ALU.mult), inc="sv")
                            ev_tok[b] = tev
                            if pend is not None:
                                p3.wait("vector", "sa", pend_sa)
                                flush_pend()
                            p3.wait("scalar", "sv", tev)
                            tsi = p3.on("scalar", lambda e, par=par, tt=tt: e.activation(
                                hT[par][:, tt * 512:(tt + 1) * 512],
                                hT[par][:, tt * 512:(tt + 1) * 512], AF.Silu), inc="sa")
                            pend = (par, tt)
                            pend_sa = tsi
                            tile += 1
                            if tile == 3 and fw + 1 < NCH:
                                t_mul_next = w1_mult(t_w1n)
                            if tile == 6 and fw + 1 < NCH:
                                w1_clip(t_mul_next)
                        p3.wait("gpsimd", "sa", tsi)
                        sp_tok[par] = p3.on("gpsimd", lambda e, fw=fw, fc=fc, par=par: e.dma_start(
                            out=h_spT[(fw * 4 + fc) * 128:(fw * 4 + fc) * 128 + 128, :],
                            in_=hT[par][:]), inc="d_hsp%d" % par, k=16)
                    mm_last = mm
                    if fw + 1 < NCH:
                        p3.wait("vector", "st", mm_last)
                        q3_tok = w1_round()
                    for hh in range(2):
                        idx = fw * 2 + hh
                        p3.wait("scalar", "d_w2", w2_in[idx])
                        t_m2 = p3.on("scalar", lambda e: e.activation(
                            w2st[:], w2st[:], AF.Copy, scale=cstc[:, 2:3]), inc="sa")
                        p3.wait("vector", "sa", t_m2)
                        p3.on("vector", lambda e: e.tensor_scalar(
                            w2st[:], w2st[:], 1.0, -1.0, ALU.min, ALU.max), inc="sv")
                        p3.wait("vector", "d_w2q", w2_out.get(idx - 1))
                        w2_q[idx] = p3.on("vector", lambda e: e.tensor_scalar(
                            w2qb[:], w2st[:], mcol[:], mcol[:], ALU.add, ALU.subtract),
                            inc="sv")
                        p3.wait("gpsimd", "sv", w2_q[idx])
                        w2_out[idx] = p3.on("gpsimd", lambda e, idx=idx: e.dma_start(
                            out=w2q_sp[idx // 2][:, (idx % 2) * 4096:(idx % 2) * 4096 + 4096],
                            in_=w2qb[:]), inc="d_w2q", k=16)
                        if idx + 1 < 2 * NCH:
                            p3.wait("sync", "sv", w2_q[idx])
                            w2_in[idx + 1] = p3.on("sync", lambda e, idx=idx: e.dma_start(
                                out=w2st[:],
                                in_=w2tm_d[(idx + 1) // 2]
                                [:, ((idx + 1) % 2) * 4096:((idx + 1) % 2) * 4096 + 4096]),
                                inc="d_w2", k=16)
                p3.wait("vector", "sa", pend_sa)
                flush_pend()
                tcmb = p3.chain("vector", "sv", [
                    lambda e: e.tensor_scalar(mnacc[:], mnacc[:], -1.0, None, ALU.mult),
                    lambda e: e.tensor_tensor(mxacc[:], mxacc[:], mnacc[:], ALU.max),
                ])
                p3.wait("tensor", "sv", tcmb)
                tr3 = {}
                for mi in range(NCH):
                    b = mi % 2
                    p3.wait("tensor", "sv", tr3.get(b, ev_tok.get(b)))
                    tt_ = p3.on("tensor", lambda e, mi=mi, b=b: e.transpose(
                        ps_f1[b][:, 0:128], mxacc[:, mi * 128:(mi + 1) * 128],
                        identf_s[:]), inc="st")
                    p3.wait("vector", "st", tt_)
                    tr3[b] = p3.on("vector", lambda e, mi=mi, b=b: e.tensor_reduce(
                        hamax[:, mi:mi + 1], ps_f1[b][:, 0:128], axis=AX, op=ALU.max),
                        inc="sv")
                tok["hamax"] = max(tr3.values())
                tok["w2q"] = w2_out[2 * NCH - 1]
                if "f1" in dbg:
                    d_r = dout("rs1rep", [128, T])
                    d_hm = dout("hamaxq", [128, NCH])
                    d_mx = dout("mxaccq", [128, T])
                    d_h = dout("hspT", [DFF, T])
                    d_x3 = dout("x3q", [T, E])
                    d_am = dout("amaxq", [128, NCH])
                    d_w2 = dout("w2qq", [16, 128, DFF], BF16)
                    p3.wait("sync", "sv", tok["hamax"])
                    p3.wait("sync", "d_hsp0", ctx.sems["d_hsp0"][1])
                    p3.wait("sync", "d_hsp1", ctx.sems["d_hsp1"][1])
                    p3.wait("sync", "d_w2q", ctx.sems["d_w2q"][1])
                    p3.wait("sync", "d_x3", ctx.sems["d_x3"][1])
                    p3.on("sync", lambda e: e.dma_start(out=d_r, in_=rs1rep[:]), inc="d_out", k=16)
                    p3.on("sync", lambda e: e.dma_start(out=d_hm, in_=hamax[:]), inc="d_out", k=16)
                    p3.on("sync", lambda e: e.dma_start(out=d_mx, in_=mxacc[:]), inc="d_out", k=16)
                    p3.on("sync", lambda e: e.dma_start(out=d_h, in_=h_spT), inc="d_out", k=16)
                    p3.on("sync", lambda e: e.dma_start(out=d_x3, in_=x3_sp), inc="d_out", k=16)
                    p3.on("sync", lambda e: e.dma_start(out=d_am, in_=amax_all[:]), inc="d_out", k=16)
                    p3.on("sync", lambda e: e.dma_start(out=d_w2, in_=w2q_sp), inc="d_out", k=16)
                p3.emit()

        with ExitStack() as e4:
            sl4 = lambda n, sh, dt=F32: e4.enter_context(nc.sbuf_tensor(n, sh, dt))
            hq_fm = sl4("hq_fm", [128, 64 * 512], BF16)
            hst = [sl4("hst%d" % i, [128, 512]) for i in range(2)]
            w2sg = [sl4("w2sg%d" % i, [128, DFF], BF16) for i in range(2)]
            h2fm = sl4("h2fm", [128, NCH * 512], BF16)
            sc127 = sl4("sc127", [128, T])
            x3c = sl4("x3c", [128, E])
            y2 = sl4("y2", [128, E])
            sq4 = sl4("sq4", [128, E])
            st4 = sl4("st4", [128, 24])
            rs2 = sl4("rs2", [128, NCH])
            rcol4 = sl4("rcol4", [128, 128])
            rrow4 = sl4("rrow4", [16, 128])

            ps_h2 = [e4.enter_context(nc.psum_tensor("ps_h2%d" % i, [128, 512], F32))
                     for i in range(2)]
            ps_t4 = [e4.enter_context(nc.psum_tensor("ps_t4%d" % i, [128, 128], BF16))
                     for i in range(2)]

            p4 = Phase(ctx, "p4")
            p4.barrier()
            t_rs2 = p4.on("vector", lambda e: e.tensor_scalar(
                rs2[:], hamax[:], cstc[:, 4:5], None, ALU.mult), inc="sv")
            t_sc = p4.chain("vector", "sv", [
                lambda e: e.memset(rcol4[:], 0.0),
                lambda e: e.tensor_scalar(rcol4[:, 0:NCH], hamax[:], 1e-5, None, ALU.max),
                lambda e: e.reciprocal(rcol4[:, 0:NCH], rcol4[:, 0:NCH]),
                lambda e: e.tensor_scalar(rcol4[:, 0:NCH], rcol4[:, 0:NCH], 127.0, None, ALU.mult),
            ])
            p4.wait("tensor", "sv", t_sc)
            t_st4 = p4.on("tensor", lambda e: e.transpose(
                ps_h2[0][:, 0:128], rcol4[:], identf_s[:]), inc="st")
            p4.wait("vector", "st", t_st4)
            t_sv4 = p4.on("vector", lambda e: e.tensor_copy(
                rrow4[0:16, :], ps_h2[0][0:16, 0:128]), inc="sv")
            ps0_guard = t_sv4
            p4.wait("sync", "sv", t_sv4)
            t_sr = p4.on("sync", lambda e: e.dma_start(
                out=bass.AP(row_sp_t, 0, [[128, 16], [1, 128]]),
                in_=rrow4[0:16, :]), inc="d_row", k=16)
            p4.wait("sync", "d_row", t_sr)
            tok["sc127"] = p4.on("sync", lambda e: e.dma_start(
                out=sc127[:], in_=bass.AP(row_sp_t, 0, [[0, 128], [1, T]])),
                inc="d_row", k=16)

            t_w2in = {}
            hq_q = {}
            h2_mm = {}
            h2ev = {}
            bt_last = None
            t_hin = {}
            out_prev = None
            out_tev = None
            tr4_free = {}
            for tq in range(4):
                for fc in range(64):
                    par = fc % 2
                    if tq == 0 and fc == 0:
                        p4.wait("gpsimd", "d_hsp0", ctx.sems["d_hsp0"][1])
                        p4.wait("gpsimd", "d_hsp1", ctx.sems["d_hsp1"][1])
                    if fc >= 2:
                        p4.wait("gpsimd", "sv", hq_q.get((tq, fc - 2)))
                    elif tq > 0:
                        p4.wait("gpsimd", "sv", hq_q.get((tq - 1, 62 + fc)))
                    t_hin[(tq, fc)] = p4.on("gpsimd", lambda e, tq=tq, fc=fc, par=par: e.dma_start(
                        out=hst[par][:],
                        in_=h_spT[fc * 128:(fc + 1) * 128, tq * 512:(tq + 1) * 512]),
                        inc="d_hin%d" % par, k=16)
                    p4.wait("vector", "d_hin%d" % par, t_hin[(tq, fc)])
                    if tq == 0 and fc == 0:
                        p4.wait("vector", "d_row", tok["sc127"])
                    if tq > 0 and fc == 0:
                        p4.wait("vector", "st", h2_mm[(tq - 1, NCH - 1)])
                    hq_q[(tq, fc)] = p4.chain("vector", "sv", [
                        lambda e, par=par, tq=tq: e.tensor_tensor(
                            hst[par][:], hst[par][:],
                            sc127[:, tq * 512:(tq + 1) * 512], ALU.mult),
                        lambda e, par=par: e.tensor_scalar(
                            hst[par][:], hst[par][:], mcol[:], None, ALU.add),
                        lambda e, par=par, fc=fc: e.tensor_scalar(
                            hq_fm[:, fc * 512:(fc + 1) * 512], hst[par][:],
                            mcol[:], None, ALU.subtract),
                    ])
                if tq == 0:
                    p4.wait("sync", "d_w2q", tok["w2q"])
                p4.wait("sync", "st", h2_mm.get((tq - 1, NCH - 1)))
                t_w2in[0] = p4.on("sync", lambda e: e.dma_start(
                    out=w2sg[0][:], in_=w2q_sp[0]), inc="d_w2s0", k=16)
                for me in range(NCH):
                    par = me % 2
                    if me + 1 < NCH:
                        p4.wait("sync", "st", h2_mm.get((tq, me - 1)))
                        t_w2in[me + 1] = p4.on("sync", lambda e, me=me: e.dma_start(
                            out=w2sg[(me + 1) % 2][:], in_=w2q_sp[me + 1]),
                            inc="d_w2s%d" % ((me + 1) % 2), k=16)
                    p4.wait("tensor", "d_w2s%d" % par, t_w2in[me])
                    hprev = h2ev.get((tq, me - 2)) if me >= 2 else h2ev.get((tq - 1, NCH - 2 + me))
                    if hprev is None:
                        hprev = ps0_guard
                    p4.wait("tensor", "sv", hprev)
                    mm = None
                    for k in range(64):
                        if me == 0:
                            p4.wait("tensor", "sv", hq_q[(tq, k)])
                        mm = p4.on("tensor", lambda e, k=k, me=me, par=par: e.matmul(
                            ps_h2[par][:, :], w2sg[par][:, k * 128:(k + 1) * 128],
                            hq_fm[:, k * 512:(k + 1) * 512],
                            start=(k == 0), stop=(k == 63)), inc="st")
                    h2_mm[(tq, me)] = mm
                    p4.wait("vector", "st", mm)
                    if tq > 0 and me < 2:
                        p4.wait("vector", "st", bt_last)
                    h2ev[(tq, me)] = p4.on("vector", lambda e, me=me, par=par: e.tensor_copy(
                        h2fm[:, me * 512:(me + 1) * 512], ps_h2[par][:, :]), inc="sv")
                t_h2 = h2ev[(tq, NCH - 1)]
                if "f3" in dbg:
                    d_hqt = dout("hq_tq%d" % tq, [128, 64 * 512], BF16)
                    p4.wait("sync", "sv", hq_q[(tq, 63)])
                    p4.on("sync", lambda e, d_hqt=d_hqt: e.dma_start(
                        out=d_hqt, in_=hq_fm[:]), inc="d_out", k=16)

                for ml in range(4):
                    m = tq * 4 + ml
                    p4.wait("sync", "d_x3", tok["x3done"])
                    p4.wait("sync", "sv", out_tev)
                    t_x3 = p4.on("sync", lambda e, m=m: e.dma_start(
                        out=x3c[:], in_=x3_sp[m * 128:(m + 1) * 128, :]), inc="d_oin", k=16)
                    p4.wait("vector", "d_oin", t_x3)
                    p4.wait("vector", "sv", max(t_h2, t_rs2))
                    tev = None
                    if ml == 0:
                        p4.wait("tensor", "sv", t_h2)
                    for me in range(NCH):
                        bb = me % 2
                        tf = tr4_free.get(bb)
                        if tf:
                            p4.wait("tensor", "sv", tf)
                        tt = p4.on("tensor", lambda e, me=me, ml=ml, bb=bb: e.transpose(
                            ps_t4[bb][:, :],
                            h2fm[:, me * 512 + ml * 128:me * 512 + ml * 128 + 128],
                            ident[:]), inc="st")
                        bt_last = tt
                        p4.wait("vector", "st", tt)
                        tev = p4.on("vector", lambda e, me=me, m=m, bb=bb: e.scalar_tensor_tensor(
                            y2[:, me * 128:(me + 1) * 128], ps_t4[bb][:, :],
                            rs2[:, m:m + 1], x3c[:, me * 128:(me + 1) * 128],
                            ALU.mult, ALU.add), inc="sv")
                        tr4_free[bb] = tev
                    out_tev = tev
                    p4.wait("vector", "sv", tev)
                    tmu = p4.on("vector", lambda e: e.tensor_reduce(
                        st4[:, 20:21], y2[:], axis=AX, op=ALU.add), inc="sv")
                    p4.wait("scalar", "sv", tev)
                    p4.wait("scalar", "d_out", out_prev)
                    tsq4 = p4.on("scalar", lambda e: e.activation(sq4[:], y2[:], AF.Square), inc="sa")
                    p4.wait("vector", "sa", tsq4)
                    tss = p4.on("vector", lambda e: e.tensor_reduce(
                        st4[:, 21:22], sq4[:], axis=AX, op=ALU.add), inc="sv")
                    p4.wait("vector", "sv", max(tmu, tss))
                    tvar = p4.chain("vector", "sv", [
                        lambda e: e.tensor_scalar(st4[:, 20:22], st4[:, 20:22],
                                                  0.00048828125, None, ALU.mult),
                        lambda e: e.tensor_tensor(st4[:, 22:23], st4[:, 20:21], st4[:, 20:21], ALU.mult),
                        lambda e: e.tensor_tensor(st4[:, 21:22], st4[:, 21:22], st4[:, 22:23], ALU.subtract),
                        lambda e: e.tensor_scalar(st4[:, 21:22], st4[:, 21:22], 1e-5, None, ALU.add),
                    ])
                    p4.wait("scalar", "sv", tvar)
                    ta1 = p4.on("scalar", lambda e: e.activation(st4[:, 21:22], st4[:, 21:22], AF.Ln), inc="sa")
                    p4.wait("scalar", "sa", ta1)
                    ta2 = p4.on("scalar", lambda e: e.activation(st4[:, 21:22], st4[:, 21:22], AF.Exp, scale=-0.5), inc="sa")
                    p4.wait("vector", "sa", ta2)
                    tout = p4.on("vector", lambda e: e.tensor_scalar(
                        sq4[:], y2[:], st4[:, 20:21], st4[:, 21:22],
                        ALU.subtract, ALU.mult), inc="sv")
                    p4.wait("sync", "sv", tout)
                    out_prev = p4.on("sync", lambda e, m=m: e.dma_start(
                        out=out_d[m * 128:(m + 1) * 128, :], in_=sq4[:]), inc="d_out", k=16)
            if "f2" in dbg:
                d_sc = dout("sc127q", [128, T])
                d_hq = dout("hqfmq", [128, 64 * 512], BF16)
                d_h2 = dout("h2fmq", [128, NCH * 512], BF16)
                d_y2 = dout("y2q", [128, E])
                d_rs2 = dout("rs2q", [128, NCH])
                p4.wait("sync", "sv", ctx.sems["sv"][1])
                p4.wait("sync", "sa", ctx.sems["sa"][1])
                p4.on("sync", lambda e: e.dma_start(out=d_sc, in_=sc127[:]), inc="d_out", k=16)
                p4.on("sync", lambda e: e.dma_start(out=d_hq, in_=hq_fm[:]), inc="d_out", k=16)
                p4.on("sync", lambda e: e.dma_start(out=d_h2, in_=h2fm[:]), inc="d_out", k=16)
                p4.on("sync", lambda e: e.dma_start(out=d_y2, in_=y2[:]), inc="d_out", k=16)
                p4.on("sync", lambda e: e.dma_start(out=d_rs2, in_=rs2[:]), inc="d_out", k=16)
            p4.emit()

        _finish(ctx, out_d)
    return nc, dbg_d


def _finish(ctx, out_d):
    p = Phase(ctx, "fin")
    for s, rec in ctx.sems.items():
        if rec[1] > 0:
            p.wait("sync", s, rec[1])
    p.emit()


def kernel(**inputs):
    shards = host_prepare(
        inputs["L0_attn_query"], inputs["L0_attn_Wq"], inputs["L0_attn_Wk"],
        inputs["L0_attn_Wv"], inputs["L0_attn_Wout"],
        inputs["L0_attn_lambda_q1"], inputs["L0_attn_lambda_k1"],
        inputs["L0_attn_lambda_q2"], inputs["L0_attn_lambda_k2"],
        inputs["L0_attn_rms_weight"], inputs["L0_ffn_W1"], inputs["L0_ffn_W2"])
    nc, _ = build_nc()
    res = run_bass_kernel_spmd(nc, shards, core_ids=list(range(len(shards))))
    return np.stack([np.asarray(r["out"], dtype=np.float32) for r in res.results])


# version 10 (final)
# speedup vs baseline: 1.4019x; 1.0106x over previous
import math
from contextlib import ExitStack

import numpy as np
import ml_dtypes

import concourse.bass as bass
import concourse.mybir as mybir
from concourse.alu_op_type import AluOpType as ALU
from concourse.bass_utils import run_bass_kernel_spmd

F32 = mybir.dt.float32
BF16 = mybir.dt.bfloat16
AF = mybir.ActivationFunctionType
AX = mybir.AxisListType.X

T = 2048
E = 2048
H = 16
D = 64
DFF = 8192
NCH = 16
HPS = 2
NSUP = H // HPS
SUPW = HPS * 128
LAMBDA_INIT = 0.8 - 0.6 * math.exp(-0.3 * 12)
MAGIC32 = 12582912.0
MAGIC_MUL = 27021597764222976.0
CLAMP_PI = float(np.nextafter(np.float32(np.pi), np.float32(0)))


def _split(c):
    hi = float(np.float32(c))
    lo = float(np.float32(c - hi))
    return hi, lo


def host_prepare(x_all, Wq, Wk, Wv, Wout, lq1, lk1, lq2, lk2, rmsw, W1, W2):
    f32 = np.float32
    x_all = np.asarray(x_all)
    perm = np.empty(D, dtype=np.int64)
    perm[:32] = np.arange(0, D, 2)
    perm[32:] = np.arange(1, D, 2)
    full_perm = (np.arange(2 * H)[:, None] * D + perm[None, :]).reshape(-1)
    wq_p = np.ascontiguousarray(np.asarray(Wq, f32)[:, full_perm])
    wk_p = np.ascontiguousarray(np.asarray(Wk, f32)[:, full_perm])

    w1t = np.ascontiguousarray(np.asarray(W1, f32).T)
    w2tm = np.ascontiguousarray(
        np.asarray(W2, f32).T.reshape(64, 128, 16, 128)
        .transpose(2, 1, 0, 3).reshape(16, 128, DFF))

    tcols = (np.arange(NCH)[None, :] * 128 + np.arange(128)[:, None]).astype(f32)
    pfreq = np.broadcast_to(np.arange(32, dtype=f32)[None, :], (128, 32)).copy()

    k_ = np.arange(1152, dtype=np.int64)[:, None]
    j_ = np.arange(T, dtype=np.int64)[None, :]
    m = ((k_ * j_ + 512 + 1024) % T) - 1024
    absm = np.abs(m)
    big = absm > 512
    m2 = np.where(big, np.sign(m) * (1024 - absm), m)
    sgn = np.ones_like(m, dtype=np.float64)

    kcol = (np.arange(9)[None, :] * 128 + np.arange(128)[:, None]).astype(f32)
    kk = np.arange(9 * 128).reshape(9, 128).T
    wcol = np.zeros((128, 9), dtype=f32)
    wcol[(kk >= 1) & (kk <= 1023)] = 2.0
    wcol[kk == 0] = 1.0
    wcol[kk == 1024] = 1.0

    common = {
        "wq": wq_p, "wk": wk_p,
        "wv": np.ascontiguousarray(np.asarray(Wv, f32)),
        "wo": np.ascontiguousarray(np.asarray(Wout, f32)),
        "w1t": w1t, "w2tm": w2tm,
        "lq1": np.tile(np.asarray(lq1, f32).reshape(1, D), (128, 1)),
        "lk1": np.tile(np.asarray(lk1, f32).reshape(1, D), (128, 1)),
        "lq2": np.tile(np.asarray(lq2, f32).reshape(1, D), (128, 1)),
        "lk2": np.tile(np.asarray(lk2, f32).reshape(1, D), (128, 1)),
        "rmsw128": np.tile(np.asarray(rmsw, f32).reshape(1, 128), (128, 4)),
        "tcols": tcols, "pfreq": pfreq,
        "mg2": m2.astype(f32), "gsgn": sgn.astype(f32),
        "kcol": kcol, "wcol": wcol,
        "ident": np.eye(128, dtype=ml_dtypes.bfloat16),
        "identf": np.eye(128, dtype=f32),
        "triu": np.triu(np.ones((128, 128))).astype(ml_dtypes.bfloat16),
        "halfones": np.full((2, 128), 0.5, dtype=f32),
        "ones2": np.full((128, 2), 1.0, dtype=ml_dtypes.bfloat16),
    }
    shards = []
    for c in range(x_all.shape[0]):
        x = np.ascontiguousarray(np.asarray(x_all[c], f32))
        d = dict(common)
        d["x"] = x
        d["xt"] = np.ascontiguousarray(x.T)
        shards.append(d)
    return shards


class Ctx:
    def __init__(self, nc, es):
        self.nc = nc
        self.es = es
        self.sems = {}

    def sem(self, name):
        if name not in self.sems:
            h = self.es.enter_context(self.nc.semaphore(name))
            self.sems[name] = [h, 0]


class Phase:
    ENGINES = ("sync", "vector", "scalar", "tensor", "gpsimd")

    def __init__(self, ctx, name=""):
        self.ctx = ctx
        self.name = name
        self.items = {e: [] for e in self.ENGINES}

    def on(self, eng, fn, inc=None, k=1):
        tok = None
        if inc is not None:
            s = self.ctx.sems[inc]
            s[1] += k
            tok = s[1]
            self.items[eng].append(("i", fn, s[0], k))
        else:
            self.items[eng].append(("p", fn, None, 0))
        return tok

    def wait(self, eng, sem, tok):
        if tok is None or tok <= 0:
            return
        h = self.ctx.sems[sem][0]
        self.items[eng].append(("w", None, h, tok))

    def chain(self, eng, sem, fns):
        tok = None
        for fn in fns:
            if tok is not None:
                self.wait(eng, sem, tok)
            tok = self.on(eng, fn, inc=sem)
        return tok

    def barrier(self, engines=("tensor", "vector", "scalar", "sync", "gpsimd")):
        for eng in engines:
            for s in ("sv", "sa", "st", "sg"):
                self.wait(eng, s, self.ctx.sems[s][1])

    def emit(self):
        nc = self.ctx.nc
        with nc.Block() as block:
            def runner(eng_name):
                def run(e):
                    for kind, fn, h, k in self.items[eng_name]:
                        if kind == "w":
                            e.wait_ge(h, k)
                        elif kind == "i":
                            fn(e).then_inc(h, k)
                        else:
                            fn(e)
                return run
            for eng_name in self.ENGINES:
                if self.items[eng_name]:
                    getattr(block, eng_name)(runner(eng_name))


def pstr(t):
    return t.ap().ap[0][0]


def cm3(dram_ap, p=128):
    return dram_ap.rearrange("(k p) c -> p k c", p=p)


def dst3(sb_ap, k):
    return sb_ap.rearrange("p (k c) -> p k c", k=k)


def build_nc(stage=99, debug=()):
    nc = bass.Bass("TRN2", target_bir_lowering=False)
    dbg = set(debug)
    dbg_d = {}

    def din(name, shape, dt=F32):
        return nc.dram_tensor(name, shape, dt, kind="ExternalInput").ap()

    x_d = din("x", [T, E])
    xt_d = din("xt", [E, T])
    wq_d = din("wq", [E, E]); wk_d = din("wk", [E, E])
    wv_d = din("wv", [E, E]); wo_d = din("wo", [E, E])
    w1t_d = din("w1t", [E, DFF])
    w2tm_d = din("w2tm", [16, 128, DFF])
    lq1_d = din("lq1", [128, D]); lk1_d = din("lk1", [128, D])
    lq2_d = din("lq2", [128, D]); lk2_d = din("lk2", [128, D])
    rmsw_d = din("rmsw128", [128, 512])
    tcols_d = din("tcols", [128, NCH])
    pfreq_d = din("pfreq", [128, 32])
    mg2_d = din("mg2", [1152, T]); gsgn_d = din("gsgn", [1152, T])
    kcol_d = din("kcol", [128, 9]); wcol_d = din("wcol", [128, 9])
    ident_d = din("ident", [128, 128], BF16)
    identf_d = din("identf", [128, 128], F32)
    triu_d = din("triu", [128, 128], BF16)
    halfones_d = din("halfones", [2, 128])
    ones2_d = din("ones2", [128, 2], BF16)

    out_d = nc.dram_tensor("out", [T, E], F32, kind="ExternalOutput").ap()

    o_sp = nc.dram_tensor("o_sp", [T, E], BF16).ap()
    x3_sp = nc.dram_tensor("x3_sp", [T, E], F32).ap()
    h_spT = nc.dram_tensor("h_spT", [DFF, T], F32).ap()
    row_sp_t = nc.dram_tensor("row_sp", [1, T], F32)
    w2q_sp = nc.dram_tensor("w2q_sp", [16, 128, DFF], BF16).ap()
    g_sp_t = nc.dram_tensor("g_sp", [1, 512], F32)
    g_sp = g_sp_t.ap()
    wo_bf_d = nc.dram_tensor("wo_bf", [E, E], BF16).ap()

    def dout(name, shape, dt=F32):
        dbg_d[name] = nc.dram_tensor("dbg_" + name, shape, dt,
                                     kind="ExternalOutput").ap()
        return dbg_d[name]

    es = ExitStack()
    with es:
        ctx = Ctx(nc, es)
        for s in ("sv", "sa", "st", "sg",
                  "d_tab", "d_xt", "d_w", "d_wf0", "d_osp", "d_out", "d_lam",
                  "d_g", "d_oin", "d_xc", "d_wo", "d_w1", "d_hsp", "d_w2",
                  "d_x3", "d_hin", "d_w2q", "d_w2s", "d_gb", "d_row",
                  "d_hin0", "d_hin1", "d_hsp0", "d_hsp1", "d_w2s0", "d_w2s1",
                  "d_oin0", "d_oin1", "d_oin2", "d_oin3", "d_xc0", "d_xc1",
                  "d_wf00", "d_wf01"):
            ctx.sem(s)

        def sb(name, shape, dt):
            return es.enter_context(nc.sbuf_tensor(name, shape, dt))

        ident = sb("identS", [128, 128], BF16)
        identf_s = sb("identfS", [128, 128], F32)
        triu_m = sb("triuS", [128, 128], BF16)
        ucol = sb("ucol", [128, 1], F32)
        mcol = sb("mcol", [128, 1], F32)
        rmsw_eff = sb("rmsw_eff", [128, 512], F32)
        neglam = sb("neglam", [128, 1], F32)
        gband = sb("gband", [128, 3 * 128], BF16)
        cos_all = sb("cos_all", [128, 512], F32)
        sin_all = sb("sin_all", [128, 512], F32)
        halfones_s = sb("halfonesS", [2, 128], F32)
        ones2_s = sb("ones2S", [128, 2], BF16)
        cstc = sb("cstc", [128, 8], F32)
        f0acc = sb("f0acc", [2, 1024], F32)
        amax_all = sb("amax_all", [128, NCH], F32)
        hamax = sb("hamax", [128, NCH], F32)

        tok = {}

        with ExitStack() as e0:
            sl = lambda n, sh, dt=F32: e0.enter_context(nc.sbuf_tensor(n, sh, dt))
            tcols_s = sl("tcolsS", [128, NCH])
            pfreq_s = sl("pfreqS", [128, 32])
            invf = sl("invf", [128, 32])
            lamq1 = sl("lamq1", [128, D]); lamk1 = sl("lamk1", [128, D])
            lamq2 = sl("lamq2", [128, D]); lamk2 = sl("lamk2", [128, D])
            rmsw_s = sl("rmsw_s", [128, 512])
            kcol_s = sl("kcol_s", [128, 9])
            wcol_s = sl("wcol_s", [128, 9])
            wfilt2 = sl("wfilt2", [128, 18], BF16)
            sA = sl("p0_sA", [128, 512]); sB = sl("p0_sB", [128, 512])
            sC = sl("p0_sC", [128, 512])
            theta = sl("p0_theta", [128, 512])
            gchA = sl("p0_gA", [128, T]); gchB = sl("p0_gB", [128, T])
            gcos = sl("p0_gcos", [128, T])
            gcos_bf = sl("p0_gcosbf", [128, T], BF16)
            g_ext = sl("p0_gext", [1, 512])
            grev = sl("p0_grev", [1, 512])
            lcol = sl("p0_lcol", [128, 4])
            gps = e0.enter_context(nc.psum_tensor("p0_gps", [2, 2048], F32))

            p = Phase(ctx, "p0")
            t_tab = None
            for (d_, s_) in [(ident, ident_d), (identf_s, identf_d),
                             (triu_m, triu_d), (tcols_s, tcols_d),
                             (pfreq_s, pfreq_d), (halfones_s, halfones_d),
                             (ones2_s, ones2_d), (kcol_s, kcol_d), (wcol_s, wcol_d)]:
                t_tab = p.on("sync", lambda e, d=d_, s=s_: e.dma_start(out=d[:], in_=s),
                             inc="d_tab", k=16)
            t_lam = None
            for (d_, s_) in [(lamq1, lq1_d), (lamk1, lk1_d), (lamq2, lq2_d),
                             (lamk2, lk2_d), (rmsw_s, rmsw_d)]:
                t_lam = p.on("sync", lambda e, d=d_, s=s_: e.dma_start(out=d[:], in_=s),
                             inc="d_lam", k=16)
            tok["wo_cast"] = p.on("gpsimd", lambda e: e.dma_start(out=wo_bf_d, in_=wo_d),
                                  inc="d_wo", k=16)

            p.wait("vector", "d_tab", t_tab)
            tok["mcol"] = p.chain("vector", "sv", [
                lambda e: e.memset(ucol[:], 0.25),
                lambda e: e.tensor_scalar(ucol[:], ucol[:], 8388608.0, None, ALU.add),
                lambda e: e.tensor_scalar(ucol[:], ucol[:], 8388608.0, None, ALU.subtract),
                lambda e: e.tensor_scalar(mcol[:], ucol[:], MAGIC_MUL, MAGIC32,
                                          ALU.mult, ALU.add),
            ])
            tok["f0acc"] = p.on("vector", lambda e: e.memset(f0acc[:], 0.0), inc="sv")
            tok["r127"] = p.chain("vector", "sv", [
                lambda e: e.memset(cstc[:, 0:1], 127.0),
                lambda e: e.reciprocal(cstc[:, 0:1], cstc[:, 0:1]),
                lambda e: e.memset(cstc[:, 5:6], 1e-5),
            ])

            cln_hi, cln_lo = _split(-math.log(10000.0) / 32.0)
            tv = p.chain("vector", "sv", [
                lambda e: e.tensor_scalar(sA[:, 0:32], pfreq_s[:], cln_hi, None, ALU.mult),
                lambda e: e.tensor_scalar(sB[:, 0:32], pfreq_s[:], cln_lo, None, ALU.mult),
                lambda e: e.tensor_tensor(sC[:, 0:32], sA[:, 0:32], sB[:, 0:32], ALU.add),
            ])
            p.wait("scalar", "sv", tv)
            ta = p.on("scalar", lambda e: e.activation(invf[:], sC[:, 0:32], AF.Exp),
                      inc="sa")
            p.wait("vector", "sa", ta)
            tcb = bass.AP(tcols_s, 0, [[pstr(tcols_s), 128], [1, NCH], [0, 32]])
            ivb = bass.AP(invf, 0, [[pstr(invf), 128], [0, NCH], [1, 32]])
            th3 = theta[:].rearrange("p (a b) -> p a b", a=NCH)
            t_theta = p.on("vector",
                           lambda e: e.tensor_tensor(th3, tcb, ivb, ALU.mult), inc="sv")

            i2p_hi, i2p_lo = _split(1.0 / (2 * math.pi))
            p2a = float(np.float32(2 * math.pi))
            p2b = float(np.float32(2 * math.pi - p2a))
            p2c = float(np.float32(2 * math.pi - p2a - p2b))
            hp_hi, hp_lo = _split(math.pi / 2)

            def reduce_sin(out_t, src_t, shift, tok_in):
                fns = []
                if shift:
                    fns += [
                        lambda e: e.tensor_scalar(sA[:], src_t[:], hp_hi, None, ALU.add),
                        lambda e: e.tensor_scalar(sA[:], sA[:], hp_lo, None, ALU.add),
                    ]
                    src = sA
                else:
                    src = src_t
                fns += [
                    lambda e: e.tensor_scalar(sB[:], src[:], i2p_hi, None, ALU.mult),
                    lambda e: e.tensor_scalar(sC[:], src[:], i2p_lo, None, ALU.mult),
                    lambda e: e.tensor_tensor(sB[:], sB[:], sC[:], ALU.add),
                    lambda e: e.tensor_scalar(sB[:], sB[:], mcol[:], None, ALU.add),
                    lambda e: e.tensor_scalar(sB[:], sB[:], mcol[:], None, ALU.subtract),
                    lambda e: e.tensor_scalar(sC[:], sB[:], p2a, None, ALU.mult),
                    lambda e: e.tensor_tensor(out_t[:], src[:], sC[:], ALU.subtract),
                    lambda e: e.tensor_scalar(sC[:], sB[:], p2b, None, ALU.mult),
                    lambda e: e.tensor_tensor(out_t[:], out_t[:], sC[:], ALU.subtract),
                    lambda e: e.tensor_scalar(sC[:], sB[:], p2c, None, ALU.mult),
                    lambda e: e.tensor_tensor(out_t[:], out_t[:], sC[:], ALU.subtract),
                    lambda e: e.tensor_scalar(out_t[:], out_t[:], CLAMP_PI, -CLAMP_PI,
                                              ALU.min, ALU.max),
                ]
                if tok_in is not None:
                    p.wait("vector", "sv", tok_in)
                tvv = p.chain("vector", "sv", fns)
                p.wait("scalar", "sv", tvv)
                return p.on("scalar",
                            lambda e: e.activation(out_t[:], out_t[:], AF.Sin), inc="sa")

            ta_sin = reduce_sin(sin_all, theta, False, t_theta)
            p.wait("vector", "sa", ta_sin)
            ta_cos = reduce_sin(cos_all, theta, True, None)
            tok["sincos"] = ta_cos

            li_hi, li_lo = _split(LAMBDA_INIT)
            p.wait("vector", "d_lam", t_lam)
            p.wait("vector", "sa", ta_cos)
            tv = p.chain("vector", "sv", [
                lambda e: e.tensor_tensor(sA[:, 0:D], lamq1[:], lamk1[:], ALU.mult),
                lambda e: e.tensor_tensor(sB[:, 0:D], lamq2[:], lamk2[:], ALU.mult),
                lambda e: e.tensor_reduce(lcol[:, 0:1], sA[:, 0:D], axis=AX, op=ALU.add),
                lambda e: e.tensor_reduce(lcol[:, 1:2], sB[:, 0:D], axis=AX, op=ALU.add),
            ])
            p.wait("scalar", "sv", tv)
            ta = p.on("scalar", lambda e: e.activation(lcol[:, 2:4], lcol[:, 0:2], AF.Exp),
                      inc="sa")
            p.wait("vector", "sa", ta)
            tok["neglam"] = p.chain("vector", "sv", [
                lambda e: e.tensor_tensor(neglam[:], lcol[:, 2:3], lcol[:, 3:4], ALU.subtract),
                lambda e: e.tensor_scalar(neglam[:], neglam[:], li_hi, None, ALU.add),
                lambda e: e.tensor_scalar(neglam[:], neglam[:], li_lo, None, ALU.add),
                lambda e: e.tensor_scalar(neglam[:], neglam[:], -1.0, None, ALU.mult),
            ])

            om_hi, om_lo = _split(1.0 - LAMBDA_INIT)
            tok["rmsw"] = p.chain("vector", "sv", [
                lambda e: e.tensor_scalar(sA[:], rmsw_s[:], om_hi, None, ALU.mult),
                lambda e: e.tensor_scalar(sB[:], rmsw_s[:], om_lo, None, ALU.mult),
                lambda e: e.tensor_tensor(rmsw_eff[:], sA[:], sB[:], ALU.add),
            ])

            u_hi, u_lo = _split(1.0 / (1024.0 * 0.15))
            g_hi, g_lo = _split(math.pi / 1024.0)
            tv = p.chain("vector", "sv", [
                lambda e: e.tensor_scalar(sA[:, 0:9], kcol_s[:], u_hi, None, ALU.mult),
                lambda e: e.tensor_scalar(sB[:, 0:9], kcol_s[:], u_lo, None, ALU.mult),
                lambda e: e.tensor_tensor(sA[:, 0:9], sA[:, 0:9], sB[:, 0:9], ALU.add),
                lambda e: e.tensor_tensor(sB[:, 0:9], sA[:, 0:9], sA[:, 0:9], ALU.mult),
            ])
            p.wait("scalar", "sv", tv)
            ta = p.on("scalar",
                      lambda e: e.activation(sC[:, 0:9], sB[:, 0:9], AF.Exp, scale=-0.5),
                      inc="sa")
            p.wait("vector", "sa", ta)
            w2vo = wfilt2[:].rearrange("p (a b) -> p a b", a=9)
            s3v = bass.AP(sC, 9, [[pstr(sC), 128], [1, 9], [0, 2]])
            t_wfilt = p.chain("vector", "sv", [
                lambda e: e.tensor_tensor(sC[:, 9:18], sC[:, 0:9], wcol_s[:], ALU.mult),
                lambda e: e.tensor_scalar(sC[:, 9:18], sC[:, 9:18], 0.00048828125,
                                          None, ALU.mult),
                lambda e: e.tensor_copy(w2vo, s3v),
            ])
            p.wait("tensor", "sv", t_wfilt)

            t_gmm = 0
            t_gread = None
            for r in range(9):
                p.wait("sync", "sv", t_gread)
                t_gd = p.on("sync", lambda e, r=r: e.dma_start(
                    out=gchA[:], in_=mg2_d[r * 128:(r + 1) * 128, :]), inc="d_g", k=16)
                t_gd = p.on("sync", lambda e, r=r: e.dma_start(
                    out=gchB[:], in_=gsgn_d[r * 128:(r + 1) * 128, :]), inc="d_g", k=16)
                p.wait("vector", "d_g", t_gd)
                if t_gmm:
                    p.wait("vector", "st", t_gmm)
                fns = [lambda e: e.tensor_scalar(gcos[:], gchA[:], g_hi, None, ALU.mult)]
                for pc in range(4):
                    fns += [
                        lambda e, pc=pc: e.tensor_scalar(
                            sA[:], gchA[:, pc * 512:(pc + 1) * 512], g_lo, None, ALU.mult),
                        lambda e, pc=pc: e.tensor_tensor(
                            gcos[:, pc * 512:(pc + 1) * 512],
                            gcos[:, pc * 512:(pc + 1) * 512], sA[:], ALU.add),
                    ]
                tv = p.chain("vector", "sv", fns)
                p.wait("scalar", "sv", tv)
                ta = p.on("scalar", lambda e: e.activation(gcos[:], gcos[:], AF.Sin),
                          inc="sa")
                p.wait("vector", "sa", ta)
                tv = p.chain("vector", "sv", [
                    lambda e: e.tensor_tensor(gcos[:], gcos[:], gchB[:], ALU.mult),
                    lambda e: e.tensor_copy(gcos_bf[:], gcos[:]),
                ])
                t_gread = tv
                p.wait("tensor", "sv", tv)
                for w in range(4):
                    t_gmm = p.on("tensor", lambda e, w=w, r=r: e.matmul(
                        gps[:, w * 512:(w + 1) * 512], wfilt2[:, 2 * r:2 * r + 2],
                        gcos_bf[:, w * 512:(w + 1) * 512],
                        start=(r == 0), stop=(r == 8)), inc="st")
            p.wait("vector", "st", t_gmm)
            t_grev = p.chain("vector", "sv", [
                lambda e: e.tensor_copy(sA[0:1, 0:512], gps[0:1, 0:512]),
                lambda e: e.tensor_copy(sB[0:1, 0:512], gps[0:1, 512:1024]),
                lambda e: e.tensor_copy(sC[0:1, 0:512], gps[0:1, 1024:1536]),
                lambda e: e.tensor_copy(theta[0:1, 0:512], gps[0:1, 1536:2048]),
                lambda e: e.tensor_copy(g_ext[0:1, 0:256], theta[0:1, 256:512]),
                lambda e: e.tensor_copy(g_ext[0:1, 256:512], sA[0:1, 0:256]),
            ])
            p.wait("sync", "sv", t_grev)
            t_gsp = p.on("sync", lambda e: e.dma_start(out=g_sp, in_=g_ext[0:1, :]),
                         inc="d_g", k=16)
            p.wait("gpsimd", "d_g", t_gsp)
            t_gbd = None
            for di in range(3):
                base = 129 + 128 * (di - 1)
                t_gbd = p.on("sync", lambda e, di=di, base=base: e.dma_start(
                    out=gcos[:, di * 128:(di + 1) * 128],
                    in_=bass.AP(g_sp_t, base, [[1, 128], [1, 128]])), inc="d_g", k=16)
            p.wait("vector", "d_g", t_gbd)
            t_gb = None
            for di in range(3):
                t_gb = p.on("vector", lambda e, di=di: e.tensor_copy(
                    gband[:, di * 128:(di + 1) * 128],
                    bass.AP(gcos, di * 128 + 127, [[pstr(gcos), 128], [-1, 128]])),
                    inc="sv")
            tok["gband"] = t_gb

            if "g" in dbg:
                gdb = dout("g", [1, 512])
                p.wait("sync", "sv", t_grev)
                p.on("sync", lambda e: e.dma_start(out=gdb, in_=g_ext[0:1, :]),
                     inc="d_out", k=16)
                gbdb = dout("gband", [128, 384], BF16)
                p.wait("sync", "sv", t_gb)
                p.on("sync", lambda e: e.dma_start(out=gbdb, in_=gband[:]),
                     inc="d_out", k=16)
            if "tabs" in dbg:
                cdb = dout("cos", [128, 512]); sdb = dout("sin", [128, 512])
                ldb = dout("neglam", [128, 1])
                p.wait("sync", "sa", tok["sincos"])
                p.on("sync", lambda e: e.dma_start(out=cdb, in_=cos_all[:]), inc="d_out", k=16)
                p.on("sync", lambda e: e.dma_start(out=sdb, in_=sin_all[:]), inc="d_out", k=16)
                p.wait("sync", "sv", tok["neglam"])
                p.on("sync", lambda e: e.dma_start(out=ldb, in_=neglam[:]), inc="d_out", k=16)
            p.emit()

        if stage == 0:
            _finish(ctx, out_d)
            return nc, dbg_d

        with ExitStack() as eP1:
            sl1 = lambda n, sh, dt: eP1.enter_context(nc.sbuf_tensor(n, sh, dt))
            xt_bf = sl1("xt_bf", [128, NCH * T], BF16)
            xt_src = cm3(xt_d)
            ph_x = Phase(ctx, "p1x")
            tok["xt"] = ph_x.on("gpsimd", lambda e: e.dma_start(
                out=dst3(xt_bf[:], NCH), in_=xt_src), inc="d_xt", k=16)
            ph_x.emit()

            q_fm = sl1("q_fm", [128, HPS * T], BF16)
            k_fm = sl1("k_fm", [128, HPS * T], BF16)
            v_aug = sl1("v_aug", [128, NCH * HPS * 132], BF16)
            wst_q = sl1("wst_q", [128, NCH * SUPW], BF16)
            wst_k = sl1("wst_k", [128, NCH * SUPW], BF16)
            wst_v = sl1("wst_v", [128, NCH * SUPW], BF16)
            qtm = sl1("qtm", [128, 2 * SUPW], BF16)
            ktm = sl1("ktm", [128, 2 * SUPW], BF16)
            rsc = [sl1("rsc%d" % i, [128, 128], F32) for i in range(4)]
            sqb = sl1("sqb", [128, T], BF16)
            negc = sl1("negc", [128, 4], F32)
            ncsc = sl1("ncsc", [2, 16], F32)
            wf0 = sl1("wf0", [128, 2 * 2048], F32)
            wf0a = sl1("wf0a", [128, 2048], BF16)
            wf0f = sl1("wf0f", [128, 2048], F32)
            wf0lo = sl1("wf0lo", [128, 2048], BF16)
            p1s = sl1("p1s", [128, 2 * 512], BF16)
            p2s = sl1("p2s", [128, 2 * 512], BF16)
            ost = sl1("ost", [128, NCH * SUPW], BF16)

            wq_col = lambda s: cm3(wq_d[:, s * SUPW:(s + 1) * SUPW])
            wk_col = lambda s: cm3(wk_d[:, s * SUPW:(s + 1) * SUPW])
            wv_col = lambda s: cm3(wv_d[:, s * SUPW:(s + 1) * SUPW])

            o_free = {}
            p_free = {}
            s_free = {}

            n_sup = 1 if stage <= 2 else NSUP
            for sup in range(n_sup):
                with ExitStack() as e1a:
                    pq = lambda n, sh, dt=F32: e1a.enter_context(
                        nc.psum_tensor(n + "_%d" % sup, sh, dt))
                    ps_q = [pq("ps_q%d" % i, [128, SUPW]) for i in range(2)]
                    ps_k = [pq("ps_k%d" % i, [128, SUPW]) for i in range(2)]
                    ps_v = pq("ps_v", [128, SUPW])
                    ps_tr = [pq("ps_tr%d" % i, [128, 128], BF16) for i in range(2)]
                    ps_f0 = pq("ps_f0", [128, 512])

                    pa = Phase(ctx, "p1a_%d" % sup)
                    if sup == 0:
                        t_wq = pa.on("gpsimd", lambda e, s=sup: e.dma_start(out=dst3(wst_q[:], NCH), in_=wq_col(s)), inc="d_w", k=16)
                        t_wk = pa.on("gpsimd", lambda e, s=sup: e.dma_start(out=dst3(wst_k[:], NCH), in_=wk_col(s)), inc="d_w", k=16)
                        t_wv = pa.on("gpsimd", lambda e, s=sup: e.dma_start(out=dst3(wst_v[:], NCH), in_=wv_col(s)), inc="d_w", k=16)
                    else:
                        t_wv = tok["wst_next"]
                    pa.wait("tensor", "d_w", t_wv)
                    pa.wait("tensor", "d_xt", tok["xt"])
                    if s_free:
                        pa.wait("tensor", "sa", max(s_free.values()))
                    if o_free:
                        pa.wait("tensor", "sv", max(o_free.values()))
                    if p_free:
                        pa.wait("tensor", "st", max(p_free.values()))
                    pa.wait("vector", "sv", tok["mcol"])
                    pa.wait("vector", "sa", tok["sincos"])
                    t_vones = pa.on("vector", lambda e: e.memset(
                        bass.AP(v_aug, 128, [[pstr(v_aug), 128], [132, NCH * HPS], [1, 1]]),
                        1.0), inc="sv")

                    wsrc = []
                    for qq in range(16):
                        if sup < 4:
                            ch, qtr = divmod(sup * 16 + qq, 4)
                            wsrc.append(w1t_d.rearrange("(a p) f -> a p f", p=128)[ch]
                                        [:, qtr * 2048:(qtr + 1) * 2048])
                        else:
                            me, qtr = divmod((sup - 4) * 16 + qq, 4)
                            wsrc.append(w2tm_d[me][:, qtr * 2048:(qtr + 1) * 2048])

                    f0_tok = {0: pa.on("sync", lambda e, s=wsrc[0]: e.dma_start(
                        out=wf0[:, 0:2048], in_=s), inc="d_wf00", k=16)}
                    f0_abs = {}
                    f0_mm = 0

                    rot_tok = {}
                    vev_tok = {}
                    tr_tok = {}
                    for ti in range(NCH):
                        par = ti % 2
                        if ti >= 2:
                            pa.wait("tensor", "sv", rot_tok[ti - 2])
                        if ti >= 1:
                            pa.wait("tensor", "sv", vev_tok[ti - 1])
                        mmv = None
                        for k in range(NCH):
                            st_ = (k == 0); sp_ = (k == NCH - 1)
                            pa.on("tensor", lambda e, k=k, ti=ti, par=par, st_=st_, sp_=sp_: e.matmul(
                                ps_q[par][:, :],
                                xt_bf[:, (k * NCH + ti) * 128:(k * NCH + ti) * 128 + 128],
                                wst_q[:, k * SUPW:(k + 1) * SUPW], start=st_, stop=sp_), inc="st")
                            pa.on("tensor", lambda e, k=k, ti=ti, par=par, st_=st_, sp_=sp_: e.matmul(
                                ps_k[par][:, :],
                                xt_bf[:, (k * NCH + ti) * 128:(k * NCH + ti) * 128 + 128],
                                wst_k[:, k * SUPW:(k + 1) * SUPW], start=st_, stop=sp_), inc="st")
                            mmv = pa.on("tensor", lambda e, k=k, ti=ti, st_=st_, sp_=sp_: e.matmul(
                                ps_v[:, :],
                                xt_bf[:, (k * NCH + ti) * 128:(k * NCH + ti) * 128 + 128],
                                wst_v[:, k * SUPW:(k + 1) * SUPW], start=st_, stop=sp_), inc="st")
                        tok["projmm"] = mmv

                        pa.wait("vector", "st", mmv)
                        tvv = None
                        for hl in range(HPS):
                            tvv = pa.on("vector", lambda e, ti=ti, hl=hl: e.tensor_copy(
                                v_aug[:, (ti * HPS + hl) * 132:(ti * HPS + hl) * 132 + 128],
                                ps_v[:, hl * 128:(hl + 1) * 128]), inc="sv")
                        vev_tok[ti] = tvv

                        cosv = bass.AP(cos_all, 32 * ti, [[pstr(cos_all), 128], [0, 4], [1, 32]])
                        sinv = bass.AP(sin_all, 32 * ti, [[pstr(sin_all), 128], [0, 4], [1, 32]])
                        cmp = lambda t_: t_[:].rearrange("p (a b) -> p a b", b=32)
                        tlast = None
                        for (ps_, tmb) in ((ps_q[par], qtm), (ps_k[par], ktm)):
                            x1v = bass.AP(ps_, 0, [[pstr(ps_), 128], [64, 4], [1, 32]])
                            x2v = bass.AP(ps_, 32, [[pstr(ps_), 128], [64, 4], [1, 32]])
                            o1v = bass.AP(tmb, par * SUPW, [[pstr(tmb), 128], [64, 4], [1, 32]])
                            o2v = bass.AP(tmb, par * SUPW + 32, [[pstr(tmb), 128], [64, 4], [1, 32]])
                            tv1 = pa.on("vector", lambda e, x1v=x1v, cosv=cosv: e.tensor_tensor(cmp(rsc[0]), x1v, cosv, ALU.mult), inc="sv")
                            tv2 = pa.on("vector", lambda e, x2v=x2v, sinv=sinv: e.tensor_tensor(cmp(rsc[1]), x2v, sinv, ALU.mult), inc="sv")
                            tv3 = pa.on("vector", lambda e, x1v=x1v, sinv=sinv: e.tensor_tensor(cmp(rsc[2]), x1v, sinv, ALU.mult), inc="sv")
                            tv4 = pa.on("vector", lambda e, x2v=x2v, cosv=cosv: e.tensor_tensor(cmp(rsc[3]), x2v, cosv, ALU.mult), inc="sv")
                            pa.wait("vector", "sv", tv2)
                            pa.on("vector", lambda e, o1v=o1v: e.tensor_tensor(o1v, cmp(rsc[0]), cmp(rsc[1]), ALU.subtract), inc="sv")
                            pa.wait("vector", "sv", tv4)
                            tlast = pa.on("vector", lambda e, o2v=o2v: e.tensor_tensor(o2v, cmp(rsc[2]), cmp(rsc[3]), ALU.add), inc="sv")
                        rot_tok[ti] = tlast

                        pa.wait("tensor", "sv", tlast)
                        tte = None
                        tile_i = 0
                        for (tmb, fmb) in ((qtm, q_fm), (ktm, k_fm)):
                            for lc in range(HPS):
                                b = tile_i % 2
                                if tr_tok.get((ti, b)) or tr_tok.get((ti - 1, b)):
                                    pa.wait("tensor", "sa", tr_tok.get((ti, b)) or tr_tok.get((ti - 1, b)))
                                tt = pa.on("tensor", lambda e, tmb=tmb, par=par, lc=lc, b=b: e.transpose(
                                    ps_tr[b][:, :],
                                    tmb[:, par * SUPW + lc * 128:par * SUPW + (lc + 1) * 128],
                                    ident[:]), inc="st")
                                pa.wait("scalar", "st", tt)
                                tte = pa.on("scalar", lambda e, fmb=fmb, lc=lc, ti=ti, b=b: e.copy(
                                    fmb[:, lc * T + ti * 128:lc * T + (ti + 1) * 128],
                                    ps_tr[b][:, :]), inc="sa")
                                tr_tok[(ti, b)] = tte
                                tile_i += 1
                        tr_tok[ti] = tte

                        if ti < len(wsrc):
                            slot = ti % 2
                            compensate = sup >= 4
                            if ti + 1 < len(wsrc):
                                pa.wait("sync", "sa", f0_abs.get(ti - 1))
                                f0_tok[ti + 1] = pa.on("sync", lambda e, s=wsrc[ti + 1], sl_=(ti + 1) % 2: e.dma_start(
                                    out=wf0[:, sl_ * 2048:(sl_ + 1) * 2048], in_=s),
                                    inc="d_wf0%d" % ((ti + 1) % 2), k=16)
                            pa.wait("scalar", "d_wf0%d" % (ti % 2), f0_tok[ti])
                            if f0_mm:
                                pa.wait("scalar", "st", f0_mm)
                            if not compensate:
                                f0_abs[ti] = pa.on("scalar", lambda e, sl_=slot: e.activation(
                                    wf0a[:], wf0[:, sl_ * 2048:(sl_ + 1) * 2048], AF.Abs), inc="sa")
                                pa.wait("tensor", "sa", f0_abs[ti])
                                if ti == 0:
                                    pa.wait("tensor", "sv", tok["f0acc"])
                                for w in range(4):
                                    f0_mm = pa.on("tensor", lambda e, w=w, ti=ti: e.matmul(
                                        ps_f0[0:2, :], ones2_s[:], wf0a[:, w * 512:(w + 1) * 512],
                                        start=(ti == 0 and w == 0), stop=(ti == 15 and w == 3)), inc="st")
                            else:
                                ta_f = pa.on("scalar", lambda e, sl_=slot: e.activation(
                                    wf0f[:], wf0[:, sl_ * 2048:(sl_ + 1) * 2048], AF.Abs), inc="sa")
                                pa.wait("scalar", "sa", ta_f)
                                f0_abs[ti] = pa.on("scalar", lambda e: e.copy(
                                    wf0a[:], wf0f[:]), inc="sa")
                                pa.wait("gpsimd", "sa", f0_abs[ti])
                                tg_lo = pa.on("gpsimd", lambda e: e.tensor_tensor(
                                    wf0lo[:], wf0f[:], wf0a[:], ALU.subtract), inc="sg")
                                pa.wait("tensor", "sa", f0_abs[ti])
                                pa.wait("tensor", "sg", tg_lo)
                                if ti == 0:
                                    pa.wait("tensor", "sv", tok["f0acc"])
                                for w in range(4):
                                    f0_mm = pa.on("tensor", lambda e, w=w, ti=ti: e.matmul(
                                        ps_f0[0:2, :], ones2_s[:], wf0a[:, w * 512:(w + 1) * 512],
                                        start=(ti == 0 and w == 0), stop=False), inc="st")
                                for w in range(4):
                                    f0_mm = pa.on("tensor", lambda e, w=w, ti=ti: e.matmul(
                                        ps_f0[0:2, :], ones2_s[:], wf0lo[:, w * 512:(w + 1) * 512],
                                        start=False, stop=(ti == 15 and w == 3)), inc="st")

                    pa.wait("vector", "st", f0_mm)
                    accoff = 0 if sup < 4 else 512
                    tok["f0acc"] = pa.on("vector", lambda e, a=accoff: e.tensor_tensor(
                        f0acc[:, a:a + 512], f0acc[:, a:a + 512], ps_f0[0:2, :], ALU.add), inc="sv")

                    t_ms_read = tok["f0acc"]
                    t_negc = None
                    for gi, fmb in enumerate((q_fm, k_fm)):
                        for lc in range(HPS):
                            if gi or lc:
                                pa.wait("scalar", "st", t_stab_mm)
                            tsq = pa.on("scalar", lambda e, fmb=fmb, lc=lc: e.activation(
                                sqb[:], fmb[:, lc * T:(lc + 1) * T], AF.Square), inc="sa")
                            for bi, base in enumerate((0, 64)):
                                col = gi * 8 + 2 * lc + bi
                                pa.wait("tensor", "sa", tsq)
                                for w in range(4):
                                    pa.wait("tensor", "sv", t_ms_read)
                                    t_stab_mm = pa.on("tensor", lambda e, base=base, w=w: e.matmul(
                                        ps_f0[0:2, :], ones2_s[base:base + 64, :],
                                        sqb[base:base + 64, w * 512:(w + 1) * 512],
                                        start=True, stop=True,
                                        tile_position=(base, 0)), inc="st")
                                    pa.wait("vector", "st", t_stab_mm)
                                    if w == 0:
                                        t_ms_read = pa.on("vector", lambda e, col=col: e.tensor_reduce(
                                            ncsc[0:2, col:col + 1], ps_f0[0:2, :], axis=AX,
                                            op=ALU.max), inc="sv")
                                    else:
                                        t_ms_read = pa.chain("vector", "sv", [
                                            lambda e: e.tensor_reduce(
                                                rsc[0][0:2, 0:1], ps_f0[0:2, :], axis=AX, op=ALU.max),
                                            lambda e, col=col: e.tensor_tensor(
                                                ncsc[0:2, col:col + 1], ncsc[0:2, col:col + 1],
                                                rsc[0][0:2, 0:1], ALU.max),
                                        ])
                    pa.wait("scalar", "sv", t_ms_read)
                    tsr = pa.on("scalar", lambda e: e.activation(ncsc[0:2, 0:16], ncsc[0:2, 0:16], AF.Sqrt), inc="sa")
                    pa.wait("vector", "sa", tsr)
                    t_nc2 = pa.chain("vector", "sv", [
                        lambda e: e.tensor_tensor(ncsc[0:2, 0:4], ncsc[0:2, 0:4], ncsc[0:2, 8:12], ALU.mult),
                        lambda e: e.tensor_scalar(ncsc[0:2, 0:4], ncsc[0:2, 0:4], -0.125, None, ALU.mult),
                    ])
                    pa.wait("tensor", "sv", t_nc2)
                    if s_free:
                        pa.wait("vector", "sa", max(s_free.values()))
                    for hh in range(4):
                        t_bc = pa.on("tensor", lambda e, hh=hh: e.matmul(
                            ps_f0[:, 0:1], halfones_s[:, :], ncsc[0:2, hh:hh + 1],
                            start=True, stop=True), inc="st")
                        pa.wait("vector", "st", t_bc)
                        t_negc = pa.on("vector", lambda e, hh=hh: e.tensor_copy(
                            negc[:, hh:hh + 1], ps_f0[:, 0:1]), inc="sv")
                        if hh < 3:
                            pa.wait("tensor", "sv", t_negc)
                    tok["negc"] = t_negc
                    tok["trlast"] = tr_tok[NCH - 1]
                    tok["vaug"] = max(vev_tok[NCH - 1], t_vones)
                    pa.emit()

                if stage == 1:
                    pd = Phase(ctx, "dbg1")
                    if "qkv" in dbg:
                        qdb = dout("q_fm", [128, HPS * T], BF16)
                        kdb = dout("k_fm", [128, HPS * T], BF16)
                        vdb = dout("v_aug", [128, NCH * HPS * 132], BF16)
                        cdb2 = dout("negc", [128, 4])
                        pd.wait("sync", "sa", ctx.sems["sa"][1])
                        pd.wait("sync", "sv", ctx.sems["sv"][1])
                        pd.on("sync", lambda e: e.dma_start(out=qdb, in_=q_fm[:]), inc="d_out", k=16)
                        pd.on("sync", lambda e: e.dma_start(out=kdb, in_=k_fm[:]), inc="d_out", k=16)
                        pd.on("sync", lambda e: e.dma_start(out=vdb, in_=v_aug[:]), inc="d_out", k=16)
                        pd.on("sync", lambda e: e.dma_start(out=cdb2, in_=negc[:]), inc="d_out", k=16)
                    pd.emit()
                    _finish(ctx, out_d)
                    return nc, dbg_d

                with ExitStack() as e1b:
                    pq = lambda n, sh: e1b.enter_context(
                        nc.psum_tensor(n + "_%d" % sup, sh, F32))
                    ps_s1 = [pq("ps_s1%d" % i, [128, 512]) for i in range(2)]
                    ps_s2 = [pq("ps_s2%d" % i, [128, 512]) for i in range(2)]
                    ps_o = [pq("ps_o%d" % i, [128, 264]) for i in range(4)]

                    pb = Phase(ctx, "p1b_%d" % sup)
                    pb.wait("tensor", "sa", tok["trlast"])
                    pb.wait("tensor", "sv", tok["vaug"])
                    pb.wait("tensor", "sv", tok["negc"])
                    pb.wait("scalar", "sv", tok["negc"])
                    pb.wait("vector", "sv", tok["neglam"])
                    pb.wait("vector", "d_osp", tok.get("osp"))

                    for hl in range(HPS):
                        for W in range(4):
                            jmax = 4 * W + 3
                            for j in range(jmax + 1):
                                jp = j % 2
                                nst = max(512 * W, 128 * j)
                                wid = 512 * W + 512 - nst
                                off = nst - 512 * W
                                pb.wait("tensor", "sa", s_free.get(jp))
                                t1 = pb.on("tensor", lambda e, hl=hl, j=j, jp=jp, nst=nst, wid=wid, off=off: e.matmul(
                                    ps_s1[jp][:, off:off + wid],
                                    k_fm[0:64, hl * T + j * 128:hl * T + (j + 1) * 128],
                                    q_fm[0:64, hl * T + nst:hl * T + nst + wid],
                                    start=True, stop=True, tile_position=(0, 0)), inc="st")
                                t2 = pb.on("tensor", lambda e, hl=hl, j=j, jp=jp, nst=nst, wid=wid, off=off: e.matmul(
                                    ps_s2[jp][:, off:off + wid],
                                    k_fm[64:128, hl * T + j * 128:hl * T + (j + 1) * 128],
                                    q_fm[64:128, hl * T + nst:hl * T + nst + wid],
                                    start=True, stop=True, tile_position=(64, 0)), inc="st")
                                pb.wait("scalar", "st", t2)
                                pb.wait("scalar", "st", p_free.get(jp))
                                e1t = pb.on("scalar", lambda e, hl=hl, jp=jp, off=off, wid=wid: e.activation(
                                    p1s[:, jp * 512 + off:jp * 512 + off + wid],
                                    ps_s1[jp][:, off:off + wid], AF.Exp,
                                    bias=negc[:, 2 * hl:2 * hl + 1], scale=0.125), inc="sa")
                                e2t = pb.on("scalar", lambda e, hl=hl, jp=jp, off=off, wid=wid: e.activation(
                                    p2s[:, jp * 512 + off:jp * 512 + off + wid],
                                    ps_s2[jp][:, off:off + wid], AF.Exp,
                                    bias=negc[:, 2 * hl + 1:2 * hl + 2], scale=0.125), inc="sa")
                                s_free[jp] = e2t
                                mask_tok = None
                                if j >= 4 * W:
                                    pb.wait("vector", "sa", e2t)
                                    pb.on("vector", lambda e, jp=jp, off=off: e.tensor_tensor(
                                        p1s[:, jp * 512 + off:jp * 512 + off + 128],
                                        p1s[:, jp * 512 + off:jp * 512 + off + 128],
                                        triu_m[:], ALU.mult), inc="sv")
                                    mask_tok = pb.on("vector", lambda e, jp=jp, off=off: e.tensor_tensor(
                                        p2s[:, jp * 512 + off:jp * 512 + off + 128],
                                        p2s[:, jp * 512 + off:jp * 512 + off + 128],
                                        triu_m[:], ALU.mult), inc="sv")
                                if mask_tok is not None:
                                    pb.wait("tensor", "sv", mask_tok)
                                else:
                                    pb.wait("tensor", "sa", e2t)
                                avt = None
                                for i in range(max(j, 4 * W), 4 * W + 4):
                                    il = i - 4 * W
                                    if j == 0:
                                        pb.wait("tensor", "sv", o_free.get(il))
                                    avt = pb.on("tensor", lambda e, jp=jp, il=il, i=i, j=j, W=W, hl=hl: e.matmul(
                                        ps_o[il][:, 0:129],
                                        p1s[:, jp * 512 + (i - 4 * W) * 128:jp * 512 + (i - 4 * W) * 128 + 128],
                                        v_aug[:, (j * HPS + hl) * 132:(j * HPS + hl) * 132 + 129],
                                        start=(j == 0), stop=(j == i)), inc="st")
                                    avt = pb.on("tensor", lambda e, jp=jp, il=il, i=i, j=j, W=W, hl=hl: e.matmul(
                                        ps_o[il][:, 132:261],
                                        p2s[:, jp * 512 + (i - 4 * W) * 128:jp * 512 + (i - 4 * W) * 128 + 128],
                                        v_aug[:, (j * HPS + hl) * 132:(j * HPS + hl) * 132 + 129],
                                        start=False, stop=(j == i)), inc="st")
                                    if i == j:
                                        pb.wait("vector", "st", avt)
                                        tvc = pb.chain("vector", "sv", [
                                            lambda e, il=il: e.reciprocal(
                                                rsc[0][:, 0:2],
                                                bass.AP(ps_o[il], 128, [[pstr(ps_o[il]), 128], [132, 2], [1, 1]])),
                                            lambda e: e.tensor_tensor(
                                                rsc[0][:, 2:3], rsc[0][:, 1:2], neglam[:], ALU.mult),
                                            lambda e, il=il: e.tensor_scalar(
                                                rsc[1][:, :], ps_o[il][:, 0:128], rsc[0][:, 0:1], None, ALU.mult),
                                        ])
                                        pb.wait("vector", "sv", tvc)
                                        o_free[il] = pb.on("vector", lambda e, il=il, i=i, hl=hl: e.scalar_tensor_tensor(
                                            ost[:, (i * HPS + hl) * 128:(i * HPS + hl) * 128 + 128],
                                            ps_o[il][:, 132:260], rsc[0][:, 2:3], rsc[1][:, :],
                                            ALU.mult, ALU.add), inc="sv")
                                p_free[jp] = avt
                    last_ev = max(o_free.values())
                    pb.wait("sync", "sv", last_ev)
                    osp_dst = cm3(o_sp[:, sup * SUPW:(sup + 1) * SUPW])
                    t_osp = pb.on("sync", lambda e: e.dma_start(
                        out=osp_dst, in_=dst3(ost[:], NCH)), inc="d_osp", k=16)
                    tok["osp"] = t_osp
                    if sup + 1 < NSUP:
                        pb.wait("gpsimd", "st", tok["projmm"])
                        pb.on("gpsimd", lambda e, s=sup + 1: e.dma_start(
                            out=dst3(wst_q[:], NCH), in_=wq_col(s)), inc="d_w", k=16)
                        pb.on("gpsimd", lambda e, s=sup + 1: e.dma_start(
                            out=dst3(wst_k[:], NCH), in_=wk_col(s)), inc="d_w", k=16)
                        tok["wst_next"] = pb.on("gpsimd", lambda e, s=sup + 1: e.dma_start(
                            out=dst3(wst_v[:], NCH), in_=wv_col(s)), inc="d_w", k=16)
                    pb.emit()

                if stage == 2:
                    pd = Phase(ctx, "dbg2")
                    if "attn" in dbg:
                        odb = dout("ost", [128, NCH * SUPW], BF16)
                        pd.wait("sync", "sv", ctx.sems["sv"][1])
                        pd.on("sync", lambda e: e.dma_start(out=odb, in_=ost[:]), inc="d_out", k=16)
                    pd.emit()
                    _finish(ctx, out_d)
                    return nc, dbg_d

        if stage == 3:
            if "osp" in dbg:
                pd = Phase(ctx, "dbg3")
                odb = dout("osp", [T, E], BF16)
                pd.wait("sync", "d_osp", ctx.sems["d_osp"][1])
                pd.on("sync", lambda e: e.dma_start(out=odb, in_=o_sp), inc="d_out", k=16)
                pd.emit()
            _finish(ctx, out_d)
            return nc, dbg_d

        with ExitStack() as eQ:
            xq_fm = eQ.enter_context(nc.sbuf_tensor("xq_fm", [128, NCH * E], BF16))

            with ExitStack() as e2:
                sl2 = lambda n, sh, dt=F32: e2.enter_context(nc.sbuf_tensor(n, sh, dt))
                wo_sb = sl2("wo_sb", [128, NCH * E], BF16)
                oring = sl2("oring", [128, 4 * T], BF16)
                x2tm = sl2("x2tm", [128, E], BF16)
                x2fm = sl2("x2fm", [128, E], BF16)
                xqtm = sl2("xqtm", [128, E], BF16)
                xch = sl2("xch", [128, 2 * E])
                y1 = sl2("y1", [128, E])
                sq2 = sl2("sq2", [128, E])
                st2 = sl2("st2", [128, 16])
                ms2 = sl2("ms2", [128, 16])

                p2q = lambda n, sh, dt=F32: e2.enter_context(nc.psum_tensor(n, sh, dt))
                ps_go = [p2q("ps_go%d" % i, [128, 512]) for i in range(2)]
                ps_t2 = [p2q("ps_t2%d" % i, [128, 128], BF16) for i in range(2)]
                ps_y = [p2q("ps_y%d" % i, [128, 512]) for i in range(4)]

                p2 = Phase(ctx, "p2")
                p2.barrier()
                p2.wait("vector", "sv", tok["f0acc"])
                p2.wait("vector", "sv", tok["r127"])
                tmean = p2.chain("vector", "sv", [
                    lambda e: e.tensor_reduce(st2[0:2, 0:1], f0acc[:, 0:512], axis=AX, op=ALU.add),
                    lambda e: e.tensor_reduce(st2[0:2, 1:2], f0acc[:, 512:1024], axis=AX, op=ALU.add),
                    lambda e: e.tensor_scalar(st2[0:2, 4:6], st2[0:2, 0:2],
                                              1.0 / 512.0, None, ALU.mult),
                    lambda e: e.tensor_scalar(sq2[0:2, 0:512], f0acc[:, 0:512],
                                              st2[0:2, 4:5], None, ALU.subtract),
                    lambda e: e.tensor_scalar(sq2[0:2, 512:1024], f0acc[:, 512:1024],
                                              st2[0:2, 5:6], None, ALU.subtract),
                    lambda e: e.tensor_reduce(st2[0:2, 6:7], sq2[0:2, 0:512], axis=AX, op=ALU.add),
                    lambda e: e.tensor_reduce(st2[0:2, 7:8], sq2[0:2, 512:1024], axis=AX, op=ALU.add),
                    lambda e: e.tensor_tensor(st2[0:2, 0:2], st2[0:2, 0:2], st2[0:2, 6:8], ALU.add),
                    lambda e: e.tensor_scalar(st2[0:2, 0:2], st2[0:2, 0:2],
                                              1.0 / 16777216.0, None, ALU.mult),
                ])
                p2.wait("tensor", "sv", tmean)
                tb1 = p2.on("tensor", lambda e: e.matmul(
                    ps_y[0][:, 0:1], halfones_s[:, :], st2[0:2, 0:1], start=True, stop=True), inc="st")
                tb2 = p2.on("tensor", lambda e: e.matmul(
                    ps_y[1][:, 0:1], halfones_s[:, :], st2[0:2, 1:2], start=True, stop=True), inc="st")
                p2.wait("vector", "st", tb2)
                tmean2 = p2.chain("vector", "sv", [
                    lambda e: e.tensor_copy(ms2[:, 0:1], ps_y[0][:, 0:1]),
                    lambda e: e.tensor_copy(ms2[:, 1:2], ps_y[1][:, 0:1]),
                    lambda e: e.tensor_tensor(cstc[:, 3:4], ms2[:, 0:1], cstc[:, 0:1], ALU.mult),
                    lambda e: e.tensor_tensor(cstc[:, 4:5], ms2[:, 1:2], cstc[:, 0:1], ALU.mult),
                    lambda e: e.tensor_scalar(ms2[:, 0:2], ms2[:, 0:2], 1e-5, None, ALU.max),
                    lambda e: e.reciprocal(cstc[:, 1:3], ms2[:, 0:2]),
                ])
                tok["means"] = tmean2

                p2.wait("sync", "d_wo", tok["wo_cast"])
                wo_src = cm3(wo_bf_d)
                t_wo = p2.on("sync", lambda e: e.dma_start(
                    out=dst3(wo_sb[:], NCH), in_=wo_src), inc="d_w", k=16)

                p2.wait("sync", "d_osp", ctx.sems["d_osp"][1])
                orow = lambda c: o_sp[(c % NCH) * 128:(c % NCH) * 128 + 128, :]
                t_oin = {}
                for c in (-1, 0, 1):
                    t_oin[c % NCH] = p2.on("sync", lambda e, c=c: e.dma_start(
                        out=oring[:, (c % 4) * T:(c % 4) * T + T], in_=orow(c)),
                        inc="d_oin%d" % (c % 4), k=16)
                t_xc = {0: p2.on("sync", lambda e: e.dma_start(
                    out=xch[:, 0:E], in_=x_d[0:128, :]), inc="d_xc0", k=16)}

                p2.wait("tensor", "sv", tok["gband"])
                p2.wait("tensor", "d_w", t_wo)
                p2.wait("vector", "sv", tok["rmsw"])
                p2.wait("vector", "sv", tok["means"])

                go_free = {}
                y_free = {}
                t2_free = {}
                xq_tr = {}
                gmm_last = {}
                tres_hist = {}
                tsq2_hist = {}
                for mi in range(NCH):
                    if mi + 2 <= NCH:
                        cnext = (mi + 2) % NCH
                        p2.wait("sync", "st", gmm_last.get(mi - 1))
                        t_oin[cnext] = p2.on("sync", lambda e, c=cnext: e.dma_start(
                            out=oring[:, (c % 4) * T:(c % 4) * T + T], in_=orow(c)),
                            inc="d_oin%d" % (cnext % 4), k=16)
                    if mi + 1 < NCH:
                        p2.wait("sync", "sv", tres_hist.get(mi - 1))
                        t_xc[mi + 1] = p2.on("sync", lambda e, m=mi + 1: e.dma_start(
                            out=xch[:, (m % 2) * E:(m % 2) * E + E],
                            in_=x_d[m * 128:(m + 1) * 128, :]),
                            inc="d_xc%d" % ((mi + 1) % 2), k=16)
                    for c in ((mi - 1) % NCH, mi, (mi + 1) % NCH):
                        p2.wait("tensor", "d_oin%d" % (c % 4), t_oin[c])

                    rstd_tok = None
                    for wp in range(2):
                        gof = go_free.get((mi, 0)) if wp == 1 else go_free.get((mi - 1, 1))
                        if gof:
                            p2.wait("tensor", "sv", gof)
                        gmt = None
                        for wloc in range(2):
                            w = wp * 2 + wloc
                            for dd in range(3):
                                jb = (mi + dd - 1) % NCH
                                gmt = p2.on("tensor", lambda e, w=w, wloc=wloc, dd=dd, jb=jb: e.matmul(
                                    ps_go[wloc][:, :],
                                    gband[:, dd * 128:(dd + 1) * 128],
                                    oring[:, (jb % 4) * T + w * 512:(jb % 4) * T + w * 512 + 512],
                                    start=(dd == 0), stop=(dd == 2)), inc="st")
                        gmm_last[mi] = gmt
                        p2.wait("scalar", "st", gmt)
                        tsq = {}
                        for wloc in range(2):
                            w = wp * 2 + wloc
                            tsq[wloc] = p2.on("scalar", lambda e, wloc=wloc, w=w: e.activation(
                                sq2[:, w * 512:(w + 1) * 512], ps_go[wloc][:, :], AF.Square), inc="sa")
                        tred = None
                        for wloc in range(2):
                            w = wp * 2 + wloc
                            p2.wait("vector", "sa", tsq[wloc])
                            for g in range(4):
                                tred = p2.on("vector", lambda e, w=w, g=g: e.tensor_reduce(
                                    ms2[:, w * 4 + g:w * 4 + g + 1],
                                    sq2[:, w * 512 + g * 128:w * 512 + (g + 1) * 128],
                                    axis=AX, op=ALU.add), inc="sv")
                        p2.wait("scalar", "sv", tred)
                        ta1 = p2.on("scalar", lambda e, wp=wp: e.activation(
                            st2[:, wp * 8:wp * 8 + 8], ms2[:, wp * 8:wp * 8 + 8], AF.Ln,
                            scale=0.0078125, bias=cstc[:, 5:6]), inc="sa")
                        p2.wait("scalar", "sa", ta1)
                        rstd_tok = p2.on("scalar", lambda e, wp=wp: e.activation(
                            st2[:, wp * 8:wp * 8 + 8], st2[:, wp * 8:wp * 8 + 8], AF.Exp,
                            scale=-0.5), inc="sa")
                        p2.wait("vector", "sa", rstd_tok)
                        tev = None
                        for wloc in range(2):
                            w = wp * 2 + wloc
                            for g in range(4):
                                tev = p2.on("vector", lambda e, w=w, g=g, wloc=wloc: e.scalar_tensor_tensor(
                                    x2tm[:, w * 512 + g * 128:w * 512 + (g + 1) * 128],
                                    ps_go[wloc][:, g * 128:(g + 1) * 128],
                                    st2[:, w * 4 + g:w * 4 + g + 1],
                                    rmsw_eff[:, g * 128:(g + 1) * 128],
                                    ALU.mult, ALU.mult), inc="sv")
                        go_free[(mi, wp)] = tev

                    t_tr = None
                    for k in range(NCH):
                        b = k % 2
                        p2.wait("tensor", "sv", go_free[(mi, k // 8)] if k // 8 in (0, 1) else None)
                        tf = t2_free.get(b)
                        if tf:
                            p2.wait("tensor", "sa", tf)
                        tt = p2.on("tensor", lambda e, k=k, b=b: e.transpose(
                            ps_t2[b][:, :], x2tm[:, k * 128:(k + 1) * 128], ident[:]), inc="st")
                        p2.wait("scalar", "st", tt)
                        t2_free[b] = p2.on("scalar", lambda e, k=k, b=b: e.copy(
                            x2fm[:, k * 128:(k + 1) * 128], ps_t2[b][:, :]), inc="sa")
                    t_tr = t2_free[(NCH - 1) % 2]

                    p2.wait("tensor", "sa", t_tr)
                    ymt = None
                    for k in range(NCH):
                        if k == 0:
                            for ew in range(4):
                                yf = y_free.get(ew)
                                if yf:
                                    p2.wait("tensor", "sv", yf)
                        for ew in range(4):
                            ymt = p2.on("tensor", lambda e, k=k, ew=ew: e.matmul(
                                ps_y[ew][:, :], x2fm[:, k * 128:(k + 1) * 128],
                                wo_sb[:, k * E + ew * 512:k * E + ew * 512 + 512],
                                start=(k == 0), stop=(k == NCH - 1)), inc="st")

                    p2.wait("vector", "st", ymt)
                    p2.wait("vector", "d_xc%d" % (mi % 2), t_xc[mi])
                    p2.wait("vector", "sa", tsq2_hist.get(mi - 1) if mi else None)
                    tres = None
                    for ew in range(4):
                        tres = p2.on("vector", lambda e, ew=ew, mi=mi: e.tensor_tensor(
                            y1[:, ew * 512:(ew + 1) * 512], ps_y[ew][:, :],
                            xch[:, (mi % 2) * E + ew * 512:(mi % 2) * E + ew * 512 + 512],
                            ALU.add), inc="sv")
                        y_free[ew] = tres
                    tres_hist[mi] = tres
                    p2.wait("vector", "sv", tres)
                    tmu = p2.on("vector", lambda e: e.tensor_reduce(
                        st2[:, 0:1], y1[:], axis=AX, op=ALU.add), inc="sv")
                    p2.wait("scalar", "sv", tres)
                    tsq2 = p2.on("scalar", lambda e: e.activation(sq2[:], y1[:], AF.Square), inc="sa")
                    tsq2_hist[mi] = tsq2
                    p2.wait("vector", "sa", tsq2)
                    tss = p2.on("vector", lambda e: e.tensor_reduce(
                        st2[:, 1:2], sq2[:], axis=AX, op=ALU.add), inc="sv")
                    p2.wait("vector", "sv", max(tmu, tss))
                    tvar = p2.chain("vector", "sv", [
                        lambda e: e.tensor_scalar(st2[:, 0:2], st2[:, 0:2],
                                                  0.00048828125, None, ALU.mult),
                        lambda e: e.tensor_tensor(st2[:, 2:3], st2[:, 0:1], st2[:, 0:1], ALU.mult),
                        lambda e: e.tensor_tensor(st2[:, 1:2], st2[:, 1:2], st2[:, 2:3], ALU.subtract),
                        lambda e: e.tensor_scalar(st2[:, 1:2], st2[:, 1:2], 1e-5, None, ALU.add),
                    ])
                    p2.wait("scalar", "sv", tvar)
                    ta1 = p2.on("scalar", lambda e: e.activation(st2[:, 1:2], st2[:, 1:2], AF.Ln), inc="sa")
                    p2.wait("scalar", "sa", ta1)
                    ta2 = p2.on("scalar", lambda e: e.activation(st2[:, 1:2], st2[:, 1:2], AF.Exp, scale=-0.5), inc="sa")
                    p2.wait("vector", "sa", ta2)
                    tx3 = p2.on("vector", lambda e: e.tensor_scalar(
                        sq2[:], y1[:], st2[:, 0:1], st2[:, 1:2], ALU.subtract, ALU.mult), inc="sv")
                    p2.wait("sync", "sv", tx3)
                    t_x3 = p2.on("sync", lambda e, mi=mi: e.dma_start(
                        out=x3_sp[mi * 128:(mi + 1) * 128, :], in_=sq2[:]), inc="d_x3", k=16)
                    p2.wait("vector", "sv", tx3)
                    tq = p2.chain("vector", "sv", [
                        lambda e, mi=mi: e.tensor_reduce(
                            amax_all[:, mi:mi + 1], sq2[:], axis=AX, op=ALU.max,
                            apply_absolute_value=True),
                        lambda e, mi=mi: e.tensor_scalar(
                            st2[:, 3:4], amax_all[:, mi:mi + 1], 1e-5, None, ALU.max),
                        lambda e: e.reciprocal(st2[:, 3:4], st2[:, 3:4]),
                        lambda e: e.tensor_scalar(st2[:, 3:4], st2[:, 3:4], 127.0, None, ALU.mult),
                        lambda e: e.tensor_scalar(y1[:], sq2[:], st2[:, 3:4], None, ALU.mult),
                        lambda e: e.tensor_scalar(y1[:], y1[:], mcol[:], None, ALU.add),
                        lambda e: e.tensor_scalar(xqtm[:], y1[:], mcol[:], None, ALU.subtract),
                    ])
                    for k in range(NCH):
                        b = k % 2
                        if k == 0:
                            p2.wait("tensor", "sv", tq)
                        tf = t2_free.get(b)
                        if tf:
                            p2.wait("tensor", "sa", tf)
                        tt = p2.on("tensor", lambda e, k=k, b=b: e.transpose(
                            ps_t2[b][:, :], xqtm[:, k * 128:(k + 1) * 128], ident[:]), inc="st")
                        p2.wait("scalar", "st", tt)
                        t2_free[b] = p2.on("scalar", lambda e, k=k, b=b, mi=mi: e.copy(
                            xq_fm[:, (k * NCH + mi) * 128:(k * NCH + mi) * 128 + 128],
                            ps_t2[b][:, :]), inc="sa")
                    xq_tr[mi] = t2_free[(NCH - 1) % 2]
                    p2.wait("scalar", "d_x3", t_x3)
                    p2.wait("scalar", "sv", tq)
                tok["xq"] = max(xq_tr.values())
                tok["x3done"] = ctx.sems["d_x3"][1]
                p2.emit()

                if stage == 4 and "p2" in dbg:
                    pd = Phase(ctx, "dbgp2")
                    xdb = dout("xq_fm", [128, NCH * E], BF16)
                    adb = dout("amax", [128, NCH])
                    pd.wait("sync", "sa", ctx.sems["sa"][1])
                    pd.wait("sync", "sv", ctx.sems["sv"][1])
                    pd.on("sync", lambda e: e.dma_start(out=xdb, in_=xq_fm[:]), inc="d_out", k=16)
                    pd.on("sync", lambda e: e.dma_start(out=adb, in_=amax_all[:]), inc="d_out", k=16)
                    pd.emit()
            if stage == 4:
                _finish(ctx, out_d)
                return nc, dbg_d

            with ExitStack() as e3:
                sl3 = lambda n, sh, dt=F32: e3.enter_context(nc.sbuf_tensor(n, sh, dt))
                w1a = sl3("w1a", [128, NCH * 512])
                w1q = sl3("w1q", [128, NCH * 512], BF16)
                hT = [sl3("hT%d" % i, [128, T]) for i in range(2)]
                mxacc = sl3("mxacc", [128, T])
                mnacc = sl3("mnacc", [128, T])
                rs1rep = sl3("rs1rep", [128, T])
                w2st = sl3("w2st", [128, DFF // 2])
                w2qb = sl3("w2qb", [128, DFF // 2], BF16)
                rcol = sl3("rcol", [128, 128])
                rrow16 = sl3("rrow16", [16, 128])

                ps_f1 = [e3.enter_context(nc.psum_tensor("ps_f1%d" % i, [128, 512], F32))
                         for i in range(8)]

                p3 = Phase(ctx, "p3")
                p3.barrier()
                t_rc = p3.chain("vector", "sv", [
                    lambda e: e.memset(rcol[:], 0.0),
                    lambda e: e.tensor_scalar(
                        rcol[:, 0:NCH], amax_all[:], cstc[:, 3:4], None, ALU.mult),
                ])
                p3.wait("tensor", "sv", t_rc)
                t_rt = p3.on("tensor", lambda e: e.transpose(
                    ps_f1[0][:, 0:128], rcol[:], identf_s[:]), inc="st")
                p3.wait("vector", "st", t_rt)
                t_rv = p3.on("vector", lambda e: e.tensor_copy(
                    rrow16[0:16, :], ps_f1[0][0:16, 0:128]), inc="sv")
                ev_tok0 = t_rv
                p3.wait("sync", "sv", t_rv)
                t_rr = p3.on("sync", lambda e: e.dma_start(
                    out=bass.AP(row_sp_t, 0, [[128, 16], [1, 128]]),
                    in_=rrow16[0:16, :]), inc="d_row", k=16)
                p3.wait("sync", "d_row", t_rr)
                tok["rs1rep"] = p3.on("sync", lambda e: e.dma_start(
                    out=rs1rep[:], in_=bass.AP(row_sp_t, 0, [[0, 128], [1, T]])),
                    inc="d_row", k=16)
                t_mm0 = p3.on("vector", lambda e: e.memset(mxacc[:], 0.0), inc="sv")
                t_mm0 = p3.on("vector", lambda e: e.memset(mnacc[:], 0.0), inc="sv")

                ev_tok = {0: ev_tok0}
                po_tok = {}
                sp_tok = {}
                mm_last = None
                w2_in = {}
                w2_q = {}
                w2_out = {}
                w2_in[0] = p3.on("sync", lambda e: e.dma_start(
                    out=w2st[:], in_=w2tm_d[0][:, 0:4096]), inc="d_w2", k=16)

                def w1_mult(t_dma):
                    p3.wait("scalar", "d_w1", t_dma)
                    return p3.on("scalar", lambda e: e.activation(
                        w1a[:], w1a[:], AF.Copy, scale=cstc[:, 1:2]), inc="sa")

                def w1_clip(t_mult):
                    p3.wait("vector", "sa", t_mult)
                    return p3.on("vector", lambda e: e.tensor_scalar(
                        w1a[:], w1a[:], 1.0, -1.0, ALU.min, ALU.max), inc="sv")

                def w1_round():
                    return p3.on("vector", lambda e: e.tensor_scalar(
                        w1q[:], w1a[:], mcol[:], mcol[:], ALU.add, ALU.subtract),
                        inc="sv")

                def w1_dma(fw):
                    return p3.on("sync", lambda e, fw=fw: e.dma_start(
                        out=dst3(w1a[:], NCH),
                        in_=cm3(w1t_d[:, fw * 512:(fw + 1) * 512])), inc="d_w1", k=16)

                pend = None

                def flush_pend(e_unused=None):
                    par, tt = pend
                    p3.on("vector", lambda e, par=par, tt=tt: e.tensor_tensor(
                        mxacc[:, tt * 512:(tt + 1) * 512],
                        mxacc[:, tt * 512:(tt + 1) * 512],
                        hT[par][:, tt * 512:(tt + 1) * 512], ALU.max), inc="sv")
                    return p3.on("vector", lambda e, par=par, tt=tt: e.tensor_tensor(
                        mnacc[:, tt * 512:(tt + 1) * 512],
                        mnacc[:, tt * 512:(tt + 1) * 512],
                        hT[par][:, tt * 512:(tt + 1) * 512], ALU.min), inc="sv")

                t_w1 = w1_dma(0)
                w1_clip(w1_mult(t_w1))
                q3_tok = w1_round()
                for fw in range(NCH):
                    p3.wait("tensor", "sv", q3_tok)
                    if fw == 0:
                        p3.wait("tensor", "sa", tok["xq"])
                    if fw + 1 < NCH:
                        p3.wait("sync", "sv", q3_tok)
                        t_w1n = w1_dma(fw + 1)
                    tile = 0
                    mm = None
                    for fc in range(4):
                        par = fc % 2
                        first_ev = True
                        tsi = None
                        for tt in range(4):
                            b = (fc * 4 + tt) % 8
                            p3.wait("tensor", "sv", ev_tok.get(b))
                            for k in range(NCH):
                                mm = p3.on("tensor", lambda e, fc=fc, tt=tt, k=k, b=b: e.matmul(
                                    ps_f1[b][:, :],
                                    w1q[:, k * 512 + fc * 128:k * 512 + fc * 128 + 128],
                                    xq_fm[:, k * T + tt * 512:k * T + tt * 512 + 512],
                                    start=(k == 0), stop=(k == NCH - 1)), inc="st")
                            p3.wait("vector", "st", mm)
                            if fw == 0 and fc == 0 and tt == 0:
                                p3.wait("vector", "d_row", tok["rs1rep"])
                            if first_ev:
                                p3.wait("vector", "d_hsp%d" % par, sp_tok.get(par))
                                first_ev = False
                            tev = p3.on("vector", lambda e, par=par, tt=tt, b=b: e.tensor_tensor(
                                hT[par][:, tt * 512:(tt + 1) * 512], ps_f1[b][:, :],
                                rs1rep[:, tt * 512:(tt + 1) * 512], ALU.mult), inc="sv")
                            ev_tok[b] = tev
                            if pend is not None:
                                p3.wait("vector", "sa", pend_sa)
                                flush_pend()
                            p3.wait("scalar", "sv", tev)
                            tsi = p3.on("scalar", lambda e, par=par, tt=tt: e.activation(
                                hT[par][:, tt * 512:(tt + 1) * 512],
                                hT[par][:, tt * 512:(tt + 1) * 512], AF.Silu), inc="sa")
                            pend = (par, tt)
                            pend_sa = tsi
                            tile += 1
                            if tile == 3 and fw + 1 < NCH:
                                t_mul_next = w1_mult(t_w1n)
                            if tile == 6 and fw + 1 < NCH:
                                w1_clip(t_mul_next)
                        p3.wait("gpsimd", "sa", tsi)
                        sp_tok[par] = p3.on("gpsimd", lambda e, fw=fw, fc=fc, par=par: e.dma_start(
                            out=h_spT[(fw * 4 + fc) * 128:(fw * 4 + fc) * 128 + 128, :],
                            in_=hT[par][:]), inc="d_hsp%d" % par, k=16)
                    mm_last = mm
                    if fw + 1 < NCH:
                        p3.wait("vector", "st", mm_last)
                        q3_tok = w1_round()
                    for hh in range(2):
                        idx = fw * 2 + hh
                        p3.wait("scalar", "d_w2", w2_in[idx])
                        t_m2 = p3.on("scalar", lambda e: e.activation(
                            w2st[:], w2st[:], AF.Copy, scale=cstc[:, 2:3]), inc="sa")
                        p3.wait("vector", "sa", t_m2)
                        p3.on("vector", lambda e: e.tensor_scalar(
                            w2st[:], w2st[:], 1.0, -1.0, ALU.min, ALU.max), inc="sv")
                        p3.wait("vector", "d_w2q", w2_out.get(idx - 1))
                        w2_q[idx] = p3.on("vector", lambda e: e.tensor_scalar(
                            w2qb[:], w2st[:], mcol[:], mcol[:], ALU.add, ALU.subtract),
                            inc="sv")
                        p3.wait("gpsimd", "sv", w2_q[idx])
                        w2_out[idx] = p3.on("gpsimd", lambda e, idx=idx: e.dma_start(
                            out=w2q_sp[idx // 2][:, (idx % 2) * 4096:(idx % 2) * 4096 + 4096],
                            in_=w2qb[:]), inc="d_w2q", k=16)
                        if idx + 1 < 2 * NCH:
                            p3.wait("sync", "sv", w2_q[idx])
                            w2_in[idx + 1] = p3.on("sync", lambda e, idx=idx: e.dma_start(
                                out=w2st[:],
                                in_=w2tm_d[(idx + 1) // 2]
                                [:, ((idx + 1) % 2) * 4096:((idx + 1) % 2) * 4096 + 4096]),
                                inc="d_w2", k=16)
                p3.wait("vector", "sa", pend_sa)
                flush_pend()
                tcmb = p3.chain("vector", "sv", [
                    lambda e: e.tensor_scalar(mnacc[:], mnacc[:], -1.0, None, ALU.mult),
                    lambda e: e.tensor_tensor(mxacc[:], mxacc[:], mnacc[:], ALU.max),
                ])
                p3.wait("tensor", "sv", tcmb)
                tr3 = {}
                for mi in range(NCH):
                    b = mi % 2
                    p3.wait("tensor", "sv", tr3.get(b, ev_tok.get(b)))
                    tt_ = p3.on("tensor", lambda e, mi=mi, b=b: e.transpose(
                        ps_f1[b][:, 0:128], mxacc[:, mi * 128:(mi + 1) * 128],
                        identf_s[:]), inc="st")
                    p3.wait("vector", "st", tt_)
                    tr3[b] = p3.on("vector", lambda e, mi=mi, b=b: e.tensor_reduce(
                        hamax[:, mi:mi + 1], ps_f1[b][:, 0:128], axis=AX, op=ALU.max),
                        inc="sv")
                tok["hamax"] = max(tr3.values())
                tok["w2q"] = w2_out[2 * NCH - 1]
                if "f1" in dbg:
                    d_r = dout("rs1rep", [128, T])
                    d_hm = dout("hamaxq", [128, NCH])
                    d_mx = dout("mxaccq", [128, T])
                    d_h = dout("hspT", [DFF, T])
                    d_x3 = dout("x3q", [T, E])
                    d_am = dout("amaxq", [128, NCH])
                    d_w2 = dout("w2qq", [16, 128, DFF], BF16)
                    p3.wait("sync", "sv", tok["hamax"])
                    p3.wait("sync", "d_hsp0", ctx.sems["d_hsp0"][1])
                    p3.wait("sync", "d_hsp1", ctx.sems["d_hsp1"][1])
                    p3.wait("sync", "d_w2q", ctx.sems["d_w2q"][1])
                    p3.wait("sync", "d_x3", ctx.sems["d_x3"][1])
                    p3.on("sync", lambda e: e.dma_start(out=d_r, in_=rs1rep[:]), inc="d_out", k=16)
                    p3.on("sync", lambda e: e.dma_start(out=d_hm, in_=hamax[:]), inc="d_out", k=16)
                    p3.on("sync", lambda e: e.dma_start(out=d_mx, in_=mxacc[:]), inc="d_out", k=16)
                    p3.on("sync", lambda e: e.dma_start(out=d_h, in_=h_spT), inc="d_out", k=16)
                    p3.on("sync", lambda e: e.dma_start(out=d_x3, in_=x3_sp), inc="d_out", k=16)
                    p3.on("sync", lambda e: e.dma_start(out=d_am, in_=amax_all[:]), inc="d_out", k=16)
                    p3.on("sync", lambda e: e.dma_start(out=d_w2, in_=w2q_sp), inc="d_out", k=16)
                p3.emit()

        with ExitStack() as e4:
            sl4 = lambda n, sh, dt=F32: e4.enter_context(nc.sbuf_tensor(n, sh, dt))
            hq_fm = sl4("hq_fm", [128, 64 * 512], BF16)
            hst = [sl4("hst%d" % i, [128, 512]) for i in range(2)]
            w2sg = [sl4("w2sg%d" % i, [128, DFF], BF16) for i in range(2)]
            h2fm = sl4("h2fm", [128, NCH * 512], BF16)
            sc127 = sl4("sc127", [128, T])
            x3c = sl4("x3c", [128, E])
            y2 = sl4("y2", [128, E])
            sq4 = sl4("sq4", [128, E])
            st4 = sl4("st4", [128, 24])
            rs2 = sl4("rs2", [128, NCH])
            rcol4 = sl4("rcol4", [128, 128])
            rrow4 = sl4("rrow4", [16, 128])

            ps_h2 = [e4.enter_context(nc.psum_tensor("ps_h2%d" % i, [128, 512], F32))
                     for i in range(2)]
            ps_t4 = [e4.enter_context(nc.psum_tensor("ps_t4%d" % i, [128, 128], BF16))
                     for i in range(2)]

            p4 = Phase(ctx, "p4")
            p4.barrier()
            t_rs2 = p4.on("vector", lambda e: e.tensor_scalar(
                rs2[:], hamax[:], cstc[:, 4:5], None, ALU.mult), inc="sv")
            t_sc = p4.chain("vector", "sv", [
                lambda e: e.memset(rcol4[:], 0.0),
                lambda e: e.tensor_scalar(rcol4[:, 0:NCH], hamax[:], 1e-5, None, ALU.max),
                lambda e: e.reciprocal(rcol4[:, 0:NCH], rcol4[:, 0:NCH]),
                lambda e: e.tensor_scalar(rcol4[:, 0:NCH], rcol4[:, 0:NCH], 127.0, None, ALU.mult),
            ])
            p4.wait("tensor", "sv", t_sc)
            t_st4 = p4.on("tensor", lambda e: e.transpose(
                ps_h2[0][:, 0:128], rcol4[:], identf_s[:]), inc="st")
            p4.wait("vector", "st", t_st4)
            t_sv4 = p4.on("vector", lambda e: e.tensor_copy(
                rrow4[0:16, :], ps_h2[0][0:16, 0:128]), inc="sv")
            ps0_guard = t_sv4
            p4.wait("sync", "sv", t_sv4)
            t_sr = p4.on("sync", lambda e: e.dma_start(
                out=bass.AP(row_sp_t, 0, [[128, 16], [1, 128]]),
                in_=rrow4[0:16, :]), inc="d_row", k=16)
            p4.wait("sync", "d_row", t_sr)
            tok["sc127"] = p4.on("sync", lambda e: e.dma_start(
                out=sc127[:], in_=bass.AP(row_sp_t, 0, [[0, 128], [1, T]])),
                inc="d_row", k=16)

            t_w2in = {}
            hq_q = {}
            h2_mm = {}
            h2ev = {}
            bt_last = None
            t_hin = {}
            out_prev = None
            out_tev = None
            tr4_free = {}
            for tq in range(4):
                for fc in range(64):
                    par = fc % 2
                    if tq == 0 and fc == 0:
                        p4.wait("gpsimd", "d_hsp0", ctx.sems["d_hsp0"][1])
                        p4.wait("gpsimd", "d_hsp1", ctx.sems["d_hsp1"][1])
                    if fc >= 2:
                        p4.wait("gpsimd", "sv", hq_q.get((tq, fc - 2)))
                    elif tq > 0:
                        p4.wait("gpsimd", "sv", hq_q.get((tq - 1, 62 + fc)))
                    t_hin[(tq, fc)] = p4.on("gpsimd", lambda e, tq=tq, fc=fc, par=par: e.dma_start(
                        out=hst[par][:],
                        in_=h_spT[fc * 128:(fc + 1) * 128, tq * 512:(tq + 1) * 512]),
                        inc="d_hin%d" % par, k=16)
                    p4.wait("vector", "d_hin%d" % par, t_hin[(tq, fc)])
                    if tq == 0 and fc == 0:
                        p4.wait("vector", "d_row", tok["sc127"])
                    if tq > 0 and fc == 0:
                        p4.wait("vector", "st", h2_mm[(tq - 1, NCH - 1)])
                    hq_q[(tq, fc)] = p4.chain("vector", "sv", [
                        lambda e, par=par, tq=tq: e.tensor_tensor(
                            hst[par][:], hst[par][:],
                            sc127[:, tq * 512:(tq + 1) * 512], ALU.mult),
                        lambda e, par=par: e.tensor_scalar(
                            hst[par][:], hst[par][:], mcol[:], None, ALU.add),
                        lambda e, par=par, fc=fc: e.tensor_scalar(
                            hq_fm[:, fc * 512:(fc + 1) * 512], hst[par][:],
                            mcol[:], None, ALU.subtract),
                    ])
                if tq == 0:
                    p4.wait("sync", "d_w2q", tok["w2q"])
                p4.wait("sync", "st", h2_mm.get((tq - 1, NCH - 1)))
                t_w2in[0] = p4.on("sync", lambda e: e.dma_start(
                    out=w2sg[0][:], in_=w2q_sp[0]), inc="d_w2s0", k=16)
                for me in range(NCH):
                    par = me % 2
                    if me + 1 < NCH:
                        p4.wait("sync", "st", h2_mm.get((tq, me - 1)))
                        t_w2in[me + 1] = p4.on("sync", lambda e, me=me: e.dma_start(
                            out=w2sg[(me + 1) % 2][:], in_=w2q_sp[me + 1]),
                            inc="d_w2s%d" % ((me + 1) % 2), k=16)
                    p4.wait("tensor", "d_w2s%d" % par, t_w2in[me])
                    hprev = h2ev.get((tq, me - 2)) if me >= 2 else h2ev.get((tq - 1, NCH - 2 + me))
                    if hprev is None:
                        hprev = ps0_guard
                    p4.wait("tensor", "sv", hprev)
                    mm = None
                    for k in range(64):
                        if me == 0:
                            p4.wait("tensor", "sv", hq_q[(tq, k)])
                        mm = p4.on("tensor", lambda e, k=k, me=me, par=par: e.matmul(
                            ps_h2[par][:, :], w2sg[par][:, k * 128:(k + 1) * 128],
                            hq_fm[:, k * 512:(k + 1) * 512],
                            start=(k == 0), stop=(k == 63)), inc="st")
                    h2_mm[(tq, me)] = mm
                    p4.wait("vector", "st", mm)
                    if tq > 0 and me < 2:
                        p4.wait("vector", "st", bt_last)
                    h2ev[(tq, me)] = p4.on("vector", lambda e, me=me, par=par: e.tensor_copy(
                        h2fm[:, me * 512:(me + 1) * 512], ps_h2[par][:, :]), inc="sv")
                t_h2 = h2ev[(tq, NCH - 1)]
                if "f3" in dbg:
                    d_hqt = dout("hq_tq%d" % tq, [128, 64 * 512], BF16)
                    p4.wait("sync", "sv", hq_q[(tq, 63)])
                    p4.on("sync", lambda e, d_hqt=d_hqt: e.dma_start(
                        out=d_hqt, in_=hq_fm[:]), inc="d_out", k=16)

                for ml in range(4):
                    m = tq * 4 + ml
                    p4.wait("sync", "d_x3", tok["x3done"])
                    p4.wait("sync", "sv", out_tev)
                    t_x3 = p4.on("sync", lambda e, m=m: e.dma_start(
                        out=x3c[:], in_=x3_sp[m * 128:(m + 1) * 128, :]), inc="d_oin", k=16)
                    p4.wait("vector", "d_oin", t_x3)
                    p4.wait("vector", "sv", max(t_h2, t_rs2))
                    tev = None
                    if ml == 0:
                        p4.wait("tensor", "sv", t_h2)
                    for me in range(NCH):
                        bb = me % 2
                        tf = tr4_free.get(bb)
                        if tf:
                            p4.wait("tensor", "sv", tf)
                        tt = p4.on("tensor", lambda e, me=me, ml=ml, bb=bb: e.transpose(
                            ps_t4[bb][:, :],
                            h2fm[:, me * 512 + ml * 128:me * 512 + ml * 128 + 128],
                            ident[:]), inc="st")
                        bt_last = tt
                        p4.wait("vector", "st", tt)
                        tev = p4.on("vector", lambda e, me=me, m=m, bb=bb: e.scalar_tensor_tensor(
                            y2[:, me * 128:(me + 1) * 128], ps_t4[bb][:, :],
                            rs2[:, m:m + 1], x3c[:, me * 128:(me + 1) * 128],
                            ALU.mult, ALU.add), inc="sv")
                        tr4_free[bb] = tev
                    out_tev = tev
                    p4.wait("vector", "sv", tev)
                    tmu = p4.on("vector", lambda e: e.tensor_reduce(
                        st4[:, 20:21], y2[:], axis=AX, op=ALU.add), inc="sv")
                    p4.wait("scalar", "sv", tev)
                    p4.wait("scalar", "d_out", out_prev)
                    tsq4 = p4.on("scalar", lambda e: e.activation(sq4[:], y2[:], AF.Square), inc="sa")
                    p4.wait("vector", "sa", tsq4)
                    tss = p4.on("vector", lambda e: e.tensor_reduce(
                        st4[:, 21:22], sq4[:], axis=AX, op=ALU.add), inc="sv")
                    p4.wait("vector", "sv", max(tmu, tss))
                    tvar = p4.chain("vector", "sv", [
                        lambda e: e.tensor_scalar(st4[:, 20:22], st4[:, 20:22],
                                                  0.00048828125, None, ALU.mult),
                        lambda e: e.tensor_tensor(st4[:, 22:23], st4[:, 20:21], st4[:, 20:21], ALU.mult),
                        lambda e: e.tensor_tensor(st4[:, 21:22], st4[:, 21:22], st4[:, 22:23], ALU.subtract),
                        lambda e: e.tensor_scalar(st4[:, 21:22], st4[:, 21:22], 1e-5, None, ALU.add),
                    ])
                    p4.wait("scalar", "sv", tvar)
                    ta1 = p4.on("scalar", lambda e: e.activation(st4[:, 21:22], st4[:, 21:22], AF.Ln), inc="sa")
                    p4.wait("scalar", "sa", ta1)
                    ta2 = p4.on("scalar", lambda e: e.activation(st4[:, 21:22], st4[:, 21:22], AF.Exp, scale=-0.5), inc="sa")
                    p4.wait("vector", "sa", ta2)
                    tout = p4.on("vector", lambda e: e.tensor_scalar(
                        sq4[:], y2[:], st4[:, 20:21], st4[:, 21:22],
                        ALU.subtract, ALU.mult), inc="sv")
                    p4.wait("sync", "sv", tout)
                    out_prev = p4.on("sync", lambda e, m=m: e.dma_start(
                        out=out_d[m * 128:(m + 1) * 128, :], in_=sq4[:]), inc="d_out", k=16)
            if "f2" in dbg:
                d_sc = dout("sc127q", [128, T])
                d_hq = dout("hqfmq", [128, 64 * 512], BF16)
                d_h2 = dout("h2fmq", [128, NCH * 512], BF16)
                d_y2 = dout("y2q", [128, E])
                d_rs2 = dout("rs2q", [128, NCH])
                p4.wait("sync", "sv", ctx.sems["sv"][1])
                p4.wait("sync", "sa", ctx.sems["sa"][1])
                p4.on("sync", lambda e: e.dma_start(out=d_sc, in_=sc127[:]), inc="d_out", k=16)
                p4.on("sync", lambda e: e.dma_start(out=d_hq, in_=hq_fm[:]), inc="d_out", k=16)
                p4.on("sync", lambda e: e.dma_start(out=d_h2, in_=h2fm[:]), inc="d_out", k=16)
                p4.on("sync", lambda e: e.dma_start(out=d_y2, in_=y2[:]), inc="d_out", k=16)
                p4.on("sync", lambda e: e.dma_start(out=d_rs2, in_=rs2[:]), inc="d_out", k=16)
            p4.emit()

        _finish(ctx, out_d)
    return nc, dbg_d


def _finish(ctx, out_d):
    p = Phase(ctx, "fin")
    for s, rec in ctx.sems.items():
        if rec[1] > 0:
            p.wait("sync", s, rec[1])
    p.emit()


def kernel(**inputs):
    shards = host_prepare(
        inputs["L0_attn_query"], inputs["L0_attn_Wq"], inputs["L0_attn_Wk"],
        inputs["L0_attn_Wv"], inputs["L0_attn_Wout"],
        inputs["L0_attn_lambda_q1"], inputs["L0_attn_lambda_k1"],
        inputs["L0_attn_lambda_q2"], inputs["L0_attn_lambda_k2"],
        inputs["L0_attn_rms_weight"], inputs["L0_ffn_W1"], inputs["L0_ffn_W2"])
    nc, _ = build_nc()
    res = run_bass_kernel_spmd(nc, shards, core_ids=list(range(len(shards))))
    return np.stack([np.asarray(r["out"], dtype=np.float32) for r in res.results])
